# version 03
# baseline (speedup 1.0000x reference)
import numpy as np
import ml_dtypes
from contextlib import ExitStack

import concourse.bass as bass
import concourse.mybir as mybir
from concourse.bass_utils import run_bass_kernel_spmd

F32 = mybir.dt.float32
BF16 = mybir.dt.bfloat16
I32 = mybir.dt.int32
AF = mybir.ActivationFunctionType
ALU = mybir.AluOpType
BF = ml_dtypes.bfloat16

B, SEQ, D = 2, 8192, 2048
NCORES = 8
T = (B * SEQ) // NCORES
DC = D // 128
H, DH = 16, 128
BLKA = 256
NBLK = T // BLKA
INTER = 5504
JCN = INTER // 128
GH = 4
NGRP = H // GH
LN_EPS = 1e-5
RMS_EPS = float(np.finfo(np.float32).eps)
NT = T // 512
STR = 512
NSTR = T // STR

TILES = 3 * 688
TPC = TILES // NCORES
SLAB = 8
NSLAB = (TPC + SLAB - 1) // SLAB


def _tile_src(t):
    if t < 688:
        return ("gate", (t // 16) * 128, (t % 16) * 128)
    if t < 1376:
        t -= 688
        return ("up", (t // 16) * 128, (t % 16) * 128)
    t -= 1376
    return ("down", (t // 43) * 128, (t % 43) * 128)


def _wqkv_perm():
    perm = []
    for g in range(NGRP):
        for role in range(3):
            for i in range(GH):
                perm.append((g * GH + i) * 3 + role)
    return perm


class Arena:

    def __init__(self, t, cols):
        self.t = t
        self.cols = cols
        self.off = 0

    def alloc(self, n, align=64):
        o = (self.off + align - 1) // align * align
        assert o + n <= self.cols, f"arena overflow {o}+{n}>{self.cols}"
        self.off = o + n
        return o

    def mark(self):
        return self.off

    def reset(self, m):
        self.off = m


class P:
    ENGINES = ("sync", "gpsimd", "tensor", "vector", "scalar")

    def __init__(self, sems):
        self.ops = {e: [] for e in self.ENGINES}
        self.sems = sems
        self.c = {k: 0 for k in sems}
        self.last = {}
        self.last_inc = {}

    def emit(self, eng, fn=None, wait=(), inc=None, inc_by=1):
        wait = [(self.sems[s], c) for (s, c) in wait if c > 0]
        tgt = None
        semh = by = None
        if inc is not None:
            self.c[inc] += inc_by
            tgt = self.c[inc]
            semh = self.sems[inc]
            by = inc_by

        def run(e, eng=eng, wait=wait, fn=fn, semh=semh, by=by):
            for h, c in wait:
                e.wait_ge(h, c)
            if fn is not None:
                inst = fn(e)
                self.last[eng] = inst
                if semh is not None:
                    inst.then_inc(semh, by)
                    self.last_inc[eng] = (semh, self._rt_val(semh, by))
                else:
                    self.last_inc[eng] = None
            elif semh is not None:
                att = eng in ("tensor", "vector", "scalar")
                if (att and self.last.get(eng) is not None
                        and self.last_inc.get(eng) is None):
                    self.last[eng].then_inc(semh, by)
                    self.last_inc[eng] = (semh, self._rt_val(semh, by))
                elif att and self.last_inc.get(eng) is not None:
                    ps, pv = self.last_inc[eng]
                    e.wait_ge(ps, pv)
                    e.nop().then_inc(semh, by)
                    self.last_inc[eng] = (semh, self._rt_val(semh, by))
                else:
                    e.nop().then_inc(semh, by)

        self.ops[eng].append(run)
        return tgt

    def _rt_val(self, semh, by):
        key = id(semh)
        self.rt = getattr(self, "rt", {})
        self.rt[key] = self.rt.get(key, 0) + by
        return self.rt[key]

    def now(self, sem):
        return self.c[sem]

    def barrier(self):
        marks = {}
        dmas = [(q, self.c[q]) for q in self.dma_sems]
        for e in self.ENGINES:
            self.emit(e, None, inc="b_" + e)
            marks[e] = self.c["b_" + e]
        for e in self.ENGINES:
            self.emit(e, None,
                      wait=[("b_" + x, marks[x]) for x in self.ENGINES if x != e]
                      + dmas)


def build_program(phases=("conv", "f1", "attn", "f2"), debug=False):
    nc = bass.Bass("TRN2", target_bir_lowering=False)

    xin = nc.dram_tensor("x", [T, D], F32, kind="ExternalInput").ap()
    idxs = nc.dram_tensor("idxs", [128, TPC * 128], I32, kind="ExternalInput").ap()
    lut = nc.dram_tensor("lut", [16], F32, kind="ExternalInput").ap()
    slope = nc.dram_tensor("slope", [16], F32, kind="ExternalInput").ap()
    wqkv_in = nc.dram_tensor("wqkv", [3 * H * DH, D], F32, kind="ExternalInput").ap()
    wg_in = nc.dram_tensor("wg", [D, D], F32, kind="ExternalInput").ap()
    wo_in = nc.dram_tensor("wo", [D, D], F32, kind="ExternalInput").ap()
    nw_in = nc.dram_tensor("nw", [128, 16], F32, kind="ExternalInput").ap()
    cident = nc.dram_tensor("cident", [128, 128], BF16, kind="ExternalInput").ap()
    cidentf = nc.dram_tensor("cidentf", [128, 128], F32, kind="ExternalInput").ap()
    cones_b = nc.dram_tensor("cones_b", [128, 1], BF16, kind="ExternalInput").ap()
    cones_f = nc.dram_tensor("cones_f", [1, 128], F32, kind="ExternalInput").ap()
    ciotaf = nc.dram_tensor("ciotaf", [128, 256], F32, kind="ExternalInput").ap()
    ciotapn = nc.dram_tensor("ciotapn", [128, 2], F32, kind="ExternalInput").ap()
    ciotakd = nc.dram_tensor("ciotakd", [128, 2], F32, kind="ExternalInput").ap()
    ciotablk = nc.dram_tensor("ciotablk", [128, 8], F32, kind="ExternalInput").ap()
    cmask = nc.dram_tensor("cmask", [128, 512], BF16, kind="ExternalInput").ap()
    cgap = nc.dram_tensor("cgap", [1, 4], F32, kind="ExternalInput").ap()
    cwm = nc.dram_tensor("cwm", [1, 4], F32, kind="ExternalInput").ap()

    out = nc.dram_tensor("out", [T, D], F32, kind="ExternalOutput").ap()

    agin = nc.dram_tensor("agin", [TPC, 128, 128], BF16)
    agin_a = agin.ap()
    ffnw = nc.dram_tensor("ffnw", [TILES, 128, 128], BF16)
    ffnw_a = ffnw.ap()
    X = [nc.dram_tensor(f"X{i}", [DC, 128, T], F32) for i in range(4)]
    X_a = [t.ap() for t in X]
    wqkv_bf = nc.dram_tensor("wqkv_bf", [48, 128, D], BF16)
    wqkv_bf_a = wqkv_bf.ap()
    wg_bf = nc.dram_tensor("wg_bf", [D, D], BF16)
    wg_bf_a = wg_bf.ap()
    wo_bf = nc.dram_tensor("wo_bf", [D, D], BF16)
    wo_bf_a = wo_bf.ap()
    kstage = nc.dram_tensor("kstage", [GH * 128, T], BF16)
    kstage_a = kstage.ap()
    aout_h = nc.dram_tensor("aout_h", [H, 128, T], BF16)
    aout_a = aout_h.ap()
    qcor_h = nc.dram_tensor("qcor_h", [H, 128, T], BF16)
    qcor_a = qcor_h.ap()
    kvag_i = nc.dram_tensor("kvag_i", [H * 128, 128], F32)
    kvag_i_a = kvag_i.ap()
    kvag_o = nc.dram_tensor("kvag_o", [4 * H * 128, 128], F32)
    kvag_o_a = kvag_o.ap()
    if debug:
        dbg_xn = nc.dram_tensor("dbg_xn", [DC, 128, 2048], BF16,
                                kind="ExternalOutput").ap()
        dbg_h = nc.dram_tensor("dbg_h", [JCN, 128, STR], BF16,
                               kind="ExternalOutput").ap()
        dbg_mr = nc.dram_tensor("dbg_mr", [2, 128, 2048], F32,
                                kind="ExternalOutput").ap()
        dbg_ac = nc.dram_tensor("dbg_ac", [H, 128, 2048], BF16,
                                kind="ExternalOutput").ap()
        dbg_fw = nc.dram_tensor("dbg_fw", [64, 128, 128], BF16,
                                kind="ExternalOutput").ap()
        dbg_ag = nc.dram_tensor("dbg_ag", [TPC, 128, 128], BF16,
                                kind="ExternalOutput").ap()

    stack = ExitStack()
    _uidc = [0]

    def _u(name):
        _uidc[0] += 1
        return f"{name}_{_uidc[0]}"
    with stack:
        def sbuf(name, p, f, dt):
            return stack.enter_context(nc.sbuf_tensor(name, [p, f], dt))

        sems = {}
        DMA_SEMS = (
            "sc", "sx0", "sx1", "sg0", "sg1", "su0", "su1", "sd0", "sd1",
            "sr0", "sr1", "sa", "sq0", "sq1", "sk0", "sk1",
            "gx", "gw", "gc0", "gc1", "ga0", "ga1", "go0", "go1",
            "gk", "gs0", "gs1", "gq0", "gq1", "gv",
            "ax0", "ax1", "aw", "an",
        )
        for s in DMA_SEMS + ("cc", "tT", "tV", "tS",
                             "b_sync", "b_gpsimd", "b_tensor", "b_vector",
                             "b_scalar"):
            sems[s] = stack.enter_context(nc.semaphore(s))
        pr = P(sems)
        pr.dma_sems = DMA_SEMS

        ident = sbuf("ident", 128, 128, BF16)
        identf = sbuf("identf", 128, 128, F32)
        onesb = sbuf("onesb", 128, 1, BF16)
        onesf = sbuf("onesf", 1, 128, F32)
        iotaf = sbuf("iotaf", 128, 256, F32)
        iotapn = sbuf("iotapn", 128, 2, F32)
        iotakd = sbuf("iotakd", 128, 2, F32)
        iotablk = sbuf("iotablk", 128, 8, F32)
        maskc = sbuf("maskc", 128, 512, BF16)
        gapv = sbuf("gapv", 128, 4, F32)
        wmv = sbuf("wmv", 128, 4, F32)
        sbc = sbuf("sbc", 128, 16, F32)
        negsb = sbuf("negsb", 128, 16, F32)
        nwcol = sbuf("nwcol", 128, 16, F32)
        lut128 = sbuf("lut128", 128, 16, F32)
        xn = sbuf("xn", 128, DC * 2048, BF16)
        AB_COLS = 49664
        AF_COLS = 9216
        ab_t = sbuf("ab", 128, AB_COLS, BF16)
        af_t = sbuf("af", 128, AF_COLS, F32)
        AB = Arena(ab_t, AB_COLS)
        AFA = Arena(af_t, AF_COLS)

        def abs_(o, n):
            return ab_t[:, o:o + n]

        def afs(o, n, p=128):
            return af_t[0:p, o:o + n]

        def dma(eng, outap, inap, sem, transpose=False, wait=()):
            return pr.emit(eng, lambda e, o=outap, i=inap, t=transpose:
                           e.dma_start(out=o, in_=i, transpose=t),
                           wait=wait, inc=sem, inc_by=16)

        for outap, inap in (
            (ident[:, :], cident), (identf[:, :], cidentf),
            (onesb[:, :], cones_b), (onesf[:, :], cones_f),
            (iotaf[:, :], ciotaf), (iotapn[:, :], ciotapn), (iotakd[:, :], ciotakd),
            (iotablk[:, :], ciotablk), (maskc[:, :], cmask),
            (gapv[:, :], cgap[:, :].partition_broadcast(128)),
            (wmv[:, :], cwm[:, :].partition_broadcast(128)),
            (sbc[:, :], slope[None, :].partition_broadcast(128)),
            (nwcol[:, :], nw_in),
            (lut128[:, :], lut[None, :].partition_broadcast(128)),
        ):
            dma("sync", outap, inap, "sc")
        cst_ld = pr.now("sc")
        consts_ok = pr.emit("vector", lambda e: e.tensor_scalar(
            negsb[:, :], sbc[:, :], -1.0, None, op0=ALU.mult),
            wait=[("sc", cst_ld)], inc="tV")

        perm = _wqkv_perm()
        for n in range(48):
            dma("gpsimd", wqkv_bf_a[n], wqkv_in[perm[n] * 128:perm[n] * 128 + 128, :],
                "gw")
        for i in range(4):
            dma("gpsimd", wg_bf_a[i * 512:(i + 1) * 512, :],
                wg_in[i * 512:(i + 1) * 512, :], "gw")
            dma("gpsimd", wo_bf_a[i * 512:(i + 1) * 512, :],
                wo_in[i * 512:(i + 1) * 512, :], "gw")
        wcast_done = pr.now("gw")

        m0 = AB.mark()
        m0f = AFA.mark()
        xpl_o = [AFA.alloc(2048), AFA.alloc(2048)]
        xst_o = AFA.alloc(2048)
        xhi_o = AB.alloc(2048)
        xlo_o = AB.alloc(2048)
        with nc.psum_tensor(_u("pxh"), [128, 2048], BF16) as pxh, \
             nc.psum_tensor(_u("pxl"), [128, 2048], BF16) as pxl:
            xpl_free = [0, 0]
            xst_free = 0
            hl_free = 0
            for tc in range(DC):
                sl = tc % 2
                dma("sync", afs(xpl_o[sl], 2048), xin[tc * 128:(tc + 1) * 128, :],
                    f"sx{sl}", wait=[("tV", xpl_free[sl])])
                lx = pr.now(f"sx{sl}")

                def split(e, sl=sl):
                    e.tensor_copy(abs_(xhi_o, 2048), afs(xpl_o[sl], 2048))
                    return e.tensor_tensor(abs_(xlo_o, 2048), afs(xpl_o[sl], 2048),
                                           abs_(xhi_o, 2048), op=ALU.subtract)
                sp = pr.emit("vector", split,
                             wait=[(f"sx{sl}", lx), ("tT", hl_free),
                                   ("tV", consts_ok)], inc="tV")
                xpl_free[sl] = sp
                pr.emit("tensor", None, wait=[("tV", sp), ("sc", cst_ld),
                                              ("tV", pr.now("tV"))])
                for fc in range(DC):
                    pr.emit("tensor", lambda e, fc=fc:
                            e.transpose(pxh[:, fc * 128:(fc + 1) * 128],
                                        abs_(xhi_o + fc * 128, 128), ident[:, :]))
                    pr.emit("tensor", lambda e, fc=fc:
                            e.transpose(pxl[:, fc * 128:(fc + 1) * 128],
                                        abs_(xlo_o + fc * 128, 128), ident[:, :]))
                t_mm = pr.emit("tensor", None, inc="tT")
                hl_free = t_mm

                def recomb(e):
                    e.tensor_copy(afs(xst_o, 2048), pxh[:, :])
                    return e.tensor_tensor(afs(xst_o, 2048), afs(xst_o, 2048),
                                           pxl[:, :], op=ALU.add)
                ev = pr.emit("vector", recomb,
                             wait=[("tT", t_mm), ("sa", xst_free)], inc="tV")
                dma("sync",
                    X_a[0][:, :, tc * 128:(tc + 1) * 128]
                    .rearrange("w p f -> p w f"),
                    afs(xst_o, 2048).rearrange("p (w f) -> p w f", w=16),
                    "sa", wait=[("tV", ev)])
                xst_free = pr.now("sa")
        AFA.reset(m0f)

        if "conv" in phases:
            mC = AB.mark()
            cv_o = [AB.alloc(SLAB * 128), AB.alloc(SLAB * 128)]
            wn_o = [AB.alloc(SLAB * 128), AB.alloc(SLAB * 128)]
            st_o = [AB.alloc(SLAB * 128), AB.alloc(SLAB * 128)]
            tm_o = AB.alloc(SLAB * 128)
            with nc.psum_tensor(_u("ptc"), [128, 2 * SLAB * 128], BF16) as ptc:
                in_free = [0, 0]
                wn_free = [0, 0]
                st_free = [0, 0]
                pt_free = [0, 0]
                for s in range(NSLAB):
                    w = min(SLAB, TPC - s * SLAB)
                    sl = s % 2
                    cw = w * 128
                    dma("gpsimd", abs_(cv_o[sl], cw),
                        idxs[:, s * SLAB * 128: s * SLAB * 128 + cw],
                        f"gc{sl}", wait=[("tV", in_free[sl])])
                    ldc = pr.now(f"gc{sl}")
                    pr.emit("vector", None, wait=[(f"gc{sl}", ldc), ("tV", consts_ok),
                                                  ("tT", wn_free[sl])])
                    for k in range(16):
                        dst = wn_o[sl] if k == 0 else tm_o
                        pr.emit("vector", lambda e, sl=sl, cw=cw, k=k, dst=dst:
                                e.tensor_scalar(abs_(dst, cw), abs_(cv_o[sl], cw),
                                                float(k), lut128[:, k:k + 1],
                                                op0=ALU.is_equal, op1=ALU.mult))
                        if k > 0:
                            pr.emit("vector", lambda e, sl=sl, cw=cw:
                                    e.tensor_tensor(abs_(wn_o[sl], cw),
                                                    abs_(wn_o[sl], cw),
                                                    abs_(tm_o, cw), op=ALU.add))
                    cvt = pr.emit("vector", None, inc="tV")
                    in_free[sl] = cvt
                    pr.emit("tensor", None, wait=[("tV", cvt), ("tS", pt_free[sl]),
                                                  ("sc", cst_ld)])
                    for j in range(w):
                        pr.emit("tensor", lambda e, sl=sl, j=j:
                                e.transpose(ptc[:, sl * SLAB * 128 + j * 128:
                                                sl * SLAB * 128 + (j + 1) * 128],
                                            abs_(wn_o[sl] + j * 128, 128),
                                            ident[:, :]))
                    trc = pr.emit("tensor", None, inc="tT")
                    wn_free[sl] = trc
                    ev = pr.emit("scalar", lambda e, sl=sl, cw=cw:
                                 e.copy(abs_(st_o[sl], cw),
                                        ptc[:, sl * SLAB * 128: sl * SLAB * 128 + cw]),
                                 wait=[("tT", trc), (f"ga{sl}", st_free[sl])], inc="tS")
                    pt_free[sl] = ev
                    dma("gpsimd",
                        agin_a[s * SLAB: s * SLAB + w].rearrange("w p f -> p w f"),
                        abs_(st_o[sl], cw).rearrange("p (w f) -> p w f", w=w),
                        f"ga{sl}", wait=[("tS", ev)])
                    st_free[sl] = pr.now(f"ga{sl}")
            pr.emit("gpsimd", lambda e: e.collective_compute(
                "AllGather", ALU.bypass, replica_groups=[list(range(NCORES))],
                ins=[agin.ap().opt()], outs=[ffnw.ap().opt()]),
                wait=[("ga0", pr.now("ga0")), ("ga1", pr.now("ga1"))], inc="cc")
        AB.reset(m0)
        if debug and "conv" in phases:
            dma("gpsimd", dbg_fw[0:64], ffnw_a[0:64], "gw",
                wait=[("cc", pr.now("cc"))])
            dma("gpsimd", dbg_ag[:], agin_a[:], "gw",
                wait=[("cc", pr.now("cc"))])
        ffnw_ok = pr.now("cc")

        def ln_to_xn(xsrc):
            mL = AB.mark()
            mF = AFA.mark()
            xo = [AB.alloc(2048), AB.alloc(2048)]
            sq_o = AB.alloc(2048)
            xf_o = [AFA.alloc(512), AFA.alloc(512)]
            tf_o = AFA.alloc(512)
            mb_o = AFA.alloc(2048)
            rb_o = AFA.alloc(2048)
            r_o = AFA.alloc(2560)
            rowm = lambda nch: afs(r_o + 0, 512, 1)
            rowv = lambda nch: afs(r_o + 512, 512, 1)
            rowt = lambda nch: afs(r_o + 1024, 512, 1)
            rowr0 = lambda nch: afs(r_o + 1536, 512, 1)
            with nc.psum_tensor(_u("pstat"), [64, 2048], F32) as pstat, \
                 nc.psum_tensor(_u("pbc"), [128, 1024], F32) as pbc:
                xf = [0, 0]
                for c in range(DC):
                    sl = c % 2
                    dma("gpsimd", abs_(xo[sl], 2048), xsrc[c], f"gc{sl}",
                        wait=[("tT", xf[sl])])
                    lc = pr.now(f"gc{sl}")
                    sq = pr.emit("scalar", lambda e, sl=sl:
                                 e.activation(abs_(sq_o, 2048), abs_(xo[sl], 2048),
                                              AF.Square),
                                 wait=[(f"gc{sl}", lc), ("tT", pr.now("tT"))],
                                 inc="tS")
                    pr.emit("tensor", None, wait=[(f"gc{sl}", lc), ("tS", sq)])
                    for nch in range(4):
                        pr.emit("tensor", lambda e, sl=sl, c=c, nch=nch:
                                e.matmul(pstat[0:1, nch * 512:(nch + 1) * 512],
                                         onesb[:, 0:1],
                                         abs_(xo[sl] + nch * 512, 512),
                                         start=(c == 0), stop=(c == DC - 1)))
                        pr.emit("tensor", lambda e, c=c, nch=nch:
                                e.matmul(pstat[32:33, nch * 512:(nch + 1) * 512],
                                         onesb[:, 0:1],
                                         abs_(sq_o + nch * 512, 512),
                                         start=(c == 0), stop=(c == DC - 1)))
                    xf[sl] = pr.emit("tensor", None, inc="tT")
                stats_tT = pr.now("tT")
                for nch in range(4):
                    o = nch * 512

                    def rowc(e, o=o, nch=nch):
                        e.tensor_scalar(rowm(nch), pstat[0:1, o:o + 512],
                                        1.0 / D, None, op0=ALU.mult)
                        e.tensor_scalar(rowv(nch), pstat[32:33, o:o + 512],
                                        1.0 / D, None, op0=ALU.mult)
                        e.tensor_tensor(rowt(nch), rowm(nch), rowm(nch), op=ALU.mult)
                        e.tensor_tensor(rowv(nch), rowv(nch), rowt(nch),
                                        op=ALU.subtract)
                        return e.tensor_scalar(rowv(nch), rowv(nch), LN_EPS, None,
                                               op0=ALU.add)
                    rc = pr.emit("vector", rowc, wait=[("tT", stats_tT)], inc="tV")
                    sq2 = pr.emit("scalar", lambda e, nch=nch:
                                  e.activation(rowt(nch), rowv(nch), AF.Sqrt),
                                  wait=[("tV", rc)], inc="tS")

                    def newt(e, nch=nch):
                        e.reciprocal(rowr0(nch), rowt(nch))
                        e.tensor_tensor(rowt(nch), rowr0(nch), rowr0(nch),
                                        op=ALU.mult)
                        e.tensor_tensor(rowt(nch), rowt(nch), rowv(nch), op=ALU.mult)
                        e.tensor_scalar(rowt(nch), rowt(nch), -0.5, 1.5,
                                        op0=ALU.mult, op1=ALU.add)
                        return e.tensor_tensor(rowv(nch), rowr0(nch), rowt(nch),
                                               op=ALU.mult)
                    nw = pr.emit("vector", newt, wait=[("tS", sq2)], inc="tV")
                    pr.emit("tensor", lambda e, nch=nch:
                            e.matmul(pbc[:, 0:512], onesf[:, :], rowm(nch),
                                     start=True, stop=True),
                            wait=[("tV", nw), ("tV", pr.now("tV"))])
                    bc = pr.emit("tensor", lambda e, nch=nch:
                                 e.matmul(pbc[:, 512:1024], onesf[:, :], rowv(nch),
                                          start=True, stop=True), inc="tT")
                    pr.emit("vector", lambda e, o=o: (
                            e.tensor_copy(afs(mb_o + o, 512), pbc[:, 0:512]),
                            e.tensor_copy(afs(rb_o + o, 512), pbc[:, 512:1024]))[-1],
                            wait=[("tT", bc)], inc="tV")
                bcast_tV = pr.now("tV")
            if debug:
                dma("gpsimd", dbg_mr[0], afs(mb_o, 2048), "gw",
                    wait=[("tV", bcast_tV)])
                dma("gpsimd", dbg_mr[1], afs(rb_o, 2048), "gw")
            xf2 = [0, 0]
            for c in range(DC):
                for nch in range(4):
                    sl = nch % 2
                    dma("sync", afs(xf_o[sl], 512),
                        xsrc[c][:, nch * 512:(nch + 1) * 512], f"sx{sl}",
                        wait=[("tV", xf2[sl]), ("tT", stats_tT)])
                    lc = pr.now(f"sx{sl}")

                    def app(e, c=c, sl=sl, nch=nch):
                        e.tensor_tensor(afs(tf_o, 512), afs(xf_o[sl], 512),
                                        afs(mb_o + nch * 512, 512),
                                        op=ALU.subtract)
                        return e.tensor_tensor(
                            xn[:, c * 2048 + nch * 512: c * 2048 + (nch + 1) * 512],
                            afs(tf_o, 512), afs(rb_o + nch * 512, 512),
                            op=ALU.mult)
                    xf2[sl] = pr.emit("vector", app,
                                      wait=[(f"sx{sl}", lc), ("tV", bcast_tV)],
                                      inc="tV")
            AFA.reset(mF)
            return pr.now("tV")

        def ffn_phase(xsrc, xdst):
            mP = AB.mark()
            xn_tV = ln_to_xn(xsrc)
            mF = AB.mark()
            wgt_o = AB.alloc(4096)
            wup_o = AB.alloc(4096)
            wdb_o = AB.alloc(JCN * 128)
            sil_o = AB.alloc(1024)
            har_o = AB.alloc(JCN * STR)
            mFf = AFA.mark()
            xr_o = [AFA.alloc(512), AFA.alloc(512)]
            xo_o = [AFA.alloc(512), AFA.alloc(512)]
            with nc.psum_tensor(_u("pg"), [128, 1024], F32) as pg, \
                 nc.psum_tensor(_u("pu"), [128, 1024], F32) as pu, \
                 nc.psum_tensor(_u("pd"), [128, 1024], F32) as pd:
                wgt_free = [0, 0]; wup_free = [0, 0]
                pg_free = [0, 0]; pu_free = [0, 0]; pd_free = [0, 0]
                wdb_free = [0, 0]
                har_rd = 0
                xr_free = [0, 0]; xo_free = [0, 0]
                for st in range(NSTR):
                    tok0 = st * STR
                    for jc in range(JCN):
                        hl = jc % 2
                        cg = dma("sync", abs_(wgt_o + hl * 2048, 2048)
                                 .rearrange("p (w f) -> p w f", w=16),
                                 ffnw_a[jc * 16:(jc + 1) * 16]
                                 .rearrange("w p f -> p w f"),
                                 f"sg{hl}",
                                 wait=[("tT", wgt_free[hl]), ("cc", ffnw_ok)])
                        cu = dma("sync", abs_(wup_o + hl * 2048, 2048)
                                 .rearrange("p (w f) -> p w f", w=16),
                                 ffnw_a[688 + jc * 16:688 + (jc + 1) * 16]
                                 .rearrange("w p f -> p w f"), f"su{hl}",
                                 wait=[("tT", wup_free[hl])])
                        pr.emit("tensor", None, wait=[
                            (f"sg{hl}", cg), (f"su{hl}", cu), ("tV", xn_tV),
                            ("tS", pg_free[hl]), ("tV", pu_free[hl]),
                            ("tT", har_rd)])
                        for hc in range(DC):
                            pr.emit("tensor", lambda e, hl=hl, hc=hc, tok0=tok0:
                                    e.matmul(pg[:, hl * 512:hl * 512 + 512],
                                             abs_(wgt_o + hl * 2048 + hc * 128, 128),
                                             xn[:, hc * 2048 + tok0: hc * 2048 + tok0 + 512],
                                             start=(hc == 0), stop=(hc == DC - 1)))
                        g_mm = pr.emit("tensor", None, inc="tT")
                        wgt_free[hl] = g_mm
                        for hc in range(DC):
                            pr.emit("tensor", lambda e, hl=hl, hc=hc, tok0=tok0:
                                    e.matmul(pu[:, hl * 512:hl * 512 + 512],
                                             abs_(wup_o + hl * 2048 + hc * 128, 128),
                                             xn[:, hc * 2048 + tok0: hc * 2048 + tok0 + 512],
                                             start=(hc == 0), stop=(hc == DC - 1)))
                        u_mm = pr.emit("tensor", None, inc="tT")
                        wup_free[hl] = u_mm
                        sl_ev = pr.emit("scalar", lambda e, hl=hl:
                                        e.activation(abs_(sil_o + hl * 512, 512),
                                                     pg[:, hl * 512:hl * 512 + 512],
                                                     AF.Silu),
                                        wait=[("tT", g_mm), ("tV", pu_free[hl])],
                                        inc="tS")
                        pg_free[hl] = sl_ev
                        h_ev = pr.emit("vector", lambda e, hl=hl, jc=jc:
                                       e.scalar_tensor_tensor(
                                           abs_(har_o + jc * STR + 0, 512),
                                           pu[:, hl * 512:hl * 512 + 512], 1.0,
                                           abs_(sil_o + hl * 512, 512),
                                           op0=ALU.mult, op1=ALU.mult),
                                       wait=[("tT", u_mm), ("tS", sl_ev)], inc="tV")
                        pu_free[hl] = h_ev
                    h_tV = pr.now("tV")
                    if debug and st == 0:
                        for c in range(DC):
                            dma("gpsimd", dbg_xn[c], xn[:, c * 2048:(c + 1) * 2048],
                                "gx", wait=[("tV", xn_tV)])
                        for jc in range(JCN):
                            dma("gpsimd", dbg_h[jc], abs_(har_o + jc * STR, STR),
                                "gx", wait=[("tV", h_tV)])
                    for oc in range(DC):
                        hl = oc % 2
                        t0 = 1376 + oc * 43
                        cd0 = dma("sync", abs_(wdb_o, 22 * 128)
                                  .rearrange("p (w f) -> p w f", w=22),
                                  ffnw_a[t0:t0 + 22].rearrange("w p f -> p w f"),
                                  "sd0",
                                  wait=[("tT", wdb_free[0])])
                        cd1 = dma("sync", abs_(wdb_o + 22 * 128, 21 * 128)
                                  .rearrange("p (w f) -> p w f", w=21),
                                  ffnw_a[t0 + 22:t0 + 43]
                                  .rearrange("w p f -> p w f"), "sd1",
                                  wait=[("tT", wdb_free[1])])
                        lx = dma("sync", afs(xr_o[hl], 512),
                                 xsrc[oc][:, tok0:tok0 + 512], f"sr{hl}",
                                 wait=[("tV", max(xr_free[hl], xn_tV))])
                        pr.emit("tensor", None, wait=[
                            ("sd0", cd0), ("sd1", cd1), ("tV", h_tV),
                            ("tV", pd_free[hl])])
                        for jc in range(JCN):
                            pr.emit("tensor", lambda e, hl=hl, jc=jc:
                                    e.matmul(pd[:, hl * 512:hl * 512 + 512],
                                             abs_(wdb_o + jc * 128, 128),
                                             abs_(har_o + jc * STR, 512),
                                             start=(jc == 0), stop=(jc == JCN - 1)))
                            if jc == 21:
                                pr.emit("tensor", None, inc="tT")
                                wdb_free[0] = pr.now("tT")
                        d_mm = pr.emit("tensor", None, inc="tT")
                        wdb_free[1] = d_mm
                        if oc == DC - 1:
                            har_rd = d_mm
                        x_ev = pr.emit("vector", lambda e, hl=hl:
                                       e.scalar_tensor_tensor(
                                           afs(xo_o[hl], 512),
                                           pd[:, hl * 512:hl * 512 + 512], 0.5,
                                           afs(xr_o[hl], 512),
                                           op0=ALU.mult, op1=ALU.add),
                                       wait=[("tT", d_mm), (f"sr{hl}", lx),
                                             (f"go{hl}", xo_free[hl])], inc="tV")
                        pd_free[hl] = x_ev
                        xr_free[hl] = x_ev
                        dma("gpsimd", xdst[oc][:, tok0:tok0 + 512],
                            afs(xo_o[hl], 512), f"go{hl}", wait=[("tV", x_ev)])
                        xo_free[hl] = pr.now(f"go{hl}")
            AB.reset(mP)
            AFA.reset(mFf)

        def attn_phase(xsrc, xdst):
            mP = AB.mark()
            xn_tV = ln_to_xn(xsrc)
            mA = AB.mark()
            mAf = AFA.mark()
            qkv_o = AB.alloc(8 * 2048)
            vna_o = AB.alloc(16 * 512)
            wqb_o = AB.alloc(16 * 512)
            ast_o = [AB.alloc(2048), AB.alloc(2048)]
            qst_o = [AB.alloc(2048), AB.alloc(2048)]
            qs_o = AB.alloc(256)
            qkb_o = AB.alloc(512)
            kst_o = AB.alloc(256)
            kvb_o = AB.alloc(128)
            dgt_o = AB.alloc(512)
            qdc_o = AB.alloc(256)
            kv_o = AFA.alloc(128)
            dt1_o = AFA.alloc(256)
            dt2_o = AFA.alloc(256)
            kd_o = AFA.alloc(2)
            bdp_o = AFA.alloc(8)

            kn_o = wqb_o

            wqb_free = 0
            kn_ok = 0
            ast_free = [0, 0]
            qst_free = [0, 0]
            kv_dG = 0

            with nc.psum_tensor(_u("pp"), [128, 2048], F32) as pp, \
                 nc.psum_tensor(_u("pA"), [128, 512], F32) as pA, \
                 nc.psum_tensor(_u("pB"), [128, 512], F32) as pB, \
                 nc.psum_tensor(_u("pC"), [128, 512], F32) as pC:
                pp_free = [0] * 4
                pA_free = 0; pB_free = 0; pC_free = 0
                kn_rd_tV = 0
                for g in range(NGRP):
                    for role in range(3):
                        for hc in range(DC):
                            dma("scalar",
                                abs_(wqb_o + hc * 512, 512),
                                wqkv_bf_a[12 * g + 4 * role: 12 * g + 4 * role + 4,
                                        :, hc * 128:(hc + 1) * 128]
                                .rearrange("a b c -> (a b) c"),
                                "aw", transpose=True,
                                wait=[("tT", wqb_free), ("gw", wcast_done),
                                      ("an", kn_ok), ("tV", kn_rd_tV)])
                        wts = pr.now("aw")
                        if role < 2:
                            for oc4 in range(4):
                                qc = (role * 4 + oc4)
                                pr.emit("tensor", None, wait=[
                                    ("aw", wts), ("tV", xn_tV)] +
                                    [("tS", pp_free[nch]) for nch in range(4)])
                                for hc in range(DC):
                                    for nch in range(4):
                                        pr.emit("tensor",
                                                lambda e, hc=hc, nch=nch, oc4=oc4:
                                                e.matmul(
                                                    pp[:, nch * 512:(nch + 1) * 512],
                                                    abs_(wqb_o + hc * 512 + oc4 * 128, 128),
                                                    xn[:, hc * 2048 + nch * 512:
                                                       hc * 2048 + (nch + 1) * 512],
                                                    start=(hc == 0),
                                                    stop=(hc == DC - 1)))
                                p_mm = pr.emit("tensor", None, inc="tT")
                                if oc4 == 3:
                                    wqb_free = p_mm
                                for nch in range(4):
                                    ev = pr.emit("scalar",
                                                 lambda e, qc=qc, nch=nch:
                                                 e.activation(
                                                     abs_(qkv_o + qc * 2048 + nch * 512, 512),
                                                     pp[:, nch * 512:(nch + 1) * 512],
                                                     AF.Silu),
                                                 wait=[("tT", p_mm),
                                                       ("tV", kn_rd_tV),
                                                       ("gk", pr.now("gk"))],
                                                 inc="tS")
                                    pp_free[nch] = ev
                                if role == 1:
                                    dma("gpsimd", kstage_a[oc4 * 128:(oc4 + 1) * 128, :],
                                        abs_(qkv_o + (4 + oc4) * 2048, 2048),
                                        "gk", wait=[("tS", pr.now("tS")),
                                                    ("an", kn_ok)])
                            if role == 1:
                                kstage_dG = pr.now("gk")
                        else:
                            for tc in range(DC):
                                nb = tc % 4
                                pr.emit("tensor", None, wait=[
                                    ("aw", wts), ("tV", xn_tV),
                                    ("tS", pp_free[nb])])
                                for hc in range(DC):
                                    pr.emit("tensor", lambda e, hc=hc, tc=tc, nb=nb:
                                            e.matmul(pp[:, nb * 512:(nb + 1) * 512],
                                                     xn[:, hc * 2048 + tc * 128:
                                                        hc * 2048 + tc * 128 + 128],
                                                     abs_(wqb_o + hc * 512, 512),
                                                     start=(hc == 0),
                                                     stop=(hc == DC - 1)))
                                v_mm = pr.emit("tensor", None, inc="tT")
                                if tc == DC - 1:
                                    wqb_free = v_mm
                                ev = pr.emit("scalar", lambda e, tc=tc, nb=nb:
                                             e.activation(abs_(vna_o + tc * 512, 512),
                                                          pp[:, nb * 512:(nb + 1) * 512],
                                                          AF.Silu),
                                             wait=[("tT", v_mm),
                                                   ("tV", kn_rd_tV)], inc="tS")
                                pp_free[nb] = ev
                    v_tS = pr.now("tS")
                    for tc in range(DC):
                        dma("scalar", abs_(kn_o + tc * 512, 512),
                            kstage_a[:, tc * 128:(tc + 1) * 128],
                            "an", transpose=True,
                            wait=[("gk", kstage_dG), ("tT", wqb_free),
                                  ("tV", kn_rd_tV)])
                    kn_ok = pr.now("an")
                    for hi in range(GH):
                        hh = g * GH + hi
                        hsl = hi % 2
                        def tbl(e, hh=hh):
                            e.tensor_scalar(afs(dt1_o, 256), iotaf[:, :],
                                            iotapn[:, 0:1], None, op0=ALU.subtract)
                            e.tensor_scalar(afs(dt1_o, 256), afs(dt1_o, 256),
                                            negsb[:, hh:hh + 1], 0.0,
                                            op0=ALU.mult, op1=ALU.min)
                            e.tensor_scalar(afs(dt2_o, 256), iotaf[:, :],
                                            iotapn[:, 1:2], None, op0=ALU.subtract)
                            e.tensor_scalar(afs(dt2_o, 256), afs(dt2_o, 256),
                                            negsb[:, hh:hh + 1], 0.0,
                                            op0=ALU.mult, op1=ALU.min)
                            e.tensor_scalar(afs(kd_o, 2), iotakd[:, :],
                                            negsb[:, hh:hh + 1], None, op0=ALU.mult)
                            return e.tensor_scalar(afs(bdp_o, 8), iotablk[:, :],
                                                   negsb[:, hh:hh + 1], None,
                                                   op0=ALU.mult)
                        tb = pr.emit("vector", tbl, wait=[("tV", consts_ok),
                                                          ("tV", pr.now("tV")),
                                                          ("tS", pr.now("tS")),
                                                          ("gv", pr.now("gv"))],
                                     inc="tV")

                        def tble(e):
                            e.activation(afs(dt1_o, 256), afs(dt1_o, 256), AF.Exp)
                            e.activation(afs(dt2_o, 256), afs(dt2_o, 256), AF.Exp)
                            e.activation(afs(kd_o, 2), afs(kd_o, 2), AF.Exp)
                            e.activation(afs(dt2_o, 0) if False else
                                         af_t[0:128, bdp_o:bdp_o + 8],
                                         afs(bdp_o, 8), AF.Exp)
                            return e.activation(af_t[0:128, dt1_o:dt1_o + 0] if False
                                                else af_t[0:1, 0:1],
                                                af_t[0:1, 0:1], AF.Copy)
                        tbe = pr.emit("scalar", lambda e: (
                                e.activation(afs(dt1_o, 256), afs(dt1_o, 256), AF.Exp),
                                e.activation(afs(dt2_o, 256), afs(dt2_o, 256), AF.Exp),
                                e.activation(afs(kd_o, 2), afs(kd_o, 2), AF.Exp),
                                e.activation(afs(bdp_o, 8), afs(bdp_o, 8), AF.Exp))[-1],
                                wait=[("tV", tb)], inc="tS")

                        def tbl2(e, hh=hh):
                            e.tensor_tensor(abs_(dgt_o, 256), afs(dt1_o, 256),
                                            maskc[:, 0:256], op=ALU.mult)
                            e.tensor_tensor(abs_(dgt_o + 256, 256), afs(dt2_o, 256),
                                            maskc[:, 256:512], op=ALU.mult)
                            e.tensor_scalar(afs(dt1_o, 256), iotaf[:, :], 1.0,
                                            negsb[:, hh:hh + 1],
                                            op0=ALU.add, op1=ALU.mult)
                            return e.memset(afs(kv_o, 128), 0.0)
                        tb2 = pr.emit("vector", tbl2, wait=[("tS", tbe)], inc="tV")
                        qde = pr.emit("scalar", lambda e:
                                      e.activation(afs(dt1_o, 256), afs(dt1_o, 256),
                                                   AF.Exp),
                                      wait=[("tV", tb2)], inc="tS")
                        tb3 = pr.emit("vector", lambda e: (
                                e.tensor_copy(abs_(qdc_o, 256), afs(dt1_o, 256)),
                                e.memset(abs_(kvb_o, 128), 0.0))[-1],
                                wait=[("tS", qde)], inc="tV")
                        tbl_tV = tb3

                        for i in range(NBLK):
                            c0 = i * 256
                            pr.emit("tensor", None, wait=[
                                ("tV", pA_free), ("tS", v_tS), ("an", kn_ok),
                                ("tV", tbl_tV)])
                            for nh in range(2):
                                pr.emit("tensor", lambda e, hi=hi, nh=nh, c0=c0:
                                        e.matmul(pA[:, nh * 256:(nh + 1) * 256],
                                                 abs_(qkv_o + (4 + hi) * 2048 + c0 + nh * 128, 128),
                                                 abs_(qkv_o + hi * 2048 + c0, 256),
                                                 start=True, stop=True))
                            qk_mm = pr.emit("tensor", None, inc="tT")
                            def dve1(e, hi=hi, i=i, c0=c0):
                                e.tensor_tensor(abs_(qkb_o, 256), pA[:, 0:256],
                                                abs_(dgt_o, 256), op=ALU.mult)
                                e.tensor_tensor(abs_(qkb_o + 256, 256), pA[:, 256:512],
                                                abs_(dgt_o + 256, 256), op=ALU.mult)
                                e.tensor_tensor(abs_(qs_o, 256),
                                                abs_(qkv_o + hi * 2048 + c0, 256),
                                                abs_(qdc_o, 256), op=ALU.mult)
                                e.tensor_scalar(abs_(kst_o, 128),
                                                abs_(kn_o + (2 * i) * 512 + hi * 128, 128),
                                                afs(kd_o, 1), None, op0=ALU.mult)
                                return e.tensor_scalar(
                                    abs_(kst_o + 128, 128),
                                    abs_(kn_o + (2 * i + 1) * 512 + hi * 128, 128),
                                    af_t[0:128, kd_o + 1:kd_o + 2], None,
                                    op0=ALU.mult)
                            d1 = pr.emit("vector", dve1,
                                         wait=[("tT", qk_mm), ("tS", pB_free),
                                               ("tT", pr.now("tT")),
                                               ("tS", pr.now("tS"))], inc="tV")
                            pA_free = d1
                            pr.emit("tensor", None, wait=[("tV", d1)])
                            for nh in range(2):
                                pr.emit("tensor", lambda e, hi=hi, i=i, nh=nh:
                                        e.matmul(pB[:, 0:256],
                                                 abs_(vna_o + (2 * i + nh) * 512 + hi * 128, 128),
                                                 abs_(qkb_o + nh * 256, 256),
                                                 start=(nh == 0), stop=False))
                            pr.emit("tensor", lambda e:
                                    e.matmul(pB[:, 0:256], abs_(kvb_o, 128),
                                             abs_(qs_o, 256),
                                             start=False, stop=True))
                            o_mm = pr.emit("tensor", None, inc="tT")
                            pr.emit("tensor", None, wait=[("tV", pC_free)])
                            for nh in range(2):
                                pr.emit("tensor", lambda e, hi=hi, i=i, nh=nh:
                                        e.matmul(pC[:, 0:128],
                                                 abs_(kst_o + nh * 128, 128),
                                                 abs_(vna_o + (2 * i + nh) * 512 + hi * 128, 128),
                                                 start=(nh == 0), stop=(nh == 1)))
                            kv_mm = pr.emit("tensor", None, inc="tT")
                            oev = pr.emit("scalar", lambda e, hsl=hsl, c0=c0:
                                          e.copy(abs_(ast_o[hsl] + c0, 256),
                                                 pB[:, 0:256]),
                                          wait=[("tT", o_mm), (f"gs{hsl}", ast_free[hsl])],
                                          inc="tS")
                            pB_free = oev
                            qcv = pr.emit("scalar", lambda e, hsl=hsl, c0=c0, i=i:
                                          e.mul(abs_(qst_o[hsl] + c0, 256),
                                                abs_(qs_o, 256),
                                                af_t[0:128, bdp_o + i:bdp_o + i + 1]),
                                          wait=[(f"gq{hsl}", qst_free[hsl])], inc="tS")
                            def kvup(e, i=i):
                                e.tensor_scalar(afs(kv_o, 128), afs(kv_o, 128),
                                                af_t[0:128, bdp_o + 1:bdp_o + 2],
                                                None, op0=ALU.mult)
                                e.tensor_tensor(afs(kv_o, 128), afs(kv_o, 128),
                                                pC[:, 0:128], op=ALU.add)
                                return e.tensor_copy(abs_(kvb_o, 128), afs(kv_o, 128))
                            kvu = pr.emit("vector", kvup, wait=[("tT", kv_mm)],
                                          inc="tV")
                            pC_free = kvu
                        dma("gpsimd", kvag_i_a[hh * 128:(hh + 1) * 128, :],
                            afs(kv_o, 128), "gv", wait=[("tV", pr.now("tV"))])
                        dma("gpsimd", aout_a[hh], abs_(ast_o[hsl], 2048),
                            f"gs{hsl}", wait=[("tS", pr.now("tS"))])
                        ast_free[hsl] = pr.now(f"gs{hsl}")
                        dma("gpsimd", qcor_a[hh], abs_(qst_o[hsl], 2048),
                            f"gq{hsl}")
                        qst_free[hsl] = pr.now(f"gq{hsl}")
                        kv_dG = pr.now("gv")
                    kn_rd_tV = pr.now("tV")
            AB.reset(mP)
            AFA.reset(mAf)

            pr.emit("gpsimd", lambda e: e.collective_compute(
                "AllGather", ALU.bypass,
                replica_groups=[[0, 1, 2, 3], [4, 5, 6, 7]],
                ins=[kvag_i.ap().opt()], outs=[kvag_o.ap().opt()]),
                wait=[("gv", kv_dG)], inc="cc")
            kvag_ok = pr.now("cc")

            mG = AB.mark()
            mGf = AFA.mark()
            gat_o = AB.alloc(DC * 2048)
            wqb2_o = AB.alloc(16 * 512)
            qcb_o = [AB.alloc(2048), AB.alloc(2048)]
            sbf_o = AB.alloc(128)
            tmp_o = AB.alloc(2048)
            kvt_o = AFA.alloc(256)
            sac_o = AFA.alloc(128)
            wj_o = AFA.alloc(64)
            with nc.psum_tensor(_u("pp2"), [128, 2048], F32) as pp2:
                pp2_free = [("tS", 0)] * 4
                wqb2_free = 0
                for og in range(4):
                    for hc in range(DC):
                        dma("scalar", abs_(wqb2_o + hc * 512, 512),
                            wg_bf_a[og * 512:(og + 1) * 512,
                                  hc * 128:(hc + 1) * 128],
                            "aw", transpose=True,
                            wait=[("tT", wqb2_free), ("gw", wcast_done),
                                  ("tV", kn_rd_tV), ("tT", pr.now("tT")),
                                  ("gs0", pr.now("gs0")), ("gs1", pr.now("gs1")),
                                  ("gq0", pr.now("gq0")), ("gq1", pr.now("gq1")),
                                  ("gv", pr.now("gv"))])
                    wts = pr.now("aw")
                    for oc4 in range(4):
                        oc = og * 4 + oc4
                        pr.emit("tensor", None, wait=[("aw", wts)] +
                                [pp2_free[n] for n in range(4)])
                        for hc in range(DC):
                            for nch in range(4):
                                pr.emit("tensor", lambda e, hc=hc, nch=nch, oc4=oc4:
                                        e.matmul(pp2[:, nch * 512:(nch + 1) * 512],
                                                 abs_(wqb2_o + hc * 512 + oc4 * 128, 128),
                                                 xn[:, hc * 2048 + nch * 512:
                                                    hc * 2048 + (nch + 1) * 512],
                                                 start=(hc == 0), stop=(hc == DC - 1)))
                        g_mm = pr.emit("tensor", None, inc="tT")
                        if oc4 == 3:
                            wqb2_free = g_mm
                        for nch in range(4):
                            ev = pr.emit("scalar", lambda e, oc=oc, nch=nch:
                                         e.activation(
                                             abs_(gat_o + oc * 2048 + nch * 512, 512),
                                             pp2[:, nch * 512:(nch + 1) * 512],
                                             AF.Sigmoid),
                                         wait=[("tT", g_mm),
                                               ("gs0", pr.now("gs0")),
                                               ("gs1", pr.now("gs1")),
                                               ("gq0", pr.now("gq0")),
                                               ("gq1", pr.now("gq1"))], inc="tS")
                            pp2_free[nch] = ("tS", ev)
                gate_tS = pr.now("tS")
                gate_tT = pr.now("tT")

                for hh in range(H):
                    dma("sync", xn[:, hh * 2048:(hh + 1) * 2048], aout_a[hh],
                        "sa", wait=[("gs0", pr.now("gs0")), ("gs1", pr.now("gs1")),
                                    ("tT", gate_tT)])
                att_in_tot = pr.now("sa")
                def wjc(e):
                    for j in range(4):
                        e.tensor_scalar(afs(wj_o + j * 16, 16), negsb[:, :],
                                        gapv[:, j:j + 1], None, op0=ALU.mult)
                    return e.tensor_scalar(afs(wj_o, 64), afs(wj_o, 64), 1.0, None,
                                           op0=ALU.mult)
                wj1 = pr.emit("vector", wjc, wait=[("tV", consts_ok),
                                                   ("tV", pr.now("tV"))], inc="tV")
                wj2 = pr.emit("scalar", lambda e:
                              e.activation(afs(wj_o, 64), afs(wj_o, 64), AF.Exp),
                              wait=[("tV", wj1)], inc="tS")
                def wjm(e):
                    for j in range(4):
                        e.tensor_scalar(afs(wj_o + j * 16, 16),
                                        afs(wj_o + j * 16, 16),
                                        wmv[:, j:j + 1], None, op0=ALU.mult)
                    return e.tensor_scalar(afs(wj_o, 64), afs(wj_o, 64), 1.0,
                                           None, op0=ALU.mult)
                wj3 = pr.emit("vector", wjm, wait=[("tS", wj2)], inc="tV")

                kvt_free = [0, 0]
                qcb_free = [0, 0]
                corr_tV = 0
                for hh in range(H):
                    hsl = hh % 2
                    dma("sync", abs_(qcb_o[hsl], 2048), qcor_a[hh], f"sq{hsl}",
                        wait=[("tT", qcb_free[hsl]),
                              ("gs0", pr.now("gs0")), ("gs1", pr.now("gs1")),
                              ("gq0", pr.now("gq0")), ("gq1", pr.now("gq1"))])
                    qcl = pr.now(f"sq{hsl}")
                    sacc = None
                    for j in range(4):
                        jsl = j % 2
                        dma("sync", afs(kvt_o + jsl * 128, 128),
                            kvag_o_a[(j * H + hh) * 128:(j * H + hh + 1) * 128, :],
                            f"sk{jsl}", wait=[("cc", kvag_ok),
                                              ("tV", kvt_free[jsl])])
                        kl = pr.now(f"sk{jsl}")
                        if j == 0:
                            fn = lambda e, jsl=jsl, hh=hh: e.tensor_scalar(
                                afs(sac_o, 128), afs(kvt_o + jsl * 128, 128),
                                af_t[0:128, wj_o + 0 * 16 + hh:wj_o + 0 * 16 + hh + 1],
                                None, op0=ALU.mult)
                        else:
                            fn = lambda e, jsl=jsl, hh=hh, j=j: e.scalar_tensor_tensor(
                                afs(sac_o, 128), afs(kvt_o + jsl * 128, 128),
                                af_t[0:128, wj_o + j * 16 + hh:wj_o + j * 16 + hh + 1],
                                afs(sac_o, 128), op0=ALU.mult, op1=ALU.add)
                        sacc = pr.emit("vector", fn,
                                       wait=[(f"sk{jsl}", kl), ("tV", wj3)],
                                       inc="tV")
                        kvt_free[jsl] = sacc
                    sbf = pr.emit("vector", lambda e:
                                  e.tensor_copy(abs_(sbf_o, 128), afs(sac_o, 128)),
                                  inc="tV")
                    pr.emit("tensor", None, wait=[
                        ("tV", sbf), (f"sq{hsl}", qcl), ("tV", corr_tV)] +
                        [pp2_free[n] for n in range(4)])
                    for nch in range(4):
                        pr.emit("tensor", lambda e, nch=nch, hsl=hsl:
                                e.matmul(pp2[:, nch * 512:(nch + 1) * 512],
                                         abs_(sbf_o, 128),
                                         abs_(qcb_o[hsl] + nch * 512, 512),
                                         start=True, stop=True))
                    c_mm = pr.emit("tensor", None, inc="tT")
                    qcb_free[hsl] = c_mm

                    def corr(e, hh=hh):
                        for nch in range(4):
                            e.tensor_tensor(
                                xn[:, hh * 2048 + nch * 512: hh * 2048 + (nch + 1) * 512],
                                xn[:, hh * 2048 + nch * 512: hh * 2048 + (nch + 1) * 512],
                                pp2[:, nch * 512:(nch + 1) * 512], op=ALU.add)
                        return e.tensor_scalar(afs(sac_o, 1), afs(sac_o, 1), 1.0,
                                               None, op0=ALU.mult)
                    corr_tV = pr.emit("vector", corr,
                                      wait=[("tT", c_mm), ("sa", att_in_tot),
                                            ("tS", gate_tS)], inc="tV")
                corr_done = corr_tV
                if debug:
                    for hh in range(H):
                        dma("gpsimd", dbg_ac[hh], xn[:, hh * 2048:(hh + 1) * 2048],
                            "gw", wait=[("tV", corr_done)])

                sq2_o = tmp_o
                r2_o = AFA.alloc(2560)
                rb2_o = AFA.alloc(2048)
                with nc.psum_tensor(_u("pbc2"), [128, 512], F32) as pbc2:
                    for c in range(DC):
                        sq = pr.emit("scalar", lambda e, c=c:
                                     e.activation(abs_(sq2_o, 2048),
                                                  xn[:, c * 2048:(c + 1) * 2048],
                                                  AF.Square),
                                     wait=[("tV", corr_done),
                                           ("tT", pr.now("tT"))], inc="tS")
                        for nch in range(4):
                            pr.emit("tensor", lambda e, c=c, nch=nch:
                                    e.matmul(pp2[0:1, nch * 512:(nch + 1) * 512],
                                             onesb[:, 0:1],
                                             abs_(sq2_o + nch * 512, 512),
                                             start=(c == 0), stop=(c == DC - 1)),
                                    wait=[("tS", sq)] if nch == 0 else ())
                        pr.emit("tensor", None, inc="tT")
                    rstat_tT = pr.now("tT")
                    for nch in range(4):
                        o = nch * 512

                        def rrow(e, o=o):
                            e.tensor_scalar(afs(r2_o, 512, 1), pp2[0:1, o:o + 512],
                                            1.0 / D, None, op0=ALU.mult)
                            return e.tensor_scalar(afs(r2_o, 512, 1),
                                                   afs(r2_o, 512, 1),
                                                   RMS_EPS, None, op0=ALU.add)
                        rc = pr.emit("vector", rrow, wait=[("tT", rstat_tT)],
                                     inc="tV")
                        sq3 = pr.emit("scalar", lambda e:
                                      e.activation(afs(r2_o + 512, 512, 1),
                                                   afs(r2_o, 512, 1), AF.Sqrt),
                                      wait=[("tV", rc)], inc="tS")

                        def newt2(e):
                            e.reciprocal(afs(r2_o + 1024, 512, 1),
                                         afs(r2_o + 512, 512, 1))
                            e.tensor_tensor(afs(r2_o + 1536, 512, 1),
                                            afs(r2_o + 1024, 512, 1),
                                            afs(r2_o + 1024, 512, 1), op=ALU.mult)
                            e.tensor_tensor(afs(r2_o + 1536, 512, 1),
                                            afs(r2_o + 1536, 512, 1),
                                            afs(r2_o, 512, 1), op=ALU.mult)
                            e.tensor_scalar(afs(r2_o + 1536, 512, 1),
                                            afs(r2_o + 1536, 512, 1), -0.5, 1.5,
                                            op0=ALU.mult, op1=ALU.add)
                            return e.tensor_tensor(afs(r2_o, 512, 1),
                                                   afs(r2_o + 1024, 512, 1),
                                                   afs(r2_o + 1536, 512, 1),
                                                   op=ALU.mult)
                        nw2 = pr.emit("vector", newt2, wait=[("tS", sq3)], inc="tV")
                        bc = pr.emit("tensor", lambda e:
                                     e.matmul(pbc2[:, 0:512], onesf[:, :],
                                              afs(r2_o, 512, 1),
                                              start=True, stop=True),
                                     wait=[("tV", nw2)], inc="tT")
                        pr.emit("vector", lambda e, o=o:
                                e.tensor_copy(afs(rb2_o + o, 512), pbc2[:, 0:512]),
                                wait=[("tT", bc)], inc="tV")
                    rb_tV = pr.now("tV")
                g2_tV = 0
                for c in range(DC):
                    def g2f(e, c=c):
                        e.scalar_tensor_tensor(
                            abs_(tmp_o, 2048), xn[:, c * 2048:(c + 1) * 2048],
                            nwcol[:, c:c + 1], afs(rb2_o, 2048),
                            op0=ALU.mult, op1=ALU.mult)
                        return e.tensor_tensor(abs_(gat_o + c * 2048, 2048),
                                               abs_(gat_o + c * 2048, 2048),
                                               abs_(tmp_o, 2048), op=ALU.mult)
                    g2_tV = pr.emit("vector", g2f, wait=[("tV", rb_tV),
                                                         ("tS", gate_tS)], inc="tV")

                xr2_o = [AFA.alloc(2048), AFA.alloc(2048)]
                wqb2_free2 = pr.now("tT")
                xr_free = [0, 0]
                xo_free = [0, 0]
                for og in range(4):
                    for hc in range(DC):
                        dma("scalar", abs_(wqb2_o + hc * 512, 512),
                            wo_bf_a[og * 512:(og + 1) * 512,
                                  hc * 128:(hc + 1) * 128],
                            "aw", transpose=True,
                            wait=[("tT", wqb2_free2), ("gw", wcast_done)])
                    wts = pr.now("aw")
                    for oc4 in range(4):
                        oc = og * 4 + oc4
                        hl = oc % 2
                        lx = dma("sync", afs(xr2_o[hl], 2048), xsrc[oc],
                                 f"sr{hl}",
                                 wait=[("tV", xr_free[hl]), ("tV", corr_done),
                                       (f"go{hl}", xo_free[hl])])
                        pr.emit("tensor", None, wait=[("aw", wts), ("tV", g2_tV)] +
                                [pp2_free[n] for n in range(4)] +
                                [("tV", corr_done)])
                        for hc in range(DC):
                            for nch in range(4):
                                pr.emit("tensor", lambda e, hc=hc, nch=nch, oc4=oc4:
                                        e.matmul(pp2[:, nch * 512:(nch + 1) * 512],
                                                 abs_(wqb2_o + hc * 512 + oc4 * 128, 128),
                                                 abs_(gat_o + hc * 2048 + nch * 512, 512),
                                                 start=(hc == 0), stop=(hc == DC - 1)))
                        o_mm = pr.emit("tensor", None, inc="tT")
                        if oc4 == 3:
                            wqb2_free2 = o_mm

                        def resid(e, hl=hl):
                            for nch in range(4):
                                e.tensor_tensor(
                                    afs(xr2_o[hl] + nch * 512, 512),
                                    afs(xr2_o[hl] + nch * 512, 512),
                                    pp2[:, nch * 512:(nch + 1) * 512], op=ALU.add)
                            return e.tensor_scalar(afs(sac_o, 1), afs(sac_o, 1),
                                                   1.0, None, op0=ALU.mult)
                        rv = pr.emit("vector", resid,
                                     wait=[("tT", o_mm), (f"sr{hl}", lx),
                                           (f"go{hl}", xo_free[hl])], inc="tV")
                        for nch in range(4):
                            pp2_free[nch] = ("tV", rv)
                        dma("gpsimd", xdst[oc], afs(xr2_o[hl], 2048),
                            f"go{hl}", wait=[("tV", rv)])
                        xr_free[hl] = rv
                        xo_free[hl] = pr.now(f"go{hl}")
            AB.reset(mG)
            AFA.reset(mGf)

        pr.barrier()
        last = 0
        if "f1" in phases:
            ffn_phase(X_a[0], X_a[1])
            pr.barrier()
            last = 1
        if "attn" in phases:
            attn_phase(X_a[last], X_a[2])
            pr.barrier()
            last = 2
        if "f2" in phases:
            ffn_phase(X_a[last], X_a[3])
            pr.barrier()
            last = 3

        mO = AB.mark()
        mOf = AFA.mark()
        xl_o = [AFA.alloc(2048), AFA.alloc(2048)]
        os_o = AFA.alloc(2048)
        ohi_o = AB.alloc(2048)
        olo_o = AB.alloc(2048)
        with nc.psum_tensor(_u("pth"), [128, 2048], BF16) as pth, \
             nc.psum_tensor(_u("ptl"), [128, 2048], BF16) as ptl:
            xl_free = [0, 0]
            os_free = 0
            hl_free2 = 0
            for tc in range(DC):
                sl = tc % 2
                dma("sync",
                    afs(xl_o[sl], 2048).rearrange("p (w f) -> p w f", w=16),
                    X_a[last][:, :, tc * 128:(tc + 1) * 128]
                    .rearrange("w p f -> p w f"),
                    "sc", wait=[("tV", xl_free[sl])])
                lx = pr.now("sc")

                def osplit(e, sl=sl):
                    e.tensor_copy(abs_(ohi_o, 2048), afs(xl_o[sl], 2048))
                    return e.tensor_tensor(abs_(olo_o, 2048), afs(xl_o[sl], 2048),
                                           abs_(ohi_o, 2048), op=ALU.subtract)
                sp = pr.emit("vector", osplit,
                             wait=[("sc", lx), ("tT", hl_free2)], inc="tV")
                xl_free[sl] = sp
                pr.emit("tensor", None, wait=[("tV", sp), ("sc", cst_ld)])
                for fc in range(DC):
                    pr.emit("tensor", lambda e, fc=fc:
                            e.transpose(pth[:, fc * 128:(fc + 1) * 128],
                                        abs_(ohi_o + fc * 128, 128), ident[:, :]))
                    pr.emit("tensor", lambda e, fc=fc:
                            e.transpose(ptl[:, fc * 128:(fc + 1) * 128],
                                        abs_(olo_o + fc * 128, 128), ident[:, :]))
                t_mm = pr.emit("tensor", None, inc="tT")
                hl_free2 = t_mm

                def orecomb(e):
                    e.tensor_copy(afs(os_o, 2048), pth[:, :])
                    return e.tensor_tensor(afs(os_o, 2048), afs(os_o, 2048),
                                           ptl[:, :], op=ALU.add)
                ev = pr.emit("vector", orecomb,
                             wait=[("tT", t_mm), ("sa", os_free)], inc="tV")
                dma("sync", out[tc * 128:(tc + 1) * 128, :], afs(os_o, 2048),
                    "sa", wait=[("tV", ev)])
                os_free = pr.now("sa")
        AB.reset(mO)
        AFA.reset(mOf)

        pr.emit("sync", None,
                wait=[(q, pr.now(q)) for q in pr.dma_sems])

        block = stack.enter_context(nc.Block())
        for eng in P.ENGINES:
            def body(e, eng=eng):
                for fn in pr.ops[eng]:
                    fn(e)
            getattr(block, eng)(body)
    return nc


def _consts():
    p = np.arange(128, dtype=np.float32)
    c = {}
    c["cident"] = np.eye(128, dtype=BF)
    c["cidentf"] = np.eye(128, dtype=np.float32)
    c["cones_b"] = np.ones((128, 1), dtype=BF)
    c["cones_f"] = np.ones((1, 128), dtype=np.float32)
    c["ciotaf"] = np.broadcast_to(np.arange(256, dtype=np.float32)[None, :],
                                  (128, 256)).copy()
    c["ciotapn"] = np.stack([p, p + 128.0], axis=1).astype(np.float32)
    c["ciotakd"] = np.stack([255.0 - p, 255.0 - (p + 128.0)], axis=1).astype(np.float32)
    c["ciotablk"] = np.broadcast_to(
        (np.arange(8, dtype=np.float32) * 256.0)[None, :], (128, 8)).copy()
    m = np.zeros((128, 512), dtype=np.float32)
    mm_ = np.arange(256)[None, :]
    m[:, 0:256] = (mm_ >= p[:, None]).astype(np.float32)
    m[:, 256:512] = (mm_ >= (p[:, None] + 128)).astype(np.float32)
    c["cmask"] = m.astype(BF)
    return c


def _pack_idx(core, gate_idx, up_idx, down_idx):
    mats = {"gate": gate_idx, "up": up_idx, "down": down_idx}
    blocks = []
    for t in range(core * TPC, (core + 1) * TPC):
        mname, r0, c0 = _tile_src(t)
        blocks.append(mats[mname][r0:r0 + 128, c0:c0 + 128])
    return np.ascontiguousarray(np.concatenate(blocks, axis=1))


_NC_CACHE = {}


def kernel(L0_attn_x, L0_attn_slope_rate, L0_attn_Wqkv, L0_attn_Wg,
           L0_attn_norm_w, L0_attn_Wo, L0_ffn_lut, L0_ffn_gate_idx,
           L0_ffn_up_idx, L0_ffn_down_idx):
    x = np.ascontiguousarray(np.asarray(L0_attn_x, dtype=np.float32)
                             .reshape(B * SEQ, D))
    slope = np.ascontiguousarray(np.asarray(L0_attn_slope_rate,
                                            dtype=np.float32).reshape(16))
    wqkv = np.ascontiguousarray(np.asarray(L0_attn_Wqkv, dtype=np.float32))
    wg = np.ascontiguousarray(np.asarray(L0_attn_Wg, dtype=np.float32))
    wo = np.ascontiguousarray(np.asarray(L0_attn_Wo, dtype=np.float32))
    nw = np.ascontiguousarray(np.asarray(L0_attn_norm_w, dtype=np.float32)
                              .reshape(DC, 128).T)
    lutv = np.ascontiguousarray(np.asarray(L0_ffn_lut, dtype=np.float32))
    gi = np.asarray(L0_ffn_gate_idx, dtype=np.int32)
    ui = np.asarray(L0_ffn_up_idx, dtype=np.int32)
    di = np.asarray(L0_ffn_down_idx, dtype=np.int32)

    consts = _consts()
    key = "full"
    if key not in _NC_CACHE:
        _NC_CACHE[key] = build_program()
    nc = _NC_CACHE[key]

    in_maps = []
    for c in range(NCORES):
        bp = c % 4
        gap = np.zeros((1, 4), dtype=np.float32)
        wm = np.zeros((1, 4), dtype=np.float32)
        for j in range(4):
            if j < bp:
                gap[0, j] = 2048.0 * (bp - 1 - j)
                wm[0, j] = 1.0
        m = dict(consts)
        m["x"] = np.ascontiguousarray(x[c * T:(c + 1) * T])
        m["idxs"] = _pack_idx(c, gi, ui, di)
        m["lut"] = lutv
        m["slope"] = slope
        m["wqkv"] = wqkv
        m["wg"] = wg
        m["wo"] = wo
        m["nw"] = nw
        m["cgap"] = gap
        m["cwm"] = wm
        in_maps.append(m)

    res = run_bass_kernel_spmd(nc, in_maps, core_ids=list(range(NCORES)))
    outs = [np.asarray(r["out"]) for r in res.results]
    return np.concatenate(outs, axis=0).reshape(B, SEQ, D).astype(np.float32)


# version 04
# speedup vs baseline: 1.0060x; 1.0060x over previous
import numpy as np
import ml_dtypes
from contextlib import ExitStack

import concourse.bass as bass
import concourse.mybir as mybir
from concourse.bass_utils import run_bass_kernel_spmd

F32 = mybir.dt.float32
BF16 = mybir.dt.bfloat16
I32 = mybir.dt.int32
AF = mybir.ActivationFunctionType
ALU = mybir.AluOpType
BF = ml_dtypes.bfloat16

B, SEQ, D = 2, 8192, 2048
NCORES = 8
T = (B * SEQ) // NCORES
DC = D // 128
H, DH = 16, 128
BLKA = 256
NBLK = T // BLKA
INTER = 5504
JCN = INTER // 128
GH = 4
NGRP = H // GH
LN_EPS = 1e-5
RMS_EPS = float(np.finfo(np.float32).eps)
NT = T // 512
STR = 512
NSTR = T // STR

TILES = 3 * 688
TPC = TILES // NCORES
SLAB = 8
NSLAB = (TPC + SLAB - 1) // SLAB


def _tile_src(t):
    if t < 688:
        return ("gate", (t // 16) * 128, (t % 16) * 128)
    if t < 1376:
        t -= 688
        return ("up", (t // 16) * 128, (t % 16) * 128)
    t -= 1376
    return ("down", (t // 43) * 128, (t % 43) * 128)


def _wqkv_perm():
    perm = []
    for g in range(NGRP):
        for role in range(3):
            for i in range(GH):
                perm.append((g * GH + i) * 3 + role)
    return perm


class Arena:

    def __init__(self, t, cols):
        self.t = t
        self.cols = cols
        self.off = 0

    def alloc(self, n, align=64):
        o = (self.off + align - 1) // align * align
        assert o + n <= self.cols, f"arena overflow {o}+{n}>{self.cols}"
        self.off = o + n
        return o

    def mark(self):
        return self.off

    def reset(self, m):
        self.off = m


class P:
    ENGINES = ("sync", "gpsimd", "tensor", "vector", "scalar")

    def __init__(self, sems):
        self.ops = {e: [] for e in self.ENGINES}
        self.sems = sems
        self.c = {k: 0 for k in sems}
        self.last = {}
        self.last_inc = {}

    def emit(self, eng, fn=None, wait=(), inc=None, inc_by=1):
        wait = [(self.sems[s], c) for (s, c) in wait if c > 0]
        tgt = None
        semh = by = None
        if inc is not None:
            self.c[inc] += inc_by
            tgt = self.c[inc]
            semh = self.sems[inc]
            by = inc_by

        def run(e, eng=eng, wait=wait, fn=fn, semh=semh, by=by):
            for h, c in wait:
                e.wait_ge(h, c)
            if fn is not None:
                inst = fn(e)
                self.last[eng] = inst
                if semh is not None:
                    inst.then_inc(semh, by)
                    self.last_inc[eng] = (semh, self._rt_val(semh, by))
                else:
                    self.last_inc[eng] = None
            elif semh is not None:
                att = eng in ("tensor", "vector", "scalar")
                if (att and self.last.get(eng) is not None
                        and self.last_inc.get(eng) is None):
                    self.last[eng].then_inc(semh, by)
                    self.last_inc[eng] = (semh, self._rt_val(semh, by))
                elif att and self.last_inc.get(eng) is not None:
                    ps, pv = self.last_inc[eng]
                    e.wait_ge(ps, pv)
                    e.nop().then_inc(semh, by)
                    self.last_inc[eng] = (semh, self._rt_val(semh, by))
                else:
                    e.nop().then_inc(semh, by)

        self.ops[eng].append(run)
        return tgt

    def _rt_val(self, semh, by):
        key = id(semh)
        self.rt = getattr(self, "rt", {})
        self.rt[key] = self.rt.get(key, 0) + by
        return self.rt[key]

    def now(self, sem):
        return self.c[sem]

    def barrier(self):
        marks = {}
        dmas = [(q, self.c[q]) for q in self.dma_sems]
        for e in self.ENGINES:
            self.emit(e, None, inc="b_" + e)
            marks[e] = self.c["b_" + e]
        for e in self.ENGINES:
            self.emit(e, None,
                      wait=[("b_" + x, marks[x]) for x in self.ENGINES if x != e]
                      + dmas)


def build_program(phases=("conv", "f1", "attn", "f2"), debug=False):
    nc = bass.Bass("TRN2", target_bir_lowering=False)

    xin = nc.dram_tensor("x", [T, D], F32, kind="ExternalInput").ap()
    idxs = nc.dram_tensor("idxs", [128, TPC * 128], I32, kind="ExternalInput").ap()
    lut = nc.dram_tensor("lut", [16], F32, kind="ExternalInput").ap()
    slope = nc.dram_tensor("slope", [16], F32, kind="ExternalInput").ap()
    wqkv_in = nc.dram_tensor("wqkv", [3 * H * DH, D], F32, kind="ExternalInput").ap()
    wg_in = nc.dram_tensor("wg", [D, D], F32, kind="ExternalInput").ap()
    wo_in = nc.dram_tensor("wo", [D, D], F32, kind="ExternalInput").ap()
    nw_in = nc.dram_tensor("nw", [128, 16], F32, kind="ExternalInput").ap()
    cident = nc.dram_tensor("cident", [128, 128], BF16, kind="ExternalInput").ap()
    cidentf = nc.dram_tensor("cidentf", [128, 128], F32, kind="ExternalInput").ap()
    cones_b = nc.dram_tensor("cones_b", [128, 1], BF16, kind="ExternalInput").ap()
    cones_f = nc.dram_tensor("cones_f", [1, 128], F32, kind="ExternalInput").ap()
    ciotaf = nc.dram_tensor("ciotaf", [128, 256], F32, kind="ExternalInput").ap()
    ciotapn = nc.dram_tensor("ciotapn", [128, 2], F32, kind="ExternalInput").ap()
    ciotakd = nc.dram_tensor("ciotakd", [128, 2], F32, kind="ExternalInput").ap()
    ciotablk = nc.dram_tensor("ciotablk", [128, 8], F32, kind="ExternalInput").ap()
    cmask = nc.dram_tensor("cmask", [128, 512], BF16, kind="ExternalInput").ap()
    cgap = nc.dram_tensor("cgap", [1, 4], F32, kind="ExternalInput").ap()
    cwm = nc.dram_tensor("cwm", [1, 4], F32, kind="ExternalInput").ap()

    out = nc.dram_tensor("out", [T, D], F32, kind="ExternalOutput").ap()

    agin = nc.dram_tensor("agin", [TPC, 128, 128], BF16)
    agin_a = agin.ap()
    ffnw = nc.dram_tensor("ffnw", [TILES, 128, 128], BF16)
    ffnw_a = ffnw.ap()
    X = [nc.dram_tensor(f"X{i}", [DC, 128, T], F32) for i in range(4)]
    X_a = [t.ap() for t in X]
    wqkv_bf = nc.dram_tensor("wqkv_bf", [48, 128, D], BF16)
    wqkv_bf_a = wqkv_bf.ap()
    wg_bf = nc.dram_tensor("wg_bf", [D, D], BF16)
    wg_bf_a = wg_bf.ap()
    wo_bf = nc.dram_tensor("wo_bf", [D, D], BF16)
    wo_bf_a = wo_bf.ap()
    kstage = nc.dram_tensor("kstage", [GH * 128, T], BF16)
    kstage_a = kstage.ap()
    aout_h = nc.dram_tensor("aout_h", [H, 128, T], BF16)
    aout_a = aout_h.ap()
    qcor_h = nc.dram_tensor("qcor_h", [H, 128, T], BF16)
    qcor_a = qcor_h.ap()
    kvag_i = nc.dram_tensor("kvag_i", [H * 128, 128], F32)
    kvag_i_a = kvag_i.ap()
    kvag_o = nc.dram_tensor("kvag_o", [4 * H * 128, 128], F32)
    kvag_o_a = kvag_o.ap()
    if debug:
        dbg_xn = nc.dram_tensor("dbg_xn", [DC, 128, 2048], BF16,
                                kind="ExternalOutput").ap()
        dbg_h = nc.dram_tensor("dbg_h", [JCN, 128, STR], BF16,
                               kind="ExternalOutput").ap()
        dbg_mr = nc.dram_tensor("dbg_mr", [2, 128, 2048], F32,
                                kind="ExternalOutput").ap()
        dbg_ac = nc.dram_tensor("dbg_ac", [H, 128, 2048], BF16,
                                kind="ExternalOutput").ap()
        dbg_fw = nc.dram_tensor("dbg_fw", [64, 128, 128], BF16,
                                kind="ExternalOutput").ap()
        dbg_x2 = nc.dram_tensor("dbg_x2", [DC, 128, T], F32,
                                kind="ExternalOutput").ap()
        dbg_kv = nc.dram_tensor("dbg_kv", [4 * H * 128, 128], F32,
                                kind="ExternalOutput").ap()
        dbg_ag = nc.dram_tensor("dbg_ag", [TPC, 128, 128], BF16,
                                kind="ExternalOutput").ap()

    stack = ExitStack()
    _uidc = [0]

    def _u(name):
        _uidc[0] += 1
        return f"{name}_{_uidc[0]}"
    with stack:
        def sbuf(name, p, f, dt):
            return stack.enter_context(nc.sbuf_tensor(name, [p, f], dt))

        sems = {}
        DMA_SEMS = (
            "sc", "sx0", "sx1", "sg0", "sg1", "su0", "su1", "sd0", "sd1",
            "sr0", "sr1", "sa", "sq0", "sq1", "sk0", "sk1",
            "gx", "gw", "gc0", "gc1", "ga0", "ga1", "go0", "go1",
            "gk", "gs0", "gs1", "gq0", "gq1", "gv",
            "ax0", "ax1", "aw", "an",
        )
        for s in DMA_SEMS + ("cc", "tT", "tV", "tS",
                             "b_sync", "b_gpsimd", "b_tensor", "b_vector",
                             "b_scalar"):
            sems[s] = stack.enter_context(nc.semaphore(s))
        pr = P(sems)
        pr.dma_sems = DMA_SEMS

        ident = sbuf("ident", 128, 128, BF16)
        identf = sbuf("identf", 128, 128, F32)
        onesb = sbuf("onesb", 128, 1, BF16)
        onesf = sbuf("onesf", 1, 128, F32)
        iotaf = sbuf("iotaf", 128, 256, F32)
        iotapn = sbuf("iotapn", 128, 2, F32)
        iotakd = sbuf("iotakd", 128, 2, F32)
        iotablk = sbuf("iotablk", 128, 8, F32)
        maskc = sbuf("maskc", 128, 512, BF16)
        gapv = sbuf("gapv", 128, 4, F32)
        wmv = sbuf("wmv", 128, 4, F32)
        sbc = sbuf("sbc", 128, 16, F32)
        negsb = sbuf("negsb", 128, 16, F32)
        nwcol = sbuf("nwcol", 128, 16, F32)
        lut128 = sbuf("lut128", 128, 16, F32)
        xn = sbuf("xn", 128, DC * 2048, BF16)
        AB_COLS = 49664
        AF_COLS = 9216
        ab_t = sbuf("ab", 128, AB_COLS, BF16)
        af_t = sbuf("af", 128, AF_COLS, F32)
        AB = Arena(ab_t, AB_COLS)
        AFA = Arena(af_t, AF_COLS)

        def abs_(o, n):
            return ab_t[:, o:o + n]

        def afs(o, n, p=128):
            return af_t[0:p, o:o + n]

        def dma(eng, outap, inap, sem, transpose=False, wait=()):
            return pr.emit(eng, lambda e, o=outap, i=inap, t=transpose:
                           e.dma_start(out=o, in_=i, transpose=t),
                           wait=wait, inc=sem, inc_by=16)

        for outap, inap in (
            (ident[:, :], cident), (identf[:, :], cidentf),
            (onesb[:, :], cones_b), (onesf[:, :], cones_f),
            (iotaf[:, :], ciotaf), (iotapn[:, :], ciotapn), (iotakd[:, :], ciotakd),
            (iotablk[:, :], ciotablk), (maskc[:, :], cmask),
            (gapv[:, :], cgap[:, :].partition_broadcast(128)),
            (wmv[:, :], cwm[:, :].partition_broadcast(128)),
            (sbc[:, :], slope[None, :].partition_broadcast(128)),
            (nwcol[:, :], nw_in),
            (lut128[:, :], lut[None, :].partition_broadcast(128)),
        ):
            dma("sync", outap, inap, "sc")
        cst_ld = pr.now("sc")
        consts_ok = pr.emit("vector", lambda e: e.tensor_scalar(
            negsb[:, :], sbc[:, :], -1.0, None, op0=ALU.mult),
            wait=[("sc", cst_ld)], inc="tV")

        perm = _wqkv_perm()
        for n in range(48):
            dma("gpsimd", wqkv_bf_a[n], wqkv_in[perm[n] * 128:perm[n] * 128 + 128, :],
                "gw")
        for i in range(4):
            dma("gpsimd", wg_bf_a[i * 512:(i + 1) * 512, :],
                wg_in[i * 512:(i + 1) * 512, :], "gw")
            dma("gpsimd", wo_bf_a[i * 512:(i + 1) * 512, :],
                wo_in[i * 512:(i + 1) * 512, :], "gw")
        wcast_done = pr.now("gw")

        m0 = AB.mark()
        m0f = AFA.mark()
        xpl_o = [AFA.alloc(2048), AFA.alloc(2048)]
        xst_o = AFA.alloc(2048)
        xhi_o = AB.alloc(2048)
        xlo_o = AB.alloc(2048)
        with nc.psum_tensor(_u("pxh"), [128, 2048], BF16) as pxh, \
             nc.psum_tensor(_u("pxl"), [128, 2048], BF16) as pxl:
            xpl_free = [0, 0]
            xst_free = 0
            hl_free = 0
            for tc in range(DC):
                sl = tc % 2
                dma("sync", afs(xpl_o[sl], 2048), xin[tc * 128:(tc + 1) * 128, :],
                    f"sx{sl}", wait=[("tV", xpl_free[sl])])
                lx = pr.now(f"sx{sl}")

                def split(e, sl=sl):
                    e.tensor_copy(abs_(xhi_o, 2048), afs(xpl_o[sl], 2048))
                    return e.tensor_tensor(abs_(xlo_o, 2048), afs(xpl_o[sl], 2048),
                                           abs_(xhi_o, 2048), op=ALU.subtract)
                sp = pr.emit("vector", split,
                             wait=[(f"sx{sl}", lx), ("tT", hl_free),
                                   ("tV", consts_ok)], inc="tV")
                xpl_free[sl] = sp
                pr.emit("tensor", None, wait=[("tV", sp), ("sc", cst_ld),
                                              ("tV", pr.now("tV"))])
                for fc in range(DC):
                    pr.emit("tensor", lambda e, fc=fc:
                            e.transpose(pxh[:, fc * 128:(fc + 1) * 128],
                                        abs_(xhi_o + fc * 128, 128), ident[:, :]))
                    pr.emit("tensor", lambda e, fc=fc:
                            e.transpose(pxl[:, fc * 128:(fc + 1) * 128],
                                        abs_(xlo_o + fc * 128, 128), ident[:, :]))
                t_mm = pr.emit("tensor", None, inc="tT")
                hl_free = t_mm

                def recomb(e):
                    e.tensor_copy(afs(xst_o, 2048), pxh[:, :])
                    return e.tensor_tensor(afs(xst_o, 2048), afs(xst_o, 2048),
                                           pxl[:, :], op=ALU.add)
                ev = pr.emit("vector", recomb,
                             wait=[("tT", t_mm), ("sa", xst_free)], inc="tV")
                dma("sync",
                    X_a[0][:, :, tc * 128:(tc + 1) * 128]
                    .rearrange("w p f -> p w f"),
                    afs(xst_o, 2048).rearrange("p (w f) -> p w f", w=16),
                    "sa", wait=[("tV", ev)])
                xst_free = pr.now("sa")
        AFA.reset(m0f)

        if "conv" in phases:
            mC = AB.mark()
            cv_o = [AB.alloc(SLAB * 128), AB.alloc(SLAB * 128)]
            wn_o = [AB.alloc(SLAB * 128), AB.alloc(SLAB * 128)]
            st_o = [AB.alloc(SLAB * 128), AB.alloc(SLAB * 128)]
            tm_o = AB.alloc(SLAB * 128)
            with nc.psum_tensor(_u("ptc"), [128, 2 * SLAB * 128], BF16) as ptc:
                in_free = [0, 0]
                wn_free = [0, 0]
                st_free = [0, 0]
                pt_free = [0, 0]
                for s in range(NSLAB):
                    w = min(SLAB, TPC - s * SLAB)
                    sl = s % 2
                    cw = w * 128
                    dma("gpsimd", abs_(cv_o[sl], cw),
                        idxs[:, s * SLAB * 128: s * SLAB * 128 + cw],
                        f"gc{sl}", wait=[("tV", in_free[sl])])
                    ldc = pr.now(f"gc{sl}")
                    pr.emit("vector", None, wait=[(f"gc{sl}", ldc), ("tV", consts_ok),
                                                  ("tT", wn_free[sl])])
                    for k in range(16):
                        dst = wn_o[sl] if k == 0 else tm_o
                        pr.emit("vector", lambda e, sl=sl, cw=cw, k=k, dst=dst:
                                e.tensor_scalar(abs_(dst, cw), abs_(cv_o[sl], cw),
                                                float(k), lut128[:, k:k + 1],
                                                op0=ALU.is_equal, op1=ALU.mult))
                        if k > 0:
                            pr.emit("vector", lambda e, sl=sl, cw=cw:
                                    e.tensor_tensor(abs_(wn_o[sl], cw),
                                                    abs_(wn_o[sl], cw),
                                                    abs_(tm_o, cw), op=ALU.add))
                    cvt = pr.emit("vector", None, inc="tV")
                    in_free[sl] = cvt
                    pr.emit("tensor", None, wait=[("tV", cvt), ("tS", pt_free[sl]),
                                                  ("sc", cst_ld)])
                    for j in range(w):
                        pr.emit("tensor", lambda e, sl=sl, j=j:
                                e.transpose(ptc[:, sl * SLAB * 128 + j * 128:
                                                sl * SLAB * 128 + (j + 1) * 128],
                                            abs_(wn_o[sl] + j * 128, 128),
                                            ident[:, :]))
                    trc = pr.emit("tensor", None, inc="tT")
                    wn_free[sl] = trc
                    ev = pr.emit("scalar", lambda e, sl=sl, cw=cw:
                                 e.copy(abs_(st_o[sl], cw),
                                        ptc[:, sl * SLAB * 128: sl * SLAB * 128 + cw]),
                                 wait=[("tT", trc), (f"ga{sl}", st_free[sl])], inc="tS")
                    pt_free[sl] = ev
                    dma("gpsimd",
                        agin_a[s * SLAB: s * SLAB + w].rearrange("w p f -> p w f"),
                        abs_(st_o[sl], cw).rearrange("p (w f) -> p w f", w=w),
                        f"ga{sl}", wait=[("tS", ev)])
                    st_free[sl] = pr.now(f"ga{sl}")
            pr.emit("gpsimd", lambda e: e.collective_compute(
                "AllGather", ALU.bypass, replica_groups=[list(range(NCORES))],
                ins=[agin.ap().opt()], outs=[ffnw.ap().opt()]),
                wait=[("ga0", pr.now("ga0")), ("ga1", pr.now("ga1"))], inc="cc")
        AB.reset(m0)
        if debug and "conv" in phases:
            dma("gpsimd", dbg_fw[0:64], ffnw_a[0:64], "gw",
                wait=[("cc", pr.now("cc"))])
            dma("gpsimd", dbg_ag[:], agin_a[:], "gw",
                wait=[("cc", pr.now("cc"))])
        ffnw_ok = pr.now("cc")

        def ln_to_xn(xsrc):
            mL = AB.mark()
            mF = AFA.mark()
            xo = [AB.alloc(2048), AB.alloc(2048)]
            sq_o = AB.alloc(2048)
            xf_o = [AFA.alloc(512), AFA.alloc(512)]
            tf_o = AFA.alloc(512)
            mb_o = AFA.alloc(2048)
            rb_o = AFA.alloc(2048)
            r_o = AFA.alloc(2560)
            rowm = lambda nch: afs(r_o + 0, 512, 1)
            rowv = lambda nch: afs(r_o + 512, 512, 1)
            rowt = lambda nch: afs(r_o + 1024, 512, 1)
            rowr0 = lambda nch: afs(r_o + 1536, 512, 1)
            with nc.psum_tensor(_u("pstat"), [64, 2048], F32) as pstat, \
                 nc.psum_tensor(_u("pbc"), [128, 1024], F32) as pbc:
                xf = [0, 0]
                for c in range(DC):
                    sl = c % 2
                    dma("gpsimd", abs_(xo[sl], 2048), xsrc[c], f"gc{sl}",
                        wait=[("tT", xf[sl])])
                    lc = pr.now(f"gc{sl}")
                    sq = pr.emit("scalar", lambda e, sl=sl:
                                 e.activation(abs_(sq_o, 2048), abs_(xo[sl], 2048),
                                              AF.Square),
                                 wait=[(f"gc{sl}", lc), ("tT", pr.now("tT"))],
                                 inc="tS")
                    pr.emit("tensor", None, wait=[(f"gc{sl}", lc), ("tS", sq)])
                    for nch in range(4):
                        pr.emit("tensor", lambda e, sl=sl, c=c, nch=nch:
                                e.matmul(pstat[0:1, nch * 512:(nch + 1) * 512],
                                         onesb[:, 0:1],
                                         abs_(xo[sl] + nch * 512, 512),
                                         start=(c == 0), stop=(c == DC - 1)))
                        pr.emit("tensor", lambda e, c=c, nch=nch:
                                e.matmul(pstat[32:33, nch * 512:(nch + 1) * 512],
                                         onesb[:, 0:1],
                                         abs_(sq_o + nch * 512, 512),
                                         start=(c == 0), stop=(c == DC - 1)))
                    xf[sl] = pr.emit("tensor", None, inc="tT")
                stats_tT = pr.now("tT")
                for nch in range(4):
                    o = nch * 512

                    def rowc(e, o=o, nch=nch):
                        e.tensor_scalar(rowm(nch), pstat[0:1, o:o + 512],
                                        1.0 / D, None, op0=ALU.mult)
                        e.tensor_scalar(rowv(nch), pstat[32:33, o:o + 512],
                                        1.0 / D, None, op0=ALU.mult)
                        e.tensor_tensor(rowt(nch), rowm(nch), rowm(nch), op=ALU.mult)
                        e.tensor_tensor(rowv(nch), rowv(nch), rowt(nch),
                                        op=ALU.subtract)
                        return e.tensor_scalar(rowv(nch), rowv(nch), LN_EPS, None,
                                               op0=ALU.add)
                    rc = pr.emit("vector", rowc, wait=[("tT", stats_tT)], inc="tV")
                    sq2 = pr.emit("scalar", lambda e, nch=nch:
                                  e.activation(rowt(nch), rowv(nch), AF.Sqrt),
                                  wait=[("tV", rc)], inc="tS")

                    def newt(e, nch=nch):
                        e.reciprocal(rowr0(nch), rowt(nch))
                        e.tensor_tensor(rowt(nch), rowr0(nch), rowr0(nch),
                                        op=ALU.mult)
                        e.tensor_tensor(rowt(nch), rowt(nch), rowv(nch), op=ALU.mult)
                        e.tensor_scalar(rowt(nch), rowt(nch), -0.5, 1.5,
                                        op0=ALU.mult, op1=ALU.add)
                        return e.tensor_tensor(rowv(nch), rowr0(nch), rowt(nch),
                                               op=ALU.mult)
                    nw = pr.emit("vector", newt, wait=[("tS", sq2)], inc="tV")
                    pr.emit("tensor", lambda e, nch=nch:
                            e.matmul(pbc[:, 0:512], onesf[:, :], rowm(nch),
                                     start=True, stop=True),
                            wait=[("tV", nw), ("tV", pr.now("tV"))])
                    bc = pr.emit("tensor", lambda e, nch=nch:
                                 e.matmul(pbc[:, 512:1024], onesf[:, :], rowv(nch),
                                          start=True, stop=True), inc="tT")
                    pr.emit("vector", lambda e, o=o: (
                            e.tensor_copy(afs(mb_o + o, 512), pbc[:, 0:512]),
                            e.tensor_copy(afs(rb_o + o, 512), pbc[:, 512:1024]))[-1],
                            wait=[("tT", bc)], inc="tV")
                bcast_tV = pr.now("tV")
            if debug:
                dma("gpsimd", dbg_mr[0], afs(mb_o, 2048), "gw",
                    wait=[("tV", bcast_tV)])
                dma("gpsimd", dbg_mr[1], afs(rb_o, 2048), "gw")
            xf2 = [0, 0]
            for c in range(DC):
                for nch in range(4):
                    sl = nch % 2
                    dma("sync", afs(xf_o[sl], 512),
                        xsrc[c][:, nch * 512:(nch + 1) * 512], f"sx{sl}",
                        wait=[("tV", xf2[sl]), ("tT", stats_tT)])
                    lc = pr.now(f"sx{sl}")

                    def app(e, c=c, sl=sl, nch=nch):
                        e.tensor_tensor(afs(tf_o, 512), afs(xf_o[sl], 512),
                                        afs(mb_o + nch * 512, 512),
                                        op=ALU.subtract)
                        return e.tensor_tensor(
                            xn[:, c * 2048 + nch * 512: c * 2048 + (nch + 1) * 512],
                            afs(tf_o, 512), afs(rb_o + nch * 512, 512),
                            op=ALU.mult)
                    xf2[sl] = pr.emit("vector", app,
                                      wait=[(f"sx{sl}", lc), ("tV", bcast_tV)],
                                      inc="tV")
            AFA.reset(mF)
            return pr.now("tV")

        def ffn_phase(xsrc, xdst):
            mP = AB.mark()
            xn_tV = ln_to_xn(xsrc)
            mF = AB.mark()
            wgt_o = AB.alloc(4096)
            wup_o = AB.alloc(4096)
            wdb_o = AB.alloc(JCN * 128)
            sil_o = AB.alloc(1024)
            har_o = AB.alloc(JCN * STR)
            mFf = AFA.mark()
            xr_o = [AFA.alloc(512), AFA.alloc(512)]
            xo_o = [AFA.alloc(512), AFA.alloc(512)]
            with nc.psum_tensor(_u("pg"), [128, 1024], F32) as pg, \
                 nc.psum_tensor(_u("pu"), [128, 1024], F32) as pu, \
                 nc.psum_tensor(_u("pd"), [128, 1024], F32) as pd:
                wgt_free = [0, 0]; wup_free = [0, 0]
                pg_free = [0, 0]; pu_free = [0, 0]; pd_free = [0, 0]
                wdb_free = [0, 0]
                har_rd = 0
                xr_free = [0, 0]; xo_free = [0, 0]
                for st in range(NSTR):
                    tok0 = st * STR
                    for jc in range(JCN):
                        hl = jc % 2
                        cg = dma("sync", abs_(wgt_o + hl * 2048, 2048)
                                 .rearrange("p (w f) -> p w f", w=16),
                                 ffnw_a[jc * 16:(jc + 1) * 16]
                                 .rearrange("w p f -> p w f"),
                                 f"sg{hl}",
                                 wait=[("tT", wgt_free[hl]), ("cc", ffnw_ok)])
                        cu = dma("sync", abs_(wup_o + hl * 2048, 2048)
                                 .rearrange("p (w f) -> p w f", w=16),
                                 ffnw_a[688 + jc * 16:688 + (jc + 1) * 16]
                                 .rearrange("w p f -> p w f"), f"su{hl}",
                                 wait=[("tT", wup_free[hl])])
                        pr.emit("tensor", None, wait=[
                            (f"sg{hl}", cg), (f"su{hl}", cu), ("tV", xn_tV),
                            ("tS", pg_free[hl]), ("tV", pu_free[hl]),
                            ("tT", har_rd)])
                        for hc in range(DC):
                            pr.emit("tensor", lambda e, hl=hl, hc=hc, tok0=tok0:
                                    e.matmul(pg[:, hl * 512:hl * 512 + 512],
                                             abs_(wgt_o + hl * 2048 + hc * 128, 128),
                                             xn[:, hc * 2048 + tok0: hc * 2048 + tok0 + 512],
                                             start=(hc == 0), stop=(hc == DC - 1)))
                        g_mm = pr.emit("tensor", None, inc="tT")
                        wgt_free[hl] = g_mm
                        for hc in range(DC):
                            pr.emit("tensor", lambda e, hl=hl, hc=hc, tok0=tok0:
                                    e.matmul(pu[:, hl * 512:hl * 512 + 512],
                                             abs_(wup_o + hl * 2048 + hc * 128, 128),
                                             xn[:, hc * 2048 + tok0: hc * 2048 + tok0 + 512],
                                             start=(hc == 0), stop=(hc == DC - 1)))
                        u_mm = pr.emit("tensor", None, inc="tT")
                        wup_free[hl] = u_mm
                        sl_ev = pr.emit("scalar", lambda e, hl=hl:
                                        e.activation(abs_(sil_o + hl * 512, 512),
                                                     pg[:, hl * 512:hl * 512 + 512],
                                                     AF.Silu),
                                        wait=[("tT", g_mm), ("tV", pu_free[hl])],
                                        inc="tS")
                        pg_free[hl] = sl_ev
                        h_ev = pr.emit("vector", lambda e, hl=hl, jc=jc:
                                       e.scalar_tensor_tensor(
                                           abs_(har_o + jc * STR + 0, 512),
                                           pu[:, hl * 512:hl * 512 + 512], 1.0,
                                           abs_(sil_o + hl * 512, 512),
                                           op0=ALU.mult, op1=ALU.mult),
                                       wait=[("tT", u_mm), ("tS", sl_ev)], inc="tV")
                        pu_free[hl] = h_ev
                    h_tV = pr.now("tV")
                    if debug and st == 0:
                        for c in range(DC):
                            dma("gpsimd", dbg_xn[c], xn[:, c * 2048:(c + 1) * 2048],
                                "gx", wait=[("tV", xn_tV)])
                        for jc in range(JCN):
                            dma("gpsimd", dbg_h[jc], abs_(har_o + jc * STR, STR),
                                "gx", wait=[("tV", h_tV)])
                    for oc in range(DC):
                        hl = oc % 2
                        t0 = 1376 + oc * 43
                        cd0 = dma("sync", abs_(wdb_o, 22 * 128)
                                  .rearrange("p (w f) -> p w f", w=22),
                                  ffnw_a[t0:t0 + 22].rearrange("w p f -> p w f"),
                                  "sd0",
                                  wait=[("tT", wdb_free[0])])
                        cd1 = dma("sync", abs_(wdb_o + 22 * 128, 21 * 128)
                                  .rearrange("p (w f) -> p w f", w=21),
                                  ffnw_a[t0 + 22:t0 + 43]
                                  .rearrange("w p f -> p w f"), "sd1",
                                  wait=[("tT", wdb_free[1])])
                        lx = dma("sync", afs(xr_o[hl], 512),
                                 xsrc[oc][:, tok0:tok0 + 512], f"sr{hl}",
                                 wait=[("tV", max(xr_free[hl], xn_tV))])
                        pr.emit("tensor", None, wait=[
                            ("sd0", cd0), ("sd1", cd1), ("tV", h_tV),
                            ("tV", pd_free[hl])])
                        for jc in range(JCN):
                            pr.emit("tensor", lambda e, hl=hl, jc=jc:
                                    e.matmul(pd[:, hl * 512:hl * 512 + 512],
                                             abs_(wdb_o + jc * 128, 128),
                                             abs_(har_o + jc * STR, 512),
                                             start=(jc == 0), stop=(jc == JCN - 1)))
                            if jc == 21:
                                pr.emit("tensor", None, inc="tT")
                                wdb_free[0] = pr.now("tT")
                        d_mm = pr.emit("tensor", None, inc="tT")
                        wdb_free[1] = d_mm
                        if oc == DC - 1:
                            har_rd = d_mm
                        x_ev = pr.emit("vector", lambda e, hl=hl:
                                       e.scalar_tensor_tensor(
                                           afs(xo_o[hl], 512),
                                           pd[:, hl * 512:hl * 512 + 512], 0.5,
                                           afs(xr_o[hl], 512),
                                           op0=ALU.mult, op1=ALU.add),
                                       wait=[("tT", d_mm), (f"sr{hl}", lx),
                                             (f"go{hl}", xo_free[hl])], inc="tV")
                        pd_free[hl] = x_ev
                        xr_free[hl] = x_ev
                        dma("gpsimd", xdst[oc][:, tok0:tok0 + 512],
                            afs(xo_o[hl], 512), f"go{hl}", wait=[("tV", x_ev)])
                        xo_free[hl] = pr.now(f"go{hl}")
            AB.reset(mP)
            AFA.reset(mFf)

        def attn_phase(xsrc, xdst):
            mP = AB.mark()
            xn_tV = ln_to_xn(xsrc)
            AB.reset(mP)
            mA = AB.mark()
            mAf = AFA.mark()
            qkv_o = AB.alloc(8 * 2048)
            vna_o = AB.alloc(16 * 512)
            wqb_o = AB.alloc(16 * 512)
            ast_o = [AB.alloc(1024), AB.alloc(1024)]
            qst_o = [AB.alloc(1024), AB.alloc(1024)]
            qs_o = AB.alloc(1024)
            qkb_o = AB.alloc(2048)
            kst_o = AB.alloc(1024)
            kvb_o = AB.alloc(512)
            dgt_o = AB.alloc(2048)
            qdc_o = AB.alloc(1024)
            kv_o = AFA.alloc(512)
            dtt_o = AFA.alloc(2048)
            kd_o = AFA.alloc(8)
            bdp_o = AFA.alloc(32)

            kn_o = wqb_o

            wqb_free = 0
            kn_ok = 0
            ast_free = [0, 0]
            qst_free = [0, 0]
            kv_dG = 0

            with nc.psum_tensor(_u("pp"), [128, 2048], F32) as pp, \
                 nc.psum_tensor(_u("pB"), [128, 1024], F32) as pB, \
                 nc.psum_tensor(_u("pC"), [128, 512], F32) as pC:
                pp_free = [0] * 4
                pA_free = 0; pB_free = 0; pC_free = 0
                kn_rd_tV = 0
                astg_free = [0, 0]
                qstg_free = [0, 0]
                for g in range(NGRP):
                    for role in range(3):
                        for hc in range(DC):
                            dma("scalar",
                                abs_(wqb_o + hc * 512, 512),
                                wqkv_bf_a[12 * g + 4 * role: 12 * g + 4 * role + 4,
                                        :, hc * 128:(hc + 1) * 128]
                                .rearrange("a b c -> (a b) c"),
                                "aw", transpose=True,
                                wait=[("tT", wqb_free), ("gw", wcast_done),
                                      ("an", kn_ok), ("tV", kn_rd_tV)])
                        wts = pr.now("aw")
                        if role < 2:
                            for oc4 in range(4):
                                qc = (role * 4 + oc4)
                                pr.emit("tensor", None, wait=[
                                    ("aw", wts), ("tV", xn_tV)] +
                                    [("tS", pp_free[nch]) for nch in range(4)])
                                for hc in range(DC):
                                    for nch in range(4):
                                        pr.emit("tensor",
                                                lambda e, hc=hc, nch=nch, oc4=oc4:
                                                e.matmul(
                                                    pp[:, nch * 512:(nch + 1) * 512],
                                                    abs_(wqb_o + hc * 512 + oc4 * 128, 128),
                                                    xn[:, hc * 2048 + nch * 512:
                                                       hc * 2048 + (nch + 1) * 512],
                                                    start=(hc == 0),
                                                    stop=(hc == DC - 1)))
                                p_mm = pr.emit("tensor", None, inc="tT")
                                if oc4 == 3:
                                    wqb_free = p_mm
                                for nch in range(4):
                                    ev = pr.emit("scalar",
                                                 lambda e, qc=qc, nch=nch:
                                                 e.activation(
                                                     abs_(qkv_o + qc * 2048 + nch * 512, 512),
                                                     pp[:, nch * 512:(nch + 1) * 512],
                                                     AF.Silu),
                                                 wait=[("tT", p_mm),
                                                       ("tV", kn_rd_tV),
                                                       ("gk", pr.now("gk"))],
                                                 inc="tS")
                                    pp_free[nch] = ev
                                if role == 1:
                                    dma("gpsimd", kstage_a[oc4 * 128:(oc4 + 1) * 128, :],
                                        abs_(qkv_o + (4 + oc4) * 2048, 2048),
                                        "gk", wait=[("tS", pr.now("tS")),
                                                    ("an", kn_ok)])
                            if role == 1:
                                kstage_dG = pr.now("gk")
                        else:
                            for tc in range(DC):
                                nb = tc % 4
                                pr.emit("tensor", None, wait=[
                                    ("aw", wts), ("tV", xn_tV),
                                    ("tS", pp_free[nb])])
                                for hc in range(DC):
                                    pr.emit("tensor", lambda e, hc=hc, tc=tc, nb=nb:
                                            e.matmul(pp[:, nb * 512:(nb + 1) * 512],
                                                     xn[:, hc * 2048 + tc * 128:
                                                        hc * 2048 + tc * 128 + 128],
                                                     abs_(wqb_o + hc * 512, 512),
                                                     start=(hc == 0),
                                                     stop=(hc == DC - 1)))
                                v_mm = pr.emit("tensor", None, inc="tT")
                                if tc == DC - 1:
                                    wqb_free = v_mm
                                ev = pr.emit("scalar", lambda e, tc=tc, nb=nb:
                                             e.activation(abs_(vna_o + tc * 512, 512),
                                                          pp[:, nb * 512:(nb + 1) * 512],
                                                          AF.Silu),
                                             wait=[("tT", v_mm),
                                                   ("tV", kn_rd_tV)], inc="tS")
                                pp_free[nb] = ev
                    v_tS = pr.now("tS")
                    for tc in range(DC):
                        dma("scalar", abs_(kn_o + tc * 512, 512),
                            kstage_a[:, tc * 128:(tc + 1) * 128],
                            "an", transpose=True,
                            wait=[("gk", kstage_dG), ("tT", wqb_free),
                                  ("tV", kn_rd_tV)])
                    kn_ok = pr.now("an")
                    def tblv(e, g=g):
                        for hi in range(GH):
                            hh = g * GH + hi
                            o = dtt_o + hi * 512
                            e.tensor_scalar(afs(o, 256), iotaf[:, :],
                                            iotapn[:, 0:1], None, op0=ALU.subtract)
                            e.tensor_scalar(afs(o, 256), afs(o, 256),
                                            negsb[:, hh:hh + 1], 0.0,
                                            op0=ALU.mult, op1=ALU.min)
                            e.tensor_scalar(afs(o + 256, 256), iotaf[:, :],
                                            iotapn[:, 1:2], None, op0=ALU.subtract)
                            e.tensor_scalar(afs(o + 256, 256), afs(o + 256, 256),
                                            negsb[:, hh:hh + 1], 0.0,
                                            op0=ALU.mult, op1=ALU.min)
                            e.tensor_scalar(afs(kd_o + hi * 2, 2), iotakd[:, :],
                                            negsb[:, hh:hh + 1], None, op0=ALU.mult)
                            inst = e.tensor_scalar(afs(bdp_o + hi * 8, 8),
                                                   iotablk[:, :],
                                                   negsb[:, hh:hh + 1], None,
                                                   op0=ALU.mult)
                        return inst
                    tb = pr.emit("vector", tblv, wait=[("tV", consts_ok),
                                                       ("tV", pr.now("tV")),
                                                       ("tS", pr.now("tS")),
                                                       ("gv", pr.now("gv"))],
                                 inc="tV")

                    def tbse(e):
                        for hi in range(GH):
                            o = dtt_o + hi * 512
                            e.activation(afs(o, 512), afs(o, 512), AF.Exp)
                        e.activation(afs(kd_o, 8), afs(kd_o, 8), AF.Exp)
                        return e.activation(afs(bdp_o, 32), afs(bdp_o, 32), AF.Exp)
                    tbe = pr.emit("scalar", tbse, wait=[("tV", tb)], inc="tS")

                    def tblv2(e, g=g):
                        for hi in range(GH):
                            hh = g * GH + hi
                            o = dtt_o + hi * 512
                            e.tensor_tensor(abs_(dgt_o + hi * 512, 256),
                                            afs(o, 256), maskc[:, 0:256],
                                            op=ALU.mult)
                            e.tensor_tensor(abs_(dgt_o + hi * 512 + 256, 256),
                                            afs(o + 256, 256), maskc[:, 256:512],
                                            op=ALU.mult)
                            e.tensor_scalar(afs(o, 256), iotaf[:, :], 1.0,
                                            negsb[:, hh:hh + 1],
                                            op0=ALU.add, op1=ALU.mult)
                        return e.memset(afs(kv_o, 512), 0.0)
                    tb2 = pr.emit("vector", tblv2, wait=[("tS", tbe)], inc="tV")

                    def tbse2(e):
                        for hi in range(GH):
                            inst = e.activation(afs(dtt_o + hi * 512, 256),
                                                afs(dtt_o + hi * 512, 256), AF.Exp)
                        return inst
                    qde = pr.emit("scalar", tbse2, wait=[("tV", tb2)], inc="tS")

                    def tblv3(e):
                        for hi in range(GH):
                            e.tensor_copy(abs_(qdc_o + hi * 256, 256),
                                          afs(dtt_o + hi * 512, 256))
                        return e.memset(abs_(kvb_o, 512), 0.0)
                    tbl_tV = pr.emit("vector", tblv3, wait=[("tS", qde)], inc="tV")

                    d1 = 0
                    oev_prev = 0
                    for i in range(NBLK):
                        c0 = i * 256
                        par = i % 2
                        pr.emit("tensor", None, wait=[
                            ("tV", max(d1, tbl_tV)), ("tS", v_tS), ("an", kn_ok)])
                        for hi in range(GH):
                            for nh in range(2):
                                pr.emit("tensor", lambda e, hi=hi, nh=nh, c0=c0:
                                        e.matmul(pp[:, hi * 512 + nh * 256:
                                                    hi * 512 + (nh + 1) * 256],
                                                 abs_(qkv_o + (4 + hi) * 2048 + c0 + nh * 128, 128),
                                                 abs_(qkv_o + hi * 2048 + c0, 256),
                                                 start=True, stop=True))
                        qk_mm = pr.emit("tensor", None, inc="tT")

                        def dve1(e, i=i, c0=c0):
                            for hi in range(GH):
                                for nh in range(2):
                                    e.tensor_tensor(
                                        abs_(qkb_o + hi * 512 + nh * 256, 256),
                                        pp[:, hi * 512 + nh * 256:
                                           hi * 512 + (nh + 1) * 256],
                                        abs_(dgt_o + hi * 512 + nh * 256, 256),
                                        op=ALU.mult)
                                e.tensor_tensor(abs_(qs_o + hi * 256, 256),
                                                abs_(qkv_o + hi * 2048 + c0, 256),
                                                abs_(qdc_o + hi * 256, 256),
                                                op=ALU.mult)
                                for nh in range(2):
                                    inst = e.tensor_scalar(
                                        abs_(kst_o + hi * 256 + nh * 128, 128),
                                        abs_(kn_o + (2 * i + nh) * 512 + hi * 128, 128),
                                        af_t[0:128, kd_o + hi * 2 + nh:
                                             kd_o + hi * 2 + nh + 1],
                                        None, op0=ALU.mult)
                            return inst
                        d1 = pr.emit("vector", dve1,
                                     wait=[("tT", qk_mm), ("tS", oev_prev),
                                           ("tT", pr.now("tT")),
                                           ("tS", pr.now("tS"))], inc="tV")
                        pr.emit("tensor", None, wait=[("tV", d1)])
                        for hi in range(GH):
                            for nh in range(2):
                                pr.emit("tensor", lambda e, hi=hi, i=i, nh=nh:
                                        e.matmul(pB[:, hi * 256:(hi + 1) * 256],
                                                 abs_(vna_o + (2 * i + nh) * 512 + hi * 128, 128),
                                                 abs_(qkb_o + hi * 512 + nh * 256, 256),
                                                 start=(nh == 0), stop=False))
                            pr.emit("tensor", lambda e, hi=hi:
                                    e.matmul(pB[:, hi * 256:(hi + 1) * 256],
                                             abs_(kvb_o + hi * 128, 128),
                                             abs_(qs_o + hi * 256, 256),
                                             start=False, stop=True))
                        o_mm = pr.emit("tensor", None, inc="tT")
                        pr.emit("tensor", None, wait=[("tV", pC_free)])
                        for hi in range(GH):
                            for nh in range(2):
                                pr.emit("tensor", lambda e, hi=hi, i=i, nh=nh:
                                        e.matmul(pC[:, hi * 128:(hi + 1) * 128],
                                                 abs_(kst_o + hi * 256 + nh * 128, 128),
                                                 abs_(vna_o + (2 * i + nh) * 512 + hi * 128, 128),
                                                 start=(nh == 0), stop=(nh == 1)))
                        kv_mm = pr.emit("tensor", None, inc="tT")

                        def oevf(e, par=par, i=i):
                            for hi in range(GH):
                                e.copy(abs_(ast_o[par] + hi * 256, 256),
                                       pB[:, hi * 256:(hi + 1) * 256])
                            for hi in range(GH):
                                inst = e.mul(abs_(qst_o[par] + hi * 256, 256),
                                             abs_(qs_o + hi * 256, 256),
                                             af_t[0:128, bdp_o + hi * 8 + i:
                                                  bdp_o + hi * 8 + i + 1])
                            return inst
                        oev = pr.emit("scalar", oevf,
                                      wait=[("tT", o_mm),
                                            (f"gs{par}", astg_free[par]),
                                            (f"gq{par}", qstg_free[par])],
                                      inc="tS")
                        oev_prev = oev
                        pB_free = oev

                        def kvupf(e, i=i):
                            for hi in range(GH):
                                e.tensor_scalar(afs(kv_o + hi * 128, 128),
                                                afs(kv_o + hi * 128, 128),
                                                af_t[0:128, bdp_o + hi * 8 + 1:
                                                     bdp_o + hi * 8 + 2],
                                                None, op0=ALU.mult)
                                e.tensor_tensor(afs(kv_o + hi * 128, 128),
                                                afs(kv_o + hi * 128, 128),
                                                pC[:, hi * 128:(hi + 1) * 128],
                                                op=ALU.add)
                                inst = e.tensor_copy(abs_(kvb_o + hi * 128, 128),
                                                     afs(kv_o + hi * 128, 128))
                            return inst
                        kvu = pr.emit("vector", kvupf, wait=[("tT", kv_mm)],
                                      inc="tV")
                        pC_free = kvu

                        dma("gpsimd",
                            aout_a[g * GH:(g + 1) * GH, :, c0:c0 + 256]
                            .rearrange("w p f -> p w f"),
                            abs_(ast_o[par], 1024)
                            .rearrange("p (w f) -> p w f", w=GH),
                            f"gs{par}", wait=[("tS", oev)])
                        astg_free[par] = pr.now(f"gs{par}")
                        dma("gpsimd",
                            qcor_a[g * GH:(g + 1) * GH, :, c0:c0 + 256]
                            .rearrange("w p f -> p w f"),
                            abs_(qst_o[par], 1024)
                            .rearrange("p (w f) -> p w f", w=GH),
                            f"gq{par}", wait=[("tS", oev)])
                        qstg_free[par] = pr.now(f"gq{par}")
                    dma("gpsimd",
                        kvag_i_a[g * GH * 128:(g + 1) * GH * 128, :]
                        .rearrange("(w p) f -> p w f", p=128),
                        afs(kv_o, 512).rearrange("p (w f) -> p w f", w=GH),
                        "gv", wait=[("tV", pr.now("tV"))])
                    kv_dG = pr.now("gv")
                    kn_rd_tV = pr.now("tV")
            AB.reset(mP)
            AFA.reset(mAf)

            pr.emit("gpsimd", lambda e: e.collective_compute(
                "AllGather", ALU.bypass,
                replica_groups=[[0, 1, 2, 3], [4, 5, 6, 7]],
                ins=[kvag_i.ap().opt()], outs=[kvag_o.ap().opt()]),
                wait=[("gv", kv_dG)], inc="cc")
            kvag_ok = pr.now("cc")

            if debug:
                dma("gpsimd", dbg_kv[:], kvag_o_a[:], "gw",
                    wait=[("cc", pr.now("cc"))])
            mG = AB.mark()
            mGf = AFA.mark()
            gat_o = AB.alloc(DC * 2048)
            wqb2_o = AB.alloc(16 * 512)
            qcb_o = [AB.alloc(2048), AB.alloc(2048)]
            sbf_o = AB.alloc(128)
            tmp_o = AB.alloc(2048)
            kvt_o = AFA.alloc(256)
            sac_o = AFA.alloc(128)
            wj_o = AFA.alloc(64)
            with nc.psum_tensor(_u("pp2"), [128, 2048], F32) as pp2:
                pp2_free = [("tS", 0)] * 4
                wqb2_free = 0
                for og in range(4):
                    for hc in range(DC):
                        dma("scalar", abs_(wqb2_o + hc * 512, 512),
                            wg_bf_a[og * 512:(og + 1) * 512,
                                  hc * 128:(hc + 1) * 128],
                            "aw", transpose=True,
                            wait=[("tT", wqb2_free), ("gw", wcast_done),
                                  ("tV", kn_rd_tV), ("tT", pr.now("tT")),
                                  ("gs0", pr.now("gs0")), ("gs1", pr.now("gs1")),
                                  ("gq0", pr.now("gq0")), ("gq1", pr.now("gq1")),
                                  ("gv", pr.now("gv"))])
                    wts = pr.now("aw")
                    for oc4 in range(4):
                        oc = og * 4 + oc4
                        pr.emit("tensor", None, wait=[("aw", wts)] +
                                [pp2_free[n] for n in range(4)])
                        for hc in range(DC):
                            for nch in range(4):
                                pr.emit("tensor", lambda e, hc=hc, nch=nch, oc4=oc4:
                                        e.matmul(pp2[:, nch * 512:(nch + 1) * 512],
                                                 abs_(wqb2_o + hc * 512 + oc4 * 128, 128),
                                                 xn[:, hc * 2048 + nch * 512:
                                                    hc * 2048 + (nch + 1) * 512],
                                                 start=(hc == 0), stop=(hc == DC - 1)))
                        g_mm = pr.emit("tensor", None, inc="tT")
                        if oc4 == 3:
                            wqb2_free = g_mm
                        for nch in range(4):
                            ev = pr.emit("scalar", lambda e, oc=oc, nch=nch:
                                         e.activation(
                                             abs_(gat_o + oc * 2048 + nch * 512, 512),
                                             pp2[:, nch * 512:(nch + 1) * 512],
                                             AF.Sigmoid),
                                         wait=[("tT", g_mm),
                                               ("gs0", pr.now("gs0")),
                                               ("gs1", pr.now("gs1")),
                                               ("gq0", pr.now("gq0")),
                                               ("gq1", pr.now("gq1"))], inc="tS")
                            pp2_free[nch] = ("tS", ev)
                gate_tS = pr.now("tS")
                gate_tT = pr.now("tT")

                for hh in range(H):
                    dma("sync", xn[:, hh * 2048:(hh + 1) * 2048], aout_a[hh],
                        "sa", wait=[("gs0", pr.now("gs0")), ("gs1", pr.now("gs1")),
                                    ("tT", gate_tT)])
                att_in_tot = pr.now("sa")
                def wjc(e):
                    for j in range(4):
                        e.tensor_scalar(afs(wj_o + j * 16, 16), negsb[:, :],
                                        gapv[:, j:j + 1], None, op0=ALU.mult)
                    return e.tensor_scalar(afs(wj_o, 64), afs(wj_o, 64), 1.0, None,
                                           op0=ALU.mult)
                wj1 = pr.emit("vector", wjc, wait=[("tV", consts_ok),
                                                   ("tV", pr.now("tV")),
                                                   ("gv", pr.now("gv"))], inc="tV")
                wj2 = pr.emit("scalar", lambda e:
                              e.activation(afs(wj_o, 64), afs(wj_o, 64), AF.Exp),
                              wait=[("tV", wj1)], inc="tS")
                def wjm(e):
                    for j in range(4):
                        e.tensor_scalar(afs(wj_o + j * 16, 16),
                                        afs(wj_o + j * 16, 16),
                                        wmv[:, j:j + 1], None, op0=ALU.mult)
                    return e.tensor_scalar(afs(wj_o, 64), afs(wj_o, 64), 1.0,
                                           None, op0=ALU.mult)
                wj3 = pr.emit("vector", wjm, wait=[("tS", wj2)], inc="tV")

                kvt_free = [0, 0]
                qcb_free = [0, 0]
                corr_tV = 0
                for hh in range(H):
                    hsl = hh % 2
                    dma("sync", abs_(qcb_o[hsl], 2048), qcor_a[hh], f"sq{hsl}",
                        wait=[("tT", qcb_free[hsl]),
                              ("gs0", pr.now("gs0")), ("gs1", pr.now("gs1")),
                              ("gq0", pr.now("gq0")), ("gq1", pr.now("gq1"))])
                    qcl = pr.now(f"sq{hsl}")
                    sacc = None
                    for j in range(4):
                        jsl = j % 2
                        dma("sync", afs(kvt_o + jsl * 128, 128),
                            kvag_o_a[(j * H + hh) * 128:(j * H + hh + 1) * 128, :],
                            f"sk{jsl}", wait=[("cc", kvag_ok),
                                              ("tV", kvt_free[jsl])])
                        kl = pr.now(f"sk{jsl}")
                        if j == 0:
                            fn = lambda e, jsl=jsl, hh=hh: e.tensor_scalar(
                                afs(sac_o, 128), afs(kvt_o + jsl * 128, 128),
                                af_t[0:128, wj_o + 0 * 16 + hh:wj_o + 0 * 16 + hh + 1],
                                None, op0=ALU.mult)
                        else:
                            fn = lambda e, jsl=jsl, hh=hh, j=j: e.scalar_tensor_tensor(
                                afs(sac_o, 128), afs(kvt_o + jsl * 128, 128),
                                af_t[0:128, wj_o + j * 16 + hh:wj_o + j * 16 + hh + 1],
                                afs(sac_o, 128), op0=ALU.mult, op1=ALU.add)
                        sacc = pr.emit("vector", fn,
                                       wait=[(f"sk{jsl}", kl), ("tV", wj3)],
                                       inc="tV")
                        kvt_free[jsl] = sacc
                    sbf = pr.emit("vector", lambda e:
                                  e.tensor_copy(abs_(sbf_o, 128), afs(sac_o, 128)),
                                  inc="tV")
                    pr.emit("tensor", None, wait=[
                        ("tV", sbf), (f"sq{hsl}", qcl), ("tV", corr_tV)] +
                        [pp2_free[n] for n in range(4)])
                    for nch in range(4):
                        pr.emit("tensor", lambda e, nch=nch, hsl=hsl:
                                e.matmul(pp2[:, nch * 512:(nch + 1) * 512],
                                         abs_(sbf_o, 128),
                                         abs_(qcb_o[hsl] + nch * 512, 512),
                                         start=True, stop=True))
                    c_mm = pr.emit("tensor", None, inc="tT")
                    qcb_free[hsl] = c_mm

                    def corr(e, hh=hh):
                        for nch in range(4):
                            e.tensor_tensor(
                                xn[:, hh * 2048 + nch * 512: hh * 2048 + (nch + 1) * 512],
                                xn[:, hh * 2048 + nch * 512: hh * 2048 + (nch + 1) * 512],
                                pp2[:, nch * 512:(nch + 1) * 512], op=ALU.add)
                        return e.tensor_scalar(afs(sac_o, 1), afs(sac_o, 1), 1.0,
                                               None, op0=ALU.mult)
                    corr_tV = pr.emit("vector", corr,
                                      wait=[("tT", c_mm), ("sa", att_in_tot),
                                            ("tS", gate_tS)], inc="tV")
                corr_done = corr_tV
                if debug:
                    for hh in range(H):
                        dma("gpsimd", dbg_ac[hh], xn[:, hh * 2048:(hh + 1) * 2048],
                            "gw", wait=[("tV", corr_done)])

                sq2_o = tmp_o
                r2_o = AFA.alloc(2560)
                rb2_o = AFA.alloc(2048)
                with nc.psum_tensor(_u("pbc2"), [128, 512], F32) as pbc2:
                    for c in range(DC):
                        sq = pr.emit("scalar", lambda e, c=c:
                                     e.activation(abs_(sq2_o, 2048),
                                                  xn[:, c * 2048:(c + 1) * 2048],
                                                  AF.Square),
                                     wait=[("tV", corr_done),
                                           ("tT", pr.now("tT"))], inc="tS")
                        for nch in range(4):
                            pr.emit("tensor", lambda e, c=c, nch=nch:
                                    e.matmul(pp2[0:1, nch * 512:(nch + 1) * 512],
                                             onesb[:, 0:1],
                                             abs_(sq2_o + nch * 512, 512),
                                             start=(c == 0), stop=(c == DC - 1)),
                                    wait=[("tS", sq)] if nch == 0 else ())
                        pr.emit("tensor", None, inc="tT")
                    rstat_tT = pr.now("tT")
                    for nch in range(4):
                        o = nch * 512

                        def rrow(e, o=o):
                            e.tensor_scalar(afs(r2_o, 512, 1), pp2[0:1, o:o + 512],
                                            1.0 / D, None, op0=ALU.mult)
                            return e.tensor_scalar(afs(r2_o, 512, 1),
                                                   afs(r2_o, 512, 1),
                                                   RMS_EPS, None, op0=ALU.add)
                        rc = pr.emit("vector", rrow, wait=[("tT", rstat_tT)],
                                     inc="tV")
                        sq3 = pr.emit("scalar", lambda e:
                                      e.activation(afs(r2_o + 512, 512, 1),
                                                   afs(r2_o, 512, 1), AF.Sqrt),
                                      wait=[("tV", rc)], inc="tS")

                        def newt2(e):
                            e.reciprocal(afs(r2_o + 1024, 512, 1),
                                         afs(r2_o + 512, 512, 1))
                            e.tensor_tensor(afs(r2_o + 1536, 512, 1),
                                            afs(r2_o + 1024, 512, 1),
                                            afs(r2_o + 1024, 512, 1), op=ALU.mult)
                            e.tensor_tensor(afs(r2_o + 1536, 512, 1),
                                            afs(r2_o + 1536, 512, 1),
                                            afs(r2_o, 512, 1), op=ALU.mult)
                            e.tensor_scalar(afs(r2_o + 1536, 512, 1),
                                            afs(r2_o + 1536, 512, 1), -0.5, 1.5,
                                            op0=ALU.mult, op1=ALU.add)
                            return e.tensor_tensor(afs(r2_o, 512, 1),
                                                   afs(r2_o + 1024, 512, 1),
                                                   afs(r2_o + 1536, 512, 1),
                                                   op=ALU.mult)
                        nw2 = pr.emit("vector", newt2, wait=[("tS", sq3)], inc="tV")
                        bc = pr.emit("tensor", lambda e:
                                     e.matmul(pbc2[:, 0:512], onesf[:, :],
                                              afs(r2_o, 512, 1),
                                              start=True, stop=True),
                                     wait=[("tV", nw2)], inc="tT")
                        pr.emit("vector", lambda e, o=o:
                                e.tensor_copy(afs(rb2_o + o, 512), pbc2[:, 0:512]),
                                wait=[("tT", bc)], inc="tV")
                    rb_tV = pr.now("tV")
                g2_tV = 0
                for c in range(DC):
                    def g2f(e, c=c):
                        e.scalar_tensor_tensor(
                            abs_(tmp_o, 2048), xn[:, c * 2048:(c + 1) * 2048],
                            nwcol[:, c:c + 1], afs(rb2_o, 2048),
                            op0=ALU.mult, op1=ALU.mult)
                        return e.tensor_tensor(abs_(gat_o + c * 2048, 2048),
                                               abs_(gat_o + c * 2048, 2048),
                                               abs_(tmp_o, 2048), op=ALU.mult)
                    g2_tV = pr.emit("vector", g2f, wait=[("tV", rb_tV),
                                                         ("tS", gate_tS)], inc="tV")

                xr2_o = [AFA.alloc(2048), AFA.alloc(2048)]
                wqb2_free2 = pr.now("tT")
                xr_free = [0, 0]
                xo_free = [0, 0]
                for og in range(4):
                    for hc in range(DC):
                        dma("scalar", abs_(wqb2_o + hc * 512, 512),
                            wo_bf_a[og * 512:(og + 1) * 512,
                                  hc * 128:(hc + 1) * 128],
                            "aw", transpose=True,
                            wait=[("tT", wqb2_free2), ("gw", wcast_done)])
                    wts = pr.now("aw")
                    for oc4 in range(4):
                        oc = og * 4 + oc4
                        hl = oc % 2
                        lx = dma("sync", afs(xr2_o[hl], 2048), xsrc[oc],
                                 f"sr{hl}",
                                 wait=[("tV", xr_free[hl]), ("tV", corr_done),
                                       (f"go{hl}", xo_free[hl])])
                        pr.emit("tensor", None, wait=[("aw", wts), ("tV", g2_tV)] +
                                [pp2_free[n] for n in range(4)] +
                                [("tV", corr_done)])
                        for hc in range(DC):
                            for nch in range(4):
                                pr.emit("tensor", lambda e, hc=hc, nch=nch, oc4=oc4:
                                        e.matmul(pp2[:, nch * 512:(nch + 1) * 512],
                                                 abs_(wqb2_o + hc * 512 + oc4 * 128, 128),
                                                 abs_(gat_o + hc * 2048 + nch * 512, 512),
                                                 start=(hc == 0), stop=(hc == DC - 1)))
                        o_mm = pr.emit("tensor", None, inc="tT")
                        if oc4 == 3:
                            wqb2_free2 = o_mm

                        def resid(e, hl=hl):
                            for nch in range(4):
                                e.tensor_tensor(
                                    afs(xr2_o[hl] + nch * 512, 512),
                                    afs(xr2_o[hl] + nch * 512, 512),
                                    pp2[:, nch * 512:(nch + 1) * 512], op=ALU.add)
                            return e.tensor_scalar(afs(sac_o, 1), afs(sac_o, 1),
                                                   1.0, None, op0=ALU.mult)
                        rv = pr.emit("vector", resid,
                                     wait=[("tT", o_mm), (f"sr{hl}", lx),
                                           (f"go{hl}", xo_free[hl])], inc="tV")
                        for nch in range(4):
                            pp2_free[nch] = ("tV", rv)
                        dma("gpsimd", xdst[oc], afs(xr2_o[hl], 2048),
                            f"go{hl}", wait=[("tV", rv)])
                        xr_free[hl] = rv
                        xo_free[hl] = pr.now(f"go{hl}")
            AB.reset(mG)
            AFA.reset(mGf)

        pr.barrier()
        last = 0
        if "f1" in phases:
            ffn_phase(X_a[0], X_a[1])
            pr.barrier()
            last = 1
        if "attn" in phases:
            attn_phase(X_a[last], X_a[2])
            pr.barrier()
            last = 2
        if debug and "attn" in phases:
            for c in range(DC):
                dma("gpsimd", dbg_x2[c], X_a[2][c], "gw")
        if "f2" in phases:
            ffn_phase(X_a[last], X_a[3])
            pr.barrier()
            last = 3

        mO = AB.mark()
        mOf = AFA.mark()
        xl_o = [AFA.alloc(2048), AFA.alloc(2048)]
        os_o = AFA.alloc(2048)
        ohi_o = AB.alloc(2048)
        olo_o = AB.alloc(2048)
        with nc.psum_tensor(_u("pth"), [128, 2048], BF16) as pth, \
             nc.psum_tensor(_u("ptl"), [128, 2048], BF16) as ptl:
            xl_free = [0, 0]
            os_free = 0
            hl_free2 = 0
            for tc in range(DC):
                sl = tc % 2
                dma("sync",
                    afs(xl_o[sl], 2048).rearrange("p (w f) -> p w f", w=16),
                    X_a[last][:, :, tc * 128:(tc + 1) * 128]
                    .rearrange("w p f -> p w f"),
                    "sc", wait=[("tV", xl_free[sl])])
                lx = pr.now("sc")

                def osplit(e, sl=sl):
                    e.tensor_copy(abs_(ohi_o, 2048), afs(xl_o[sl], 2048))
                    return e.tensor_tensor(abs_(olo_o, 2048), afs(xl_o[sl], 2048),
                                           abs_(ohi_o, 2048), op=ALU.subtract)
                sp = pr.emit("vector", osplit,
                             wait=[("sc", lx), ("tT", hl_free2)], inc="tV")
                xl_free[sl] = sp
                pr.emit("tensor", None, wait=[("tV", sp), ("sc", cst_ld)])
                for fc in range(DC):
                    pr.emit("tensor", lambda e, fc=fc:
                            e.transpose(pth[:, fc * 128:(fc + 1) * 128],
                                        abs_(ohi_o + fc * 128, 128), ident[:, :]))
                    pr.emit("tensor", lambda e, fc=fc:
                            e.transpose(ptl[:, fc * 128:(fc + 1) * 128],
                                        abs_(olo_o + fc * 128, 128), ident[:, :]))
                t_mm = pr.emit("tensor", None, inc="tT")
                hl_free2 = t_mm

                def orecomb(e):
                    e.tensor_copy(afs(os_o, 2048), pth[:, :])
                    return e.tensor_tensor(afs(os_o, 2048), afs(os_o, 2048),
                                           ptl[:, :], op=ALU.add)
                ev = pr.emit("vector", orecomb,
                             wait=[("tT", t_mm), ("sa", os_free)], inc="tV")
                dma("sync", out[tc * 128:(tc + 1) * 128, :], afs(os_o, 2048),
                    "sa", wait=[("tV", ev)])
                os_free = pr.now("sa")
        AB.reset(mO)
        AFA.reset(mOf)

        pr.emit("sync", None,
                wait=[(q, pr.now(q)) for q in pr.dma_sems])

        block = stack.enter_context(nc.Block())
        for eng in P.ENGINES:
            def body(e, eng=eng):
                for fn in pr.ops[eng]:
                    fn(e)
            getattr(block, eng)(body)
    return nc


def _consts():
    p = np.arange(128, dtype=np.float32)
    c = {}
    c["cident"] = np.eye(128, dtype=BF)
    c["cidentf"] = np.eye(128, dtype=np.float32)
    c["cones_b"] = np.ones((128, 1), dtype=BF)
    c["cones_f"] = np.ones((1, 128), dtype=np.float32)
    c["ciotaf"] = np.broadcast_to(np.arange(256, dtype=np.float32)[None, :],
                                  (128, 256)).copy()
    c["ciotapn"] = np.stack([p, p + 128.0], axis=1).astype(np.float32)
    c["ciotakd"] = np.stack([255.0 - p, 255.0 - (p + 128.0)], axis=1).astype(np.float32)
    c["ciotablk"] = np.broadcast_to(
        (np.arange(8, dtype=np.float32) * 256.0)[None, :], (128, 8)).copy()
    m = np.zeros((128, 512), dtype=np.float32)
    mm_ = np.arange(256)[None, :]
    m[:, 0:256] = (mm_ >= p[:, None]).astype(np.float32)
    m[:, 256:512] = (mm_ >= (p[:, None] + 128)).astype(np.float32)
    c["cmask"] = m.astype(BF)
    return c


def _pack_idx(core, gate_idx, up_idx, down_idx):
    mats = {"gate": gate_idx, "up": up_idx, "down": down_idx}
    blocks = []
    for t in range(core * TPC, (core + 1) * TPC):
        mname, r0, c0 = _tile_src(t)
        blocks.append(mats[mname][r0:r0 + 128, c0:c0 + 128])
    return np.ascontiguousarray(np.concatenate(blocks, axis=1))


_NC_CACHE = {}


def kernel(L0_attn_x, L0_attn_slope_rate, L0_attn_Wqkv, L0_attn_Wg,
           L0_attn_norm_w, L0_attn_Wo, L0_ffn_lut, L0_ffn_gate_idx,
           L0_ffn_up_idx, L0_ffn_down_idx):
    x = np.ascontiguousarray(np.asarray(L0_attn_x, dtype=np.float32)
                             .reshape(B * SEQ, D))
    slope = np.ascontiguousarray(np.asarray(L0_attn_slope_rate,
                                            dtype=np.float32).reshape(16))
    wqkv = np.ascontiguousarray(np.asarray(L0_attn_Wqkv, dtype=np.float32))
    wg = np.ascontiguousarray(np.asarray(L0_attn_Wg, dtype=np.float32))
    wo = np.ascontiguousarray(np.asarray(L0_attn_Wo, dtype=np.float32))
    nw = np.ascontiguousarray(np.asarray(L0_attn_norm_w, dtype=np.float32)
                              .reshape(DC, 128).T)
    lutv = np.ascontiguousarray(np.asarray(L0_ffn_lut, dtype=np.float32))
    gi = np.asarray(L0_ffn_gate_idx, dtype=np.int32)
    ui = np.asarray(L0_ffn_up_idx, dtype=np.int32)
    di = np.asarray(L0_ffn_down_idx, dtype=np.int32)

    consts = _consts()
    key = "full"
    if key not in _NC_CACHE:
        _NC_CACHE[key] = build_program()
    nc = _NC_CACHE[key]

    in_maps = []
    for c in range(NCORES):
        bp = c % 4
        gap = np.zeros((1, 4), dtype=np.float32)
        wm = np.zeros((1, 4), dtype=np.float32)
        for j in range(4):
            if j < bp:
                gap[0, j] = 2048.0 * (bp - 1 - j)
                wm[0, j] = 1.0
        m = dict(consts)
        m["x"] = np.ascontiguousarray(x[c * T:(c + 1) * T])
        m["idxs"] = _pack_idx(c, gi, ui, di)
        m["lut"] = lutv
        m["slope"] = slope
        m["wqkv"] = wqkv
        m["wg"] = wg
        m["wo"] = wo
        m["nw"] = nw
        m["cgap"] = gap
        m["cwm"] = wm
        in_maps.append(m)

    res = run_bass_kernel_spmd(nc, in_maps, core_ids=list(range(NCORES)))
    outs = [np.asarray(r["out"]) for r in res.results]
    return np.concatenate(outs, axis=0).reshape(B, SEQ, D).astype(np.float32)


# version 05 (final)
# speedup vs baseline: 1.0283x; 1.0222x over previous
import numpy as np
import ml_dtypes
from contextlib import ExitStack

import concourse.bass as bass
import concourse.mybir as mybir
from concourse.bass_utils import run_bass_kernel_spmd

F32 = mybir.dt.float32
BF16 = mybir.dt.bfloat16
I32 = mybir.dt.int32
AF = mybir.ActivationFunctionType
ALU = mybir.AluOpType
BF = ml_dtypes.bfloat16

B, SEQ, D = 2, 8192, 2048
NCORES = 8
T = (B * SEQ) // NCORES
DC = D // 128
H, DH = 16, 128
BLKA = 256
NBLK = T // BLKA
INTER = 5504
JCN = INTER // 128
GH = 4
NGRP = H // GH
LN_EPS = 1e-5
RMS_EPS = float(np.finfo(np.float32).eps)
NT = T // 512
STR = 512
NSTR = T // STR

TILES = 3 * 688
TPC = TILES // NCORES
SLAB = 8
NSLAB = (TPC + SLAB - 1) // SLAB


def _tile_src(t):
    if t < 688:
        return ("gate", (t // 16) * 128, (t % 16) * 128)
    if t < 1376:
        t -= 688
        return ("up", (t // 16) * 128, (t % 16) * 128)
    t -= 1376
    return ("down", (t // 43) * 128, (t % 43) * 128)


def _wqkv_perm():
    perm = []
    for g in range(NGRP):
        for role in range(3):
            for i in range(GH):
                perm.append((g * GH + i) * 3 + role)
    return perm


class Arena:

    def __init__(self, t, cols):
        self.t = t
        self.cols = cols
        self.off = 0

    def alloc(self, n, align=64):
        o = (self.off + align - 1) // align * align
        assert o + n <= self.cols, f"arena overflow {o}+{n}>{self.cols}"
        self.off = o + n
        return o

    def mark(self):
        return self.off

    def reset(self, m):
        self.off = m


class P:
    ENGINES = ("sync", "gpsimd", "tensor", "vector", "scalar")

    def __init__(self, sems):
        self.ops = {e: [] for e in self.ENGINES}
        self.sems = sems
        self.c = {k: 0 for k in sems}
        self.last = {}
        self.last_inc = {}

    def emit(self, eng, fn=None, wait=(), inc=None, inc_by=1):
        wait = [(self.sems[s], c) for (s, c) in wait if c > 0]
        tgt = None
        semh = by = None
        if inc is not None:
            self.c[inc] += inc_by
            tgt = self.c[inc]
            semh = self.sems[inc]
            by = inc_by

        def run(e, eng=eng, wait=wait, fn=fn, semh=semh, by=by):
            for h, c in wait:
                e.wait_ge(h, c)
            if fn is not None:
                inst = fn(e)
                self.last[eng] = inst
                if semh is not None:
                    inst.then_inc(semh, by)
                    self.last_inc[eng] = (semh, self._rt_val(semh, by))
                else:
                    self.last_inc[eng] = None
            elif semh is not None:
                att = eng in ("tensor", "vector", "scalar")
                if (att and self.last.get(eng) is not None
                        and self.last_inc.get(eng) is None):
                    self.last[eng].then_inc(semh, by)
                    self.last_inc[eng] = (semh, self._rt_val(semh, by))
                elif att and self.last_inc.get(eng) is not None:
                    ps, pv = self.last_inc[eng]
                    e.wait_ge(ps, pv)
                    e.nop().then_inc(semh, by)
                    self.last_inc[eng] = (semh, self._rt_val(semh, by))
                else:
                    e.nop().then_inc(semh, by)

        self.ops[eng].append(run)
        return tgt

    def _rt_val(self, semh, by):
        key = id(semh)
        self.rt = getattr(self, "rt", {})
        self.rt[key] = self.rt.get(key, 0) + by
        return self.rt[key]

    def now(self, sem):
        return self.c[sem]

    def barrier(self):
        marks = {}
        dmas = [(q, self.c[q]) for q in self.dma_sems]
        for e in self.ENGINES:
            self.emit(e, None, inc="b_" + e)
            marks[e] = self.c["b_" + e]
        for e in self.ENGINES:
            self.emit(e, None,
                      wait=[("b_" + x, marks[x]) for x in self.ENGINES if x != e]
                      + dmas)


def build_program(phases=("conv", "f1", "attn", "f2"), debug=False):
    nc = bass.Bass("TRN2", target_bir_lowering=False)

    xin = nc.dram_tensor("x", [T, D], F32, kind="ExternalInput").ap()
    idxs = nc.dram_tensor("idxs", [128, TPC * 128], I32, kind="ExternalInput").ap()
    lut = nc.dram_tensor("lut", [16], F32, kind="ExternalInput").ap()
    slope = nc.dram_tensor("slope", [16], F32, kind="ExternalInput").ap()
    wqkv_in = nc.dram_tensor("wqkv", [3 * H * DH, D], F32, kind="ExternalInput").ap()
    wg_in = nc.dram_tensor("wg", [D, D], F32, kind="ExternalInput").ap()
    wo_in = nc.dram_tensor("wo", [D, D], F32, kind="ExternalInput").ap()
    nw_in = nc.dram_tensor("nw", [128, 16], F32, kind="ExternalInput").ap()
    cident = nc.dram_tensor("cident", [128, 128], BF16, kind="ExternalInput").ap()
    cidentf = nc.dram_tensor("cidentf", [128, 128], F32, kind="ExternalInput").ap()
    cones_b = nc.dram_tensor("cones_b", [128, 1], BF16, kind="ExternalInput").ap()
    cones_f = nc.dram_tensor("cones_f", [1, 128], F32, kind="ExternalInput").ap()
    ciotaf = nc.dram_tensor("ciotaf", [128, 256], F32, kind="ExternalInput").ap()
    ciotapn = nc.dram_tensor("ciotapn", [128, 2], F32, kind="ExternalInput").ap()
    ciotakd = nc.dram_tensor("ciotakd", [128, 2], F32, kind="ExternalInput").ap()
    ciotablk = nc.dram_tensor("ciotablk", [128, 8], F32, kind="ExternalInput").ap()
    cmask = nc.dram_tensor("cmask", [128, 512], BF16, kind="ExternalInput").ap()
    cgap = nc.dram_tensor("cgap", [1, 4], F32, kind="ExternalInput").ap()
    cwm = nc.dram_tensor("cwm", [1, 4], F32, kind="ExternalInput").ap()

    out = nc.dram_tensor("out", [T, D], F32, kind="ExternalOutput").ap()

    agin = nc.dram_tensor("agin", [TPC, 128, 128], BF16)
    agin_a = agin.ap()
    ffnw = nc.dram_tensor("ffnw", [TILES, 128, 128], BF16, addr_space="Shared")
    ffnw_a = ffnw.ap()
    X = [nc.dram_tensor(f"X{i}", [DC, 128, T], F32) for i in range(4)]
    X_a = [t.ap() for t in X]
    wqkv_bf = nc.dram_tensor("wqkv_bf", [48, 128, D], BF16)
    wqkv_bf_a = wqkv_bf.ap()
    wg_bf = nc.dram_tensor("wg_bf", [D, D], BF16)
    wg_bf_a = wg_bf.ap()
    wo_bf = nc.dram_tensor("wo_bf", [D, D], BF16)
    wo_bf_a = wo_bf.ap()
    kstage = nc.dram_tensor("kstage", [GH * 128, T], BF16)
    kstage_a = kstage.ap()
    aout_h = nc.dram_tensor("aout_h", [H, 128, T], BF16)
    aout_a = aout_h.ap()
    qcor_h = nc.dram_tensor("qcor_h", [H, 128, T], BF16)
    qcor_a = qcor_h.ap()
    kvag_i = nc.dram_tensor("kvag_i", [H * 128, 128], F32)
    kvag_i_a = kvag_i.ap()
    kvag_o = nc.dram_tensor("kvag_o", [4 * H * 128, 128], F32)
    kvag_o_a = kvag_o.ap()
    if debug:
        dbg_xn = nc.dram_tensor("dbg_xn", [DC, 128, 2048], BF16,
                                kind="ExternalOutput").ap()
        dbg_h = nc.dram_tensor("dbg_h", [JCN, 128, STR], BF16,
                               kind="ExternalOutput").ap()
        dbg_mr = nc.dram_tensor("dbg_mr", [2, 128, 2048], F32,
                                kind="ExternalOutput").ap()
        dbg_ac = nc.dram_tensor("dbg_ac", [H, 128, 2048], BF16,
                                kind="ExternalOutput").ap()
        dbg_fw = nc.dram_tensor("dbg_fw", [64, 128, 128], BF16,
                                kind="ExternalOutput").ap()
        dbg_x2 = nc.dram_tensor("dbg_x2", [DC, 128, T], F32,
                                kind="ExternalOutput").ap()
        dbg_kv = nc.dram_tensor("dbg_kv", [4 * H * 128, 128], F32,
                                kind="ExternalOutput").ap()
        dbg_ag = nc.dram_tensor("dbg_ag", [TPC, 128, 128], BF16,
                                kind="ExternalOutput").ap()

    stack = ExitStack()
    _uidc = [0]

    def _u(name):
        _uidc[0] += 1
        return f"{name}_{_uidc[0]}"
    with stack:
        def sbuf(name, p, f, dt):
            return stack.enter_context(nc.sbuf_tensor(name, [p, f], dt))

        sems = {}
        DMA_SEMS = (
            "sc", "sx0", "sx1", "sg0", "sg1", "su0", "su1", "sd0", "sd1",
            "sr0", "sr1", "sa", "sq0", "sq1", "sk0", "sk1",
            "gx", "gw", "gc0", "gc1", "ga0", "ga1", "go0", "go1",
            "gk", "gs0", "gs1", "gq0", "gq1", "gv",
            "ax0", "ax1", "aw", "an",
        )
        for s in DMA_SEMS + ("cc", "tT", "tV", "tS",
                             "b_sync", "b_gpsimd", "b_tensor", "b_vector",
                             "b_scalar"):
            sems[s] = stack.enter_context(nc.semaphore(s))
        pr = P(sems)
        pr.dma_sems = DMA_SEMS

        ident = sbuf("ident", 128, 128, BF16)
        identf = sbuf("identf", 128, 128, F32)
        onesb = sbuf("onesb", 128, 1, BF16)
        onesf = sbuf("onesf", 1, 128, F32)
        iotaf = sbuf("iotaf", 128, 256, F32)
        iotapn = sbuf("iotapn", 128, 2, F32)
        iotakd = sbuf("iotakd", 128, 2, F32)
        iotablk = sbuf("iotablk", 128, 8, F32)
        maskc = sbuf("maskc", 128, 512, BF16)
        gapv = sbuf("gapv", 128, 4, F32)
        wmv = sbuf("wmv", 128, 4, F32)
        sbc = sbuf("sbc", 128, 16, F32)
        negsb = sbuf("negsb", 128, 16, F32)
        nwcol = sbuf("nwcol", 128, 16, F32)
        lut128 = sbuf("lut128", 128, 16, F32)
        xn = sbuf("xn", 128, DC * 2048, BF16)
        AB_COLS = 49664
        AF_COLS = 9216
        ab_t = sbuf("ab", 128, AB_COLS, BF16)
        af_t = sbuf("af", 128, AF_COLS, F32)
        AB = Arena(ab_t, AB_COLS)
        AFA = Arena(af_t, AF_COLS)

        def abs_(o, n):
            return ab_t[:, o:o + n]

        def afs(o, n, p=128):
            return af_t[0:p, o:o + n]

        def dma(eng, outap, inap, sem, transpose=False, wait=()):
            return pr.emit(eng, lambda e, o=outap, i=inap, t=transpose:
                           e.dma_start(out=o, in_=i, transpose=t),
                           wait=wait, inc=sem, inc_by=16)

        for outap, inap in (
            (ident[:, :], cident), (identf[:, :], cidentf),
            (onesb[:, :], cones_b), (onesf[:, :], cones_f),
            (iotaf[:, :], ciotaf), (iotapn[:, :], ciotapn), (iotakd[:, :], ciotakd),
            (iotablk[:, :], ciotablk), (maskc[:, :], cmask),
            (gapv[:, :], cgap[:, :].partition_broadcast(128)),
            (wmv[:, :], cwm[:, :].partition_broadcast(128)),
            (sbc[:, :], slope[None, :].partition_broadcast(128)),
            (nwcol[:, :], nw_in),
            (lut128[:, :], lut[None, :].partition_broadcast(128)),
        ):
            dma("sync", outap, inap, "sc")
        cst_ld = pr.now("sc")
        consts_ok = pr.emit("vector", lambda e: e.tensor_scalar(
            negsb[:, :], sbc[:, :], -1.0, None, op0=ALU.mult),
            wait=[("sc", cst_ld)], inc="tV")

        perm = _wqkv_perm()
        for n in range(48):
            dma("gpsimd", wqkv_bf_a[n], wqkv_in[perm[n] * 128:perm[n] * 128 + 128, :],
                "gw")
        for i in range(4):
            dma("gpsimd", wg_bf_a[i * 512:(i + 1) * 512, :],
                wg_in[i * 512:(i + 1) * 512, :], "gw")
            dma("gpsimd", wo_bf_a[i * 512:(i + 1) * 512, :],
                wo_in[i * 512:(i + 1) * 512, :], "gw")
        wcast_done = pr.now("gw")

        m0 = AB.mark()
        m0f = AFA.mark()
        xpl_o = [AFA.alloc(2048), AFA.alloc(2048)]
        xst_o = AFA.alloc(2048)
        xhi_o = AB.alloc(2048)
        xlo_o = AB.alloc(2048)
        with nc.psum_tensor(_u("pxh"), [128, 2048], BF16) as pxh, \
             nc.psum_tensor(_u("pxl"), [128, 2048], BF16) as pxl:
            xpl_free = [0, 0]
            xst_free = 0
            hl_free = 0
            for tc in range(DC):
                sl = tc % 2
                dma("sync", afs(xpl_o[sl], 2048), xin[tc * 128:(tc + 1) * 128, :],
                    f"sx{sl}", wait=[("tV", xpl_free[sl])])
                lx = pr.now(f"sx{sl}")

                def split(e, sl=sl):
                    e.tensor_copy(abs_(xhi_o, 2048), afs(xpl_o[sl], 2048))
                    return e.tensor_tensor(abs_(xlo_o, 2048), afs(xpl_o[sl], 2048),
                                           abs_(xhi_o, 2048), op=ALU.subtract)
                sp = pr.emit("vector", split,
                             wait=[(f"sx{sl}", lx), ("tT", hl_free),
                                   ("tV", consts_ok)], inc="tV")
                xpl_free[sl] = sp
                pr.emit("tensor", None, wait=[("tV", sp), ("sc", cst_ld),
                                              ("tV", pr.now("tV"))])
                for fc in range(DC):
                    pr.emit("tensor", lambda e, fc=fc:
                            e.transpose(pxh[:, fc * 128:(fc + 1) * 128],
                                        abs_(xhi_o + fc * 128, 128), ident[:, :]))
                    pr.emit("tensor", lambda e, fc=fc:
                            e.transpose(pxl[:, fc * 128:(fc + 1) * 128],
                                        abs_(xlo_o + fc * 128, 128), ident[:, :]))
                t_mm = pr.emit("tensor", None, inc="tT")
                hl_free = t_mm

                def recomb(e):
                    e.tensor_copy(afs(xst_o, 2048), pxh[:, :])
                    return e.tensor_tensor(afs(xst_o, 2048), afs(xst_o, 2048),
                                           pxl[:, :], op=ALU.add)
                ev = pr.emit("vector", recomb,
                             wait=[("tT", t_mm), ("sa", xst_free)], inc="tV")
                dma("sync",
                    X_a[0][:, :, tc * 128:(tc + 1) * 128]
                    .rearrange("w p f -> p w f"),
                    afs(xst_o, 2048).rearrange("p (w f) -> p w f", w=16),
                    "sa", wait=[("tV", ev)])
                xst_free = pr.now("sa")
        AFA.reset(m0f)

        if "conv" in phases:
            mC = AB.mark()
            cv_o = [AB.alloc(SLAB * 128), AB.alloc(SLAB * 128)]
            wn_o = [AB.alloc(SLAB * 128), AB.alloc(SLAB * 128)]
            st_o = [AB.alloc(SLAB * 128), AB.alloc(SLAB * 128)]
            tm_o = AB.alloc(SLAB * 128)
            with nc.psum_tensor(_u("ptc"), [128, 2 * SLAB * 128], BF16) as ptc:
                in_free = [0, 0]
                wn_free = [0, 0]
                st_free = [0, 0]
                pt_free = [0, 0]
                for s in range(NSLAB):
                    w = min(SLAB, TPC - s * SLAB)
                    sl = s % 2
                    cw = w * 128
                    dma("gpsimd", abs_(cv_o[sl], cw),
                        idxs[:, s * SLAB * 128: s * SLAB * 128 + cw],
                        f"gc{sl}", wait=[("tV", in_free[sl])])
                    ldc = pr.now(f"gc{sl}")
                    pr.emit("vector", None, wait=[(f"gc{sl}", ldc), ("tV", consts_ok),
                                                  ("tT", wn_free[sl])])
                    for k in range(16):
                        dst = wn_o[sl] if k == 0 else tm_o
                        pr.emit("vector", lambda e, sl=sl, cw=cw, k=k, dst=dst:
                                e.tensor_scalar(abs_(dst, cw), abs_(cv_o[sl], cw),
                                                float(k), lut128[:, k:k + 1],
                                                op0=ALU.is_equal, op1=ALU.mult))
                        if k > 0:
                            pr.emit("vector", lambda e, sl=sl, cw=cw:
                                    e.tensor_tensor(abs_(wn_o[sl], cw),
                                                    abs_(wn_o[sl], cw),
                                                    abs_(tm_o, cw), op=ALU.add))
                    cvt = pr.emit("vector", None, inc="tV")
                    in_free[sl] = cvt
                    pr.emit("tensor", None, wait=[("tV", cvt), ("tS", pt_free[sl]),
                                                  ("sc", cst_ld)])
                    for j in range(w):
                        pr.emit("tensor", lambda e, sl=sl, j=j:
                                e.transpose(ptc[:, sl * SLAB * 128 + j * 128:
                                                sl * SLAB * 128 + (j + 1) * 128],
                                            abs_(wn_o[sl] + j * 128, 128),
                                            ident[:, :]))
                    trc = pr.emit("tensor", None, inc="tT")
                    wn_free[sl] = trc
                    ev = pr.emit("scalar", lambda e, sl=sl, cw=cw:
                                 e.copy(abs_(st_o[sl], cw),
                                        ptc[:, sl * SLAB * 128: sl * SLAB * 128 + cw]),
                                 wait=[("tT", trc), (f"ga{sl}", st_free[sl])], inc="tS")
                    pt_free[sl] = ev
                    dma("gpsimd",
                        agin_a[s * SLAB: s * SLAB + w].rearrange("w p f -> p w f"),
                        abs_(st_o[sl], cw).rearrange("p (w f) -> p w f", w=w),
                        f"ga{sl}", wait=[("tS", ev)])
                    st_free[sl] = pr.now(f"ga{sl}")
            pr.emit("gpsimd", lambda e: e.collective_compute(
                "AllGather", ALU.bypass, replica_groups=[list(range(NCORES))],
                ins=[agin.ap().opt()], outs=[ffnw.ap().opt()]),
                wait=[("ga0", pr.now("ga0")), ("ga1", pr.now("ga1"))], inc="cc")
        AB.reset(m0)
        if debug and "conv" in phases:
            dma("gpsimd", dbg_fw[0:64], ffnw_a[0:64], "gw",
                wait=[("cc", pr.now("cc"))])
            dma("gpsimd", dbg_ag[:], agin_a[:], "gw",
                wait=[("cc", pr.now("cc"))])
        ffnw_ok = pr.now("cc")

        def ln_to_xn(xsrc):
            mL = AB.mark()
            mF = AFA.mark()
            xo = [AB.alloc(2048), AB.alloc(2048)]
            sq_o = AB.alloc(2048)
            xf_o = [AFA.alloc(512), AFA.alloc(512)]
            tf_o = AFA.alloc(512)
            mb_o = AFA.alloc(2048)
            rb_o = AFA.alloc(2048)
            r_o = AFA.alloc(2560)
            rowm = lambda nch: afs(r_o + 0, 512, 1)
            rowv = lambda nch: afs(r_o + 512, 512, 1)
            rowt = lambda nch: afs(r_o + 1024, 512, 1)
            rowr0 = lambda nch: afs(r_o + 1536, 512, 1)
            with nc.psum_tensor(_u("pstat"), [64, 2048], F32) as pstat, \
                 nc.psum_tensor(_u("pbc"), [128, 1024], F32) as pbc:
                xf = [0, 0]
                for c in range(DC):
                    sl = c % 2
                    dma("gpsimd", abs_(xo[sl], 2048), xsrc[c], f"gc{sl}",
                        wait=[("tT", xf[sl])])
                    lc = pr.now(f"gc{sl}")
                    sq = pr.emit("scalar", lambda e, sl=sl:
                                 e.activation(abs_(sq_o, 2048), abs_(xo[sl], 2048),
                                              AF.Square),
                                 wait=[(f"gc{sl}", lc), ("tT", pr.now("tT"))],
                                 inc="tS")
                    pr.emit("tensor", None, wait=[(f"gc{sl}", lc), ("tS", sq)])
                    for nch in range(4):
                        pr.emit("tensor", lambda e, sl=sl, c=c, nch=nch:
                                e.matmul(pstat[0:1, nch * 512:(nch + 1) * 512],
                                         onesb[:, 0:1],
                                         abs_(xo[sl] + nch * 512, 512),
                                         start=(c == 0), stop=(c == DC - 1)))
                        pr.emit("tensor", lambda e, c=c, nch=nch:
                                e.matmul(pstat[32:33, nch * 512:(nch + 1) * 512],
                                         onesb[:, 0:1],
                                         abs_(sq_o + nch * 512, 512),
                                         start=(c == 0), stop=(c == DC - 1)))
                    xf[sl] = pr.emit("tensor", None, inc="tT")
                stats_tT = pr.now("tT")
                for nch in range(4):
                    o = nch * 512

                    def rowc(e, o=o, nch=nch):
                        e.tensor_scalar(rowm(nch), pstat[0:1, o:o + 512],
                                        1.0 / D, None, op0=ALU.mult)
                        e.tensor_scalar(rowv(nch), pstat[32:33, o:o + 512],
                                        1.0 / D, None, op0=ALU.mult)
                        e.tensor_tensor(rowt(nch), rowm(nch), rowm(nch), op=ALU.mult)
                        e.tensor_tensor(rowv(nch), rowv(nch), rowt(nch),
                                        op=ALU.subtract)
                        return e.tensor_scalar(rowv(nch), rowv(nch), LN_EPS, None,
                                               op0=ALU.add)
                    rc = pr.emit("vector", rowc, wait=[("tT", stats_tT)], inc="tV")
                    sq2 = pr.emit("scalar", lambda e, nch=nch:
                                  e.activation(rowt(nch), rowv(nch), AF.Sqrt),
                                  wait=[("tV", rc)], inc="tS")

                    def newt(e, nch=nch):
                        e.reciprocal(rowr0(nch), rowt(nch))
                        e.tensor_tensor(rowt(nch), rowr0(nch), rowr0(nch),
                                        op=ALU.mult)
                        e.tensor_tensor(rowt(nch), rowt(nch), rowv(nch), op=ALU.mult)
                        e.tensor_scalar(rowt(nch), rowt(nch), -0.5, 1.5,
                                        op0=ALU.mult, op1=ALU.add)
                        return e.tensor_tensor(rowv(nch), rowr0(nch), rowt(nch),
                                               op=ALU.mult)
                    nw = pr.emit("vector", newt, wait=[("tS", sq2)], inc="tV")
                    pr.emit("tensor", lambda e, nch=nch:
                            e.matmul(pbc[:, 0:512], onesf[:, :], rowm(nch),
                                     start=True, stop=True),
                            wait=[("tV", nw), ("tV", pr.now("tV"))])
                    bc = pr.emit("tensor", lambda e, nch=nch:
                                 e.matmul(pbc[:, 512:1024], onesf[:, :], rowv(nch),
                                          start=True, stop=True), inc="tT")
                    pr.emit("vector", lambda e, o=o: (
                            e.tensor_copy(afs(mb_o + o, 512), pbc[:, 0:512]),
                            e.tensor_copy(afs(rb_o + o, 512), pbc[:, 512:1024]))[-1],
                            wait=[("tT", bc)], inc="tV")
                bcast_tV = pr.now("tV")
            if debug:
                dma("gpsimd", dbg_mr[0], afs(mb_o, 2048), "gw",
                    wait=[("tV", bcast_tV)])
                dma("gpsimd", dbg_mr[1], afs(rb_o, 2048), "gw")
            xf2 = [0, 0]
            for c in range(DC):
                for nch in range(4):
                    sl = nch % 2
                    dma("sync", afs(xf_o[sl], 512),
                        xsrc[c][:, nch * 512:(nch + 1) * 512], f"sx{sl}",
                        wait=[("tV", xf2[sl]), ("tT", stats_tT)])
                    lc = pr.now(f"sx{sl}")

                    def app(e, c=c, sl=sl, nch=nch):
                        e.tensor_tensor(afs(tf_o, 512), afs(xf_o[sl], 512),
                                        afs(mb_o + nch * 512, 512),
                                        op=ALU.subtract)
                        return e.tensor_tensor(
                            xn[:, c * 2048 + nch * 512: c * 2048 + (nch + 1) * 512],
                            afs(tf_o, 512), afs(rb_o + nch * 512, 512),
                            op=ALU.mult)
                    xf2[sl] = pr.emit("vector", app,
                                      wait=[(f"sx{sl}", lc), ("tV", bcast_tV)],
                                      inc="tV")
            AFA.reset(mF)
            return pr.now("tV")

        def ffn_phase(xsrc, xdst):
            mP = AB.mark()
            xn_tV = ln_to_xn(xsrc)
            mF = AB.mark()
            wgt_o = AB.alloc(4096)
            wup_o = AB.alloc(4096)
            wdb_o = AB.alloc(JCN * 128)
            sil_o = AB.alloc(1024)
            har_o = AB.alloc(JCN * STR)
            mFf = AFA.mark()
            xr_o = [AFA.alloc(512), AFA.alloc(512)]
            xo_o = [AFA.alloc(512), AFA.alloc(512)]
            with nc.psum_tensor(_u("pg"), [128, 1024], F32) as pg, \
                 nc.psum_tensor(_u("pu"), [128, 1024], F32) as pu, \
                 nc.psum_tensor(_u("pd"), [128, 1024], F32) as pd:
                wgt_free = [0, 0]; wup_free = [0, 0]
                pg_free = [0, 0]; pu_free = [0, 0]; pd_free = [0, 0]
                wdb_free = [0, 0]
                har_rd = 0
                xr_free = [0, 0]; xo_free = [0, 0]
                for st in range(NSTR):
                    tok0 = st * STR
                    for jc in range(JCN):
                        hl = jc % 2
                        cg = dma("sync", abs_(wgt_o + hl * 2048, 2048)
                                 .rearrange("p (w f) -> p w f", w=16),
                                 ffnw_a[jc * 16:(jc + 1) * 16]
                                 .rearrange("w p f -> p w f"),
                                 f"sg{hl}",
                                 wait=[("tT", wgt_free[hl]), ("cc", ffnw_ok)])
                        cu = dma("sync", abs_(wup_o + hl * 2048, 2048)
                                 .rearrange("p (w f) -> p w f", w=16),
                                 ffnw_a[688 + jc * 16:688 + (jc + 1) * 16]
                                 .rearrange("w p f -> p w f"), f"su{hl}",
                                 wait=[("tT", wup_free[hl])])
                        pr.emit("tensor", None, wait=[
                            (f"sg{hl}", cg), (f"su{hl}", cu), ("tV", xn_tV),
                            ("tS", pg_free[hl]), ("tV", pu_free[hl]),
                            ("tT", har_rd)])
                        for hc in range(DC):
                            pr.emit("tensor", lambda e, hl=hl, hc=hc, tok0=tok0:
                                    e.matmul(pg[:, hl * 512:hl * 512 + 512],
                                             abs_(wgt_o + hl * 2048 + hc * 128, 128),
                                             xn[:, hc * 2048 + tok0: hc * 2048 + tok0 + 512],
                                             start=(hc == 0), stop=(hc == DC - 1)))
                        g_mm = pr.emit("tensor", None, inc="tT")
                        wgt_free[hl] = g_mm
                        for hc in range(DC):
                            pr.emit("tensor", lambda e, hl=hl, hc=hc, tok0=tok0:
                                    e.matmul(pu[:, hl * 512:hl * 512 + 512],
                                             abs_(wup_o + hl * 2048 + hc * 128, 128),
                                             xn[:, hc * 2048 + tok0: hc * 2048 + tok0 + 512],
                                             start=(hc == 0), stop=(hc == DC - 1)))
                        u_mm = pr.emit("tensor", None, inc="tT")
                        wup_free[hl] = u_mm
                        sl_ev = pr.emit("scalar", lambda e, hl=hl:
                                        e.activation(abs_(sil_o + hl * 512, 512),
                                                     pg[:, hl * 512:hl * 512 + 512],
                                                     AF.Silu),
                                        wait=[("tT", g_mm), ("tV", pu_free[hl])],
                                        inc="tS")
                        pg_free[hl] = sl_ev
                        h_ev = pr.emit("vector", lambda e, hl=hl, jc=jc:
                                       e.scalar_tensor_tensor(
                                           abs_(har_o + jc * STR + 0, 512),
                                           pu[:, hl * 512:hl * 512 + 512], 1.0,
                                           abs_(sil_o + hl * 512, 512),
                                           op0=ALU.mult, op1=ALU.mult),
                                       wait=[("tT", u_mm), ("tS", sl_ev)], inc="tV")
                        pu_free[hl] = h_ev
                    h_tV = pr.now("tV")
                    if debug and st == 0:
                        for c in range(DC):
                            dma("gpsimd", dbg_xn[c], xn[:, c * 2048:(c + 1) * 2048],
                                "gx", wait=[("tV", xn_tV)])
                        for jc in range(JCN):
                            dma("gpsimd", dbg_h[jc], abs_(har_o + jc * STR, STR),
                                "gx", wait=[("tV", h_tV)])
                    for oc in range(DC):
                        hl = oc % 2
                        t0 = 1376 + oc * 43
                        cd0 = dma("sync", abs_(wdb_o, 22 * 128)
                                  .rearrange("p (w f) -> p w f", w=22),
                                  ffnw_a[t0:t0 + 22].rearrange("w p f -> p w f"),
                                  "sd0",
                                  wait=[("tT", wdb_free[0])])
                        cd1 = dma("sync", abs_(wdb_o + 22 * 128, 21 * 128)
                                  .rearrange("p (w f) -> p w f", w=21),
                                  ffnw_a[t0 + 22:t0 + 43]
                                  .rearrange("w p f -> p w f"), "sd1",
                                  wait=[("tT", wdb_free[1])])
                        lx = dma("sync", afs(xr_o[hl], 512),
                                 xsrc[oc][:, tok0:tok0 + 512], f"sr{hl}",
                                 wait=[("tV", max(xr_free[hl], xn_tV))])
                        pr.emit("tensor", None, wait=[
                            ("sd0", cd0), ("sd1", cd1), ("tV", h_tV),
                            ("tV", pd_free[hl])])
                        for jc in range(JCN):
                            pr.emit("tensor", lambda e, hl=hl, jc=jc:
                                    e.matmul(pd[:, hl * 512:hl * 512 + 512],
                                             abs_(wdb_o + jc * 128, 128),
                                             abs_(har_o + jc * STR, 512),
                                             start=(jc == 0), stop=(jc == JCN - 1)))
                            if jc == 21:
                                pr.emit("tensor", None, inc="tT")
                                wdb_free[0] = pr.now("tT")
                        d_mm = pr.emit("tensor", None, inc="tT")
                        wdb_free[1] = d_mm
                        if oc == DC - 1:
                            har_rd = d_mm
                        x_ev = pr.emit("vector", lambda e, hl=hl:
                                       e.scalar_tensor_tensor(
                                           afs(xo_o[hl], 512),
                                           pd[:, hl * 512:hl * 512 + 512], 0.5,
                                           afs(xr_o[hl], 512),
                                           op0=ALU.mult, op1=ALU.add),
                                       wait=[("tT", d_mm), (f"sr{hl}", lx),
                                             (f"go{hl}", xo_free[hl])], inc="tV")
                        pd_free[hl] = x_ev
                        xr_free[hl] = x_ev
                        dma("gpsimd", xdst[oc][:, tok0:tok0 + 512],
                            afs(xo_o[hl], 512), f"go{hl}", wait=[("tV", x_ev)])
                        xo_free[hl] = pr.now(f"go{hl}")
            AB.reset(mP)
            AFA.reset(mFf)

        def attn_phase(xsrc, xdst):
            mP = AB.mark()
            xn_tV = ln_to_xn(xsrc)
            AB.reset(mP)
            mA = AB.mark()
            mAf = AFA.mark()
            qkv_o = AB.alloc(8 * 2048)
            vna_o = AB.alloc(16 * 512)
            wqb_o = AB.alloc(16 * 512)
            ast_o = [AB.alloc(1024), AB.alloc(1024)]
            qst_o = [AB.alloc(1024), AB.alloc(1024)]
            qs_o = AB.alloc(1024)
            qkb_o = AB.alloc(2048)
            kst_o = AB.alloc(1024)
            kvb_o = AB.alloc(512)
            dgt_o = AB.alloc(2048)
            qdc_o = AB.alloc(1024)
            kv_o = AFA.alloc(512)
            dtt_o = AFA.alloc(2048)
            kd_o = AFA.alloc(8)
            bdp_o = AFA.alloc(32)

            kn_o = wqb_o

            wqb_free = 0
            kn_ok = 0
            ast_free = [0, 0]
            qst_free = [0, 0]
            kv_dG = 0

            with nc.psum_tensor(_u("pp"), [128, 2048], F32) as pp, \
                 nc.psum_tensor(_u("pB"), [128, 1024], F32) as pB, \
                 nc.psum_tensor(_u("pC"), [128, 512], F32) as pC:
                pp_free = [0] * 4
                pA_free = 0; pB_free = 0; pC_free = 0
                kn_rd_tV = 0
                astg_free = [0, 0]
                qstg_free = [0, 0]
                for g in range(NGRP):
                    for role in range(3):
                        for hc in range(DC):
                            dma("scalar",
                                abs_(wqb_o + hc * 512, 512),
                                wqkv_bf_a[12 * g + 4 * role: 12 * g + 4 * role + 4,
                                        :, hc * 128:(hc + 1) * 128]
                                .rearrange("a b c -> (a b) c"),
                                "aw", transpose=True,
                                wait=[("tT", wqb_free), ("gw", wcast_done),
                                      ("an", kn_ok), ("tV", kn_rd_tV)])
                        wts = pr.now("aw")
                        if role < 2:
                            for oc4 in range(4):
                                qc = (role * 4 + oc4)
                                pr.emit("tensor", None, wait=[
                                    ("aw", wts), ("tV", xn_tV)] +
                                    [("tS", pp_free[nch]) for nch in range(4)])
                                for hc in range(DC):
                                    for nch in range(4):
                                        pr.emit("tensor",
                                                lambda e, hc=hc, nch=nch, oc4=oc4:
                                                e.matmul(
                                                    pp[:, nch * 512:(nch + 1) * 512],
                                                    abs_(wqb_o + hc * 512 + oc4 * 128, 128),
                                                    xn[:, hc * 2048 + nch * 512:
                                                       hc * 2048 + (nch + 1) * 512],
                                                    start=(hc == 0),
                                                    stop=(hc == DC - 1)))
                                p_mm = pr.emit("tensor", None, inc="tT")
                                if oc4 == 3:
                                    wqb_free = p_mm
                                for nch in range(4):
                                    ev = pr.emit("scalar",
                                                 lambda e, qc=qc, nch=nch:
                                                 e.activation(
                                                     abs_(qkv_o + qc * 2048 + nch * 512, 512),
                                                     pp[:, nch * 512:(nch + 1) * 512],
                                                     AF.Silu),
                                                 wait=[("tT", p_mm),
                                                       ("tV", kn_rd_tV),
                                                       ("gk", pr.now("gk"))],
                                                 inc="tS")
                                    pp_free[nch] = ev
                                if role == 1:
                                    dma("gpsimd", kstage_a[oc4 * 128:(oc4 + 1) * 128, :],
                                        abs_(qkv_o + (4 + oc4) * 2048, 2048),
                                        "gk", wait=[("tS", pr.now("tS")),
                                                    ("an", kn_ok)])
                            if role == 1:
                                kstage_dG = pr.now("gk")
                        else:
                            for tc in range(DC):
                                nb = tc % 4
                                pr.emit("tensor", None, wait=[
                                    ("aw", wts), ("tV", xn_tV),
                                    ("tS", pp_free[nb])])
                                for hc in range(DC):
                                    pr.emit("tensor", lambda e, hc=hc, tc=tc, nb=nb:
                                            e.matmul(pp[:, nb * 512:(nb + 1) * 512],
                                                     xn[:, hc * 2048 + tc * 128:
                                                        hc * 2048 + tc * 128 + 128],
                                                     abs_(wqb_o + hc * 512, 512),
                                                     start=(hc == 0),
                                                     stop=(hc == DC - 1)))
                                v_mm = pr.emit("tensor", None, inc="tT")
                                if tc == DC - 1:
                                    wqb_free = v_mm
                                ev = pr.emit("scalar", lambda e, tc=tc, nb=nb:
                                             e.activation(abs_(vna_o + tc * 512, 512),
                                                          pp[:, nb * 512:(nb + 1) * 512],
                                                          AF.Silu),
                                             wait=[("tT", v_mm),
                                                   ("tV", kn_rd_tV)], inc="tS")
                                pp_free[nb] = ev
                    v_tS = pr.now("tS")
                    for tc in range(DC):
                        dma("scalar", abs_(kn_o + tc * 512, 512),
                            kstage_a[:, tc * 128:(tc + 1) * 128],
                            "an", transpose=True,
                            wait=[("gk", kstage_dG), ("tT", wqb_free),
                                  ("tV", kn_rd_tV)])
                    kn_ok = pr.now("an")
                    def tblv(e, g=g):
                        for hi in range(GH):
                            hh = g * GH + hi
                            o = dtt_o + hi * 512
                            e.tensor_scalar(afs(o, 256), iotaf[:, :],
                                            iotapn[:, 0:1], None, op0=ALU.subtract)
                            e.tensor_scalar(afs(o, 256), afs(o, 256),
                                            negsb[:, hh:hh + 1], 0.0,
                                            op0=ALU.mult, op1=ALU.min)
                            e.tensor_scalar(afs(o + 256, 256), iotaf[:, :],
                                            iotapn[:, 1:2], None, op0=ALU.subtract)
                            e.tensor_scalar(afs(o + 256, 256), afs(o + 256, 256),
                                            negsb[:, hh:hh + 1], 0.0,
                                            op0=ALU.mult, op1=ALU.min)
                            e.tensor_scalar(afs(kd_o + hi * 2, 2), iotakd[:, :],
                                            negsb[:, hh:hh + 1], None, op0=ALU.mult)
                            inst = e.tensor_scalar(afs(bdp_o + hi * 8, 8),
                                                   iotablk[:, :],
                                                   negsb[:, hh:hh + 1], None,
                                                   op0=ALU.mult)
                        return inst
                    tb = pr.emit("vector", tblv, wait=[("tV", consts_ok),
                                                       ("tV", pr.now("tV")),
                                                       ("tS", pr.now("tS")),
                                                       ("gv", pr.now("gv"))],
                                 inc="tV")

                    def tbse(e):
                        for hi in range(GH):
                            o = dtt_o + hi * 512
                            e.activation(afs(o, 512), afs(o, 512), AF.Exp)
                        e.activation(afs(kd_o, 8), afs(kd_o, 8), AF.Exp)
                        return e.activation(afs(bdp_o, 32), afs(bdp_o, 32), AF.Exp)
                    tbe = pr.emit("scalar", tbse, wait=[("tV", tb)], inc="tS")

                    def tblv2(e, g=g):
                        for hi in range(GH):
                            hh = g * GH + hi
                            o = dtt_o + hi * 512
                            e.tensor_tensor(abs_(dgt_o + hi * 512, 256),
                                            afs(o, 256), maskc[:, 0:256],
                                            op=ALU.mult)
                            e.tensor_tensor(abs_(dgt_o + hi * 512 + 256, 256),
                                            afs(o + 256, 256), maskc[:, 256:512],
                                            op=ALU.mult)
                            e.tensor_scalar(afs(o, 256), iotaf[:, :], 1.0,
                                            negsb[:, hh:hh + 1],
                                            op0=ALU.add, op1=ALU.mult)
                        return e.memset(afs(kv_o, 512), 0.0)
                    tb2 = pr.emit("vector", tblv2, wait=[("tS", tbe)], inc="tV")

                    def tbse2(e):
                        for hi in range(GH):
                            inst = e.activation(afs(dtt_o + hi * 512, 256),
                                                afs(dtt_o + hi * 512, 256), AF.Exp)
                        return inst
                    qde = pr.emit("scalar", tbse2, wait=[("tV", tb2)], inc="tS")

                    def tblv3(e):
                        for hi in range(GH):
                            e.tensor_copy(abs_(qdc_o + hi * 256, 256),
                                          afs(dtt_o + hi * 512, 256))
                        return e.memset(abs_(kvb_o, 512), 0.0)
                    tbl_tV = pr.emit("vector", tblv3, wait=[("tS", qde)], inc="tV")

                    d1 = 0
                    oev_prev = 0
                    for i in range(NBLK):
                        c0 = i * 256
                        par = i % 2
                        pr.emit("tensor", None, wait=[
                            ("tV", max(d1, tbl_tV)), ("tS", v_tS), ("an", kn_ok)])
                        for hi in range(GH):
                            for nh in range(2):
                                pr.emit("tensor", lambda e, hi=hi, nh=nh, c0=c0:
                                        e.matmul(pp[:, hi * 512 + nh * 256:
                                                    hi * 512 + (nh + 1) * 256],
                                                 abs_(qkv_o + (4 + hi) * 2048 + c0 + nh * 128, 128),
                                                 abs_(qkv_o + hi * 2048 + c0, 256),
                                                 start=True, stop=True))
                        qk_mm = pr.emit("tensor", None, inc="tT")

                        def dve1(e, i=i, c0=c0):
                            for hi in range(GH):
                                for nh in range(2):
                                    e.tensor_tensor(
                                        abs_(qkb_o + hi * 512 + nh * 256, 256),
                                        pp[:, hi * 512 + nh * 256:
                                           hi * 512 + (nh + 1) * 256],
                                        abs_(dgt_o + hi * 512 + nh * 256, 256),
                                        op=ALU.mult)
                                e.tensor_tensor(abs_(qs_o + hi * 256, 256),
                                                abs_(qkv_o + hi * 2048 + c0, 256),
                                                abs_(qdc_o + hi * 256, 256),
                                                op=ALU.mult)
                                for nh in range(2):
                                    inst = e.tensor_scalar(
                                        abs_(kst_o + hi * 256 + nh * 128, 128),
                                        abs_(kn_o + (2 * i + nh) * 512 + hi * 128, 128),
                                        af_t[0:128, kd_o + hi * 2 + nh:
                                             kd_o + hi * 2 + nh + 1],
                                        None, op0=ALU.mult)
                            return inst
                        d1 = pr.emit("vector", dve1,
                                     wait=[("tT", qk_mm), ("tS", oev_prev),
                                           ("tT", pr.now("tT")),
                                           ("tS", pr.now("tS"))], inc="tV")
                        pr.emit("tensor", None, wait=[("tV", d1)])
                        for hi in range(GH):
                            for nh in range(2):
                                pr.emit("tensor", lambda e, hi=hi, i=i, nh=nh:
                                        e.matmul(pB[:, hi * 256:(hi + 1) * 256],
                                                 abs_(vna_o + (2 * i + nh) * 512 + hi * 128, 128),
                                                 abs_(qkb_o + hi * 512 + nh * 256, 256),
                                                 start=(nh == 0), stop=False))
                            pr.emit("tensor", lambda e, hi=hi:
                                    e.matmul(pB[:, hi * 256:(hi + 1) * 256],
                                             abs_(kvb_o + hi * 128, 128),
                                             abs_(qs_o + hi * 256, 256),
                                             start=False, stop=True))
                        o_mm = pr.emit("tensor", None, inc="tT")
                        pr.emit("tensor", None, wait=[("tV", pC_free)])
                        for hi in range(GH):
                            for nh in range(2):
                                pr.emit("tensor", lambda e, hi=hi, i=i, nh=nh:
                                        e.matmul(pC[:, hi * 128:(hi + 1) * 128],
                                                 abs_(kst_o + hi * 256 + nh * 128, 128),
                                                 abs_(vna_o + (2 * i + nh) * 512 + hi * 128, 128),
                                                 start=(nh == 0), stop=(nh == 1)))
                        kv_mm = pr.emit("tensor", None, inc="tT")

                        def oevf(e, par=par, i=i):
                            for hi in range(GH):
                                e.copy(abs_(ast_o[par] + hi * 256, 256),
                                       pB[:, hi * 256:(hi + 1) * 256])
                            for hi in range(GH):
                                inst = e.mul(abs_(qst_o[par] + hi * 256, 256),
                                             abs_(qs_o + hi * 256, 256),
                                             af_t[0:128, bdp_o + hi * 8 + i:
                                                  bdp_o + hi * 8 + i + 1])
                            return inst
                        oev = pr.emit("scalar", oevf,
                                      wait=[("tT", o_mm),
                                            (f"gs{par}", astg_free[par]),
                                            (f"gq{par}", qstg_free[par])],
                                      inc="tS")
                        oev_prev = oev
                        pB_free = oev

                        def kvupf(e, i=i):
                            for hi in range(GH):
                                e.tensor_scalar(afs(kv_o + hi * 128, 128),
                                                afs(kv_o + hi * 128, 128),
                                                af_t[0:128, bdp_o + hi * 8 + 1:
                                                     bdp_o + hi * 8 + 2],
                                                None, op0=ALU.mult)
                                e.tensor_tensor(afs(kv_o + hi * 128, 128),
                                                afs(kv_o + hi * 128, 128),
                                                pC[:, hi * 128:(hi + 1) * 128],
                                                op=ALU.add)
                                inst = e.tensor_copy(abs_(kvb_o + hi * 128, 128),
                                                     afs(kv_o + hi * 128, 128))
                            return inst
                        kvu = pr.emit("vector", kvupf, wait=[("tT", kv_mm)],
                                      inc="tV")
                        pC_free = kvu

                        dma("gpsimd",
                            aout_a[g * GH:(g + 1) * GH, :, c0:c0 + 256]
                            .rearrange("w p f -> p w f"),
                            abs_(ast_o[par], 1024)
                            .rearrange("p (w f) -> p w f", w=GH),
                            f"gs{par}", wait=[("tS", oev)])
                        astg_free[par] = pr.now(f"gs{par}")
                        dma("gpsimd",
                            qcor_a[g * GH:(g + 1) * GH, :, c0:c0 + 256]
                            .rearrange("w p f -> p w f"),
                            abs_(qst_o[par], 1024)
                            .rearrange("p (w f) -> p w f", w=GH),
                            f"gq{par}", wait=[("tS", oev)])
                        qstg_free[par] = pr.now(f"gq{par}")
                    dma("gpsimd",
                        kvag_i_a[g * GH * 128:(g + 1) * GH * 128, :]
                        .rearrange("(w p) f -> p w f", p=128),
                        afs(kv_o, 512).rearrange("p (w f) -> p w f", w=GH),
                        "gv", wait=[("tV", pr.now("tV"))])
                    kv_dG = pr.now("gv")
                    kn_rd_tV = pr.now("tV")
            AB.reset(mP)
            AFA.reset(mAf)

            pr.emit("gpsimd", lambda e: e.collective_compute(
                "AllGather", ALU.bypass,
                replica_groups=[[0, 1, 2, 3], [4, 5, 6, 7]],
                ins=[kvag_i.ap().opt()], outs=[kvag_o.ap().opt()]),
                wait=[("gv", kv_dG)], inc="cc")
            kvag_ok = pr.now("cc")

            if debug:
                dma("gpsimd", dbg_kv[:], kvag_o_a[:], "gw",
                    wait=[("cc", pr.now("cc"))])
            mG = AB.mark()
            mGf = AFA.mark()
            gat_o = AB.alloc(DC * 2048)
            wqb2_o = AB.alloc(16 * 512)
            qcb_o = [AB.alloc(2048), AB.alloc(2048)]
            sbf_o = AB.alloc(128)
            tmp_o = AB.alloc(2048)
            kvt_o = AFA.alloc(256)
            sac_o = AFA.alloc(128)
            wj_o = AFA.alloc(64)
            with nc.psum_tensor(_u("pp2"), [128, 2048], F32) as pp2:
                pp2_free = [("tS", 0)] * 4
                wqb2_free = 0
                for og in range(4):
                    for hc in range(DC):
                        dma("scalar", abs_(wqb2_o + hc * 512, 512),
                            wg_bf_a[og * 512:(og + 1) * 512,
                                  hc * 128:(hc + 1) * 128],
                            "aw", transpose=True,
                            wait=[("tT", wqb2_free), ("gw", wcast_done),
                                  ("tV", kn_rd_tV), ("tT", pr.now("tT")),
                                  ("gs0", pr.now("gs0")), ("gs1", pr.now("gs1")),
                                  ("gq0", pr.now("gq0")), ("gq1", pr.now("gq1")),
                                  ("gv", pr.now("gv"))])
                    wts = pr.now("aw")
                    for oc4 in range(4):
                        oc = og * 4 + oc4
                        pr.emit("tensor", None, wait=[("aw", wts)] +
                                [pp2_free[n] for n in range(4)])
                        for hc in range(DC):
                            for nch in range(4):
                                pr.emit("tensor", lambda e, hc=hc, nch=nch, oc4=oc4:
                                        e.matmul(pp2[:, nch * 512:(nch + 1) * 512],
                                                 abs_(wqb2_o + hc * 512 + oc4 * 128, 128),
                                                 xn[:, hc * 2048 + nch * 512:
                                                    hc * 2048 + (nch + 1) * 512],
                                                 start=(hc == 0), stop=(hc == DC - 1)))
                        g_mm = pr.emit("tensor", None, inc="tT")
                        if oc4 == 3:
                            wqb2_free = g_mm
                        for nch in range(4):
                            ev = pr.emit("scalar", lambda e, oc=oc, nch=nch:
                                         e.activation(
                                             abs_(gat_o + oc * 2048 + nch * 512, 512),
                                             pp2[:, nch * 512:(nch + 1) * 512],
                                             AF.Sigmoid),
                                         wait=[("tT", g_mm),
                                               ("gs0", pr.now("gs0")),
                                               ("gs1", pr.now("gs1")),
                                               ("gq0", pr.now("gq0")),
                                               ("gq1", pr.now("gq1"))], inc="tS")
                            pp2_free[nch] = ("tS", ev)
                gate_tS = pr.now("tS")
                gate_tT = pr.now("tT")

                for hh in range(H):
                    dma("sync", xn[:, hh * 2048:(hh + 1) * 2048], aout_a[hh],
                        "sa", wait=[("gs0", pr.now("gs0")), ("gs1", pr.now("gs1")),
                                    ("tT", gate_tT)])
                att_in_tot = pr.now("sa")
                def wjc(e):
                    for j in range(4):
                        e.tensor_scalar(afs(wj_o + j * 16, 16), negsb[:, :],
                                        gapv[:, j:j + 1], None, op0=ALU.mult)
                    return e.tensor_scalar(afs(wj_o, 64), afs(wj_o, 64), 1.0, None,
                                           op0=ALU.mult)
                wj1 = pr.emit("vector", wjc, wait=[("tV", consts_ok),
                                                   ("tV", pr.now("tV")),
                                                   ("gv", pr.now("gv"))], inc="tV")
                wj2 = pr.emit("scalar", lambda e:
                              e.activation(afs(wj_o, 64), afs(wj_o, 64), AF.Exp),
                              wait=[("tV", wj1)], inc="tS")
                def wjm(e):
                    for j in range(4):
                        e.tensor_scalar(afs(wj_o + j * 16, 16),
                                        afs(wj_o + j * 16, 16),
                                        wmv[:, j:j + 1], None, op0=ALU.mult)
                    return e.tensor_scalar(afs(wj_o, 64), afs(wj_o, 64), 1.0,
                                           None, op0=ALU.mult)
                wj3 = pr.emit("vector", wjm, wait=[("tS", wj2)], inc="tV")

                kvt_free = [0, 0]
                qcb_free = [0, 0]
                corr_tV = 0
                for hh in range(H):
                    hsl = hh % 2
                    dma("sync", abs_(qcb_o[hsl], 2048), qcor_a[hh], f"sq{hsl}",
                        wait=[("tT", qcb_free[hsl]),
                              ("gs0", pr.now("gs0")), ("gs1", pr.now("gs1")),
                              ("gq0", pr.now("gq0")), ("gq1", pr.now("gq1"))])
                    qcl = pr.now(f"sq{hsl}")
                    sacc = None
                    for j in range(4):
                        jsl = j % 2
                        dma("sync", afs(kvt_o + jsl * 128, 128),
                            kvag_o_a[(j * H + hh) * 128:(j * H + hh + 1) * 128, :],
                            f"sk{jsl}", wait=[("cc", kvag_ok),
                                              ("tV", kvt_free[jsl])])
                        kl = pr.now(f"sk{jsl}")
                        if j == 0:
                            fn = lambda e, jsl=jsl, hh=hh: e.tensor_scalar(
                                afs(sac_o, 128), afs(kvt_o + jsl * 128, 128),
                                af_t[0:128, wj_o + 0 * 16 + hh:wj_o + 0 * 16 + hh + 1],
                                None, op0=ALU.mult)
                        else:
                            fn = lambda e, jsl=jsl, hh=hh, j=j: e.scalar_tensor_tensor(
                                afs(sac_o, 128), afs(kvt_o + jsl * 128, 128),
                                af_t[0:128, wj_o + j * 16 + hh:wj_o + j * 16 + hh + 1],
                                afs(sac_o, 128), op0=ALU.mult, op1=ALU.add)
                        sacc = pr.emit("vector", fn,
                                       wait=[(f"sk{jsl}", kl), ("tV", wj3)],
                                       inc="tV")
                        kvt_free[jsl] = sacc
                    sbf = pr.emit("vector", lambda e:
                                  e.tensor_copy(abs_(sbf_o, 128), afs(sac_o, 128)),
                                  inc="tV")
                    pr.emit("tensor", None, wait=[
                        ("tV", sbf), (f"sq{hsl}", qcl), ("tV", corr_tV)] +
                        [pp2_free[n] for n in range(4)])
                    for nch in range(4):
                        pr.emit("tensor", lambda e, nch=nch, hsl=hsl:
                                e.matmul(pp2[:, nch * 512:(nch + 1) * 512],
                                         abs_(sbf_o, 128),
                                         abs_(qcb_o[hsl] + nch * 512, 512),
                                         start=True, stop=True))
                    c_mm = pr.emit("tensor", None, inc="tT")
                    qcb_free[hsl] = c_mm

                    def corr(e, hh=hh):
                        for nch in range(4):
                            e.tensor_tensor(
                                xn[:, hh * 2048 + nch * 512: hh * 2048 + (nch + 1) * 512],
                                xn[:, hh * 2048 + nch * 512: hh * 2048 + (nch + 1) * 512],
                                pp2[:, nch * 512:(nch + 1) * 512], op=ALU.add)
                        return e.tensor_scalar(afs(sac_o, 1), afs(sac_o, 1), 1.0,
                                               None, op0=ALU.mult)
                    corr_tV = pr.emit("vector", corr,
                                      wait=[("tT", c_mm), ("sa", att_in_tot),
                                            ("tS", gate_tS)], inc="tV")
                corr_done = corr_tV
                if debug:
                    for hh in range(H):
                        dma("gpsimd", dbg_ac[hh], xn[:, hh * 2048:(hh + 1) * 2048],
                            "gw", wait=[("tV", corr_done)])

                sq2_o = tmp_o
                r2_o = AFA.alloc(2560)
                rb2_o = AFA.alloc(2048)
                with nc.psum_tensor(_u("pbc2"), [128, 512], F32) as pbc2:
                    for c in range(DC):
                        sq = pr.emit("scalar", lambda e, c=c:
                                     e.activation(abs_(sq2_o, 2048),
                                                  xn[:, c * 2048:(c + 1) * 2048],
                                                  AF.Square),
                                     wait=[("tV", corr_done),
                                           ("tT", pr.now("tT"))], inc="tS")
                        for nch in range(4):
                            pr.emit("tensor", lambda e, c=c, nch=nch:
                                    e.matmul(pp2[0:1, nch * 512:(nch + 1) * 512],
                                             onesb[:, 0:1],
                                             abs_(sq2_o + nch * 512, 512),
                                             start=(c == 0), stop=(c == DC - 1)),
                                    wait=[("tS", sq)] if nch == 0 else ())
                        pr.emit("tensor", None, inc="tT")
                    rstat_tT = pr.now("tT")
                    for nch in range(4):
                        o = nch * 512

                        def rrow(e, o=o):
                            e.tensor_scalar(afs(r2_o, 512, 1), pp2[0:1, o:o + 512],
                                            1.0 / D, None, op0=ALU.mult)
                            return e.tensor_scalar(afs(r2_o, 512, 1),
                                                   afs(r2_o, 512, 1),
                                                   RMS_EPS, None, op0=ALU.add)
                        rc = pr.emit("vector", rrow, wait=[("tT", rstat_tT)],
                                     inc="tV")
                        sq3 = pr.emit("scalar", lambda e:
                                      e.activation(afs(r2_o + 512, 512, 1),
                                                   afs(r2_o, 512, 1), AF.Sqrt),
                                      wait=[("tV", rc)], inc="tS")

                        def newt2(e):
                            e.reciprocal(afs(r2_o + 1024, 512, 1),
                                         afs(r2_o + 512, 512, 1))
                            e.tensor_tensor(afs(r2_o + 1536, 512, 1),
                                            afs(r2_o + 1024, 512, 1),
                                            afs(r2_o + 1024, 512, 1), op=ALU.mult)
                            e.tensor_tensor(afs(r2_o + 1536, 512, 1),
                                            afs(r2_o + 1536, 512, 1),
                                            afs(r2_o, 512, 1), op=ALU.mult)
                            e.tensor_scalar(afs(r2_o + 1536, 512, 1),
                                            afs(r2_o + 1536, 512, 1), -0.5, 1.5,
                                            op0=ALU.mult, op1=ALU.add)
                            return e.tensor_tensor(afs(r2_o, 512, 1),
                                                   afs(r2_o + 1024, 512, 1),
                                                   afs(r2_o + 1536, 512, 1),
                                                   op=ALU.mult)
                        nw2 = pr.emit("vector", newt2, wait=[("tS", sq3)], inc="tV")
                        bc = pr.emit("tensor", lambda e:
                                     e.matmul(pbc2[:, 0:512], onesf[:, :],
                                              afs(r2_o, 512, 1),
                                              start=True, stop=True),
                                     wait=[("tV", nw2)], inc="tT")
                        pr.emit("vector", lambda e, o=o:
                                e.tensor_copy(afs(rb2_o + o, 512), pbc2[:, 0:512]),
                                wait=[("tT", bc)], inc="tV")
                    rb_tV = pr.now("tV")
                g2_tV = 0
                for c in range(DC):
                    def g2f(e, c=c):
                        e.scalar_tensor_tensor(
                            abs_(tmp_o, 2048), xn[:, c * 2048:(c + 1) * 2048],
                            nwcol[:, c:c + 1], afs(rb2_o, 2048),
                            op0=ALU.mult, op1=ALU.mult)
                        return e.tensor_tensor(abs_(gat_o + c * 2048, 2048),
                                               abs_(gat_o + c * 2048, 2048),
                                               abs_(tmp_o, 2048), op=ALU.mult)
                    g2_tV = pr.emit("vector", g2f, wait=[("tV", rb_tV),
                                                         ("tS", gate_tS)], inc="tV")

                xr2_o = [AFA.alloc(2048), AFA.alloc(2048)]
                wqb2_free2 = pr.now("tT")
                xr_free = [0, 0]
                xo_free = [0, 0]
                for og in range(4):
                    for hc in range(DC):
                        dma("scalar", abs_(wqb2_o + hc * 512, 512),
                            wo_bf_a[og * 512:(og + 1) * 512,
                                  hc * 128:(hc + 1) * 128],
                            "aw", transpose=True,
                            wait=[("tT", wqb2_free2), ("gw", wcast_done)])
                    wts = pr.now("aw")
                    for oc4 in range(4):
                        oc = og * 4 + oc4
                        hl = oc % 2
                        lx = dma("sync", afs(xr2_o[hl], 2048), xsrc[oc],
                                 f"sr{hl}",
                                 wait=[("tV", xr_free[hl]), ("tV", corr_done),
                                       (f"go{hl}", xo_free[hl])])
                        pr.emit("tensor", None, wait=[("aw", wts), ("tV", g2_tV)] +
                                [pp2_free[n] for n in range(4)] +
                                [("tV", corr_done)])
                        for hc in range(DC):
                            for nch in range(4):
                                pr.emit("tensor", lambda e, hc=hc, nch=nch, oc4=oc4:
                                        e.matmul(pp2[:, nch * 512:(nch + 1) * 512],
                                                 abs_(wqb2_o + hc * 512 + oc4 * 128, 128),
                                                 abs_(gat_o + hc * 2048 + nch * 512, 512),
                                                 start=(hc == 0), stop=(hc == DC - 1)))
                        o_mm = pr.emit("tensor", None, inc="tT")
                        if oc4 == 3:
                            wqb2_free2 = o_mm

                        def resid(e, hl=hl):
                            for nch in range(4):
                                e.tensor_tensor(
                                    afs(xr2_o[hl] + nch * 512, 512),
                                    afs(xr2_o[hl] + nch * 512, 512),
                                    pp2[:, nch * 512:(nch + 1) * 512], op=ALU.add)
                            return e.tensor_scalar(afs(sac_o, 1), afs(sac_o, 1),
                                                   1.0, None, op0=ALU.mult)
                        rv = pr.emit("vector", resid,
                                     wait=[("tT", o_mm), (f"sr{hl}", lx),
                                           (f"go{hl}", xo_free[hl])], inc="tV")
                        for nch in range(4):
                            pp2_free[nch] = ("tV", rv)
                        dma("gpsimd", xdst[oc], afs(xr2_o[hl], 2048),
                            f"go{hl}", wait=[("tV", rv)])
                        xr_free[hl] = rv
                        xo_free[hl] = pr.now(f"go{hl}")
            AB.reset(mG)
            AFA.reset(mGf)

        pr.barrier()
        last = 0
        if "f1" in phases:
            ffn_phase(X_a[0], X_a[1])
            pr.barrier()
            last = 1
        if "attn" in phases:
            attn_phase(X_a[last], X_a[2])
            pr.barrier()
            last = 2
        if debug and "attn" in phases:
            for c in range(DC):
                dma("gpsimd", dbg_x2[c], X_a[2][c], "gw")
        if "f2" in phases:
            ffn_phase(X_a[last], X_a[3])
            pr.barrier()
            last = 3

        mO = AB.mark()
        mOf = AFA.mark()
        xl_o = [AFA.alloc(2048), AFA.alloc(2048)]
        os_o = AFA.alloc(2048)
        ohi_o = AB.alloc(2048)
        olo_o = AB.alloc(2048)
        with nc.psum_tensor(_u("pth"), [128, 2048], BF16) as pth, \
             nc.psum_tensor(_u("ptl"), [128, 2048], BF16) as ptl:
            xl_free = [0, 0]
            os_free = 0
            hl_free2 = 0
            for tc in range(DC):
                sl = tc % 2
                dma("sync",
                    afs(xl_o[sl], 2048).rearrange("p (w f) -> p w f", w=16),
                    X_a[last][:, :, tc * 128:(tc + 1) * 128]
                    .rearrange("w p f -> p w f"),
                    "sc", wait=[("tV", xl_free[sl])])
                lx = pr.now("sc")

                def osplit(e, sl=sl):
                    e.tensor_copy(abs_(ohi_o, 2048), afs(xl_o[sl], 2048))
                    return e.tensor_tensor(abs_(olo_o, 2048), afs(xl_o[sl], 2048),
                                           abs_(ohi_o, 2048), op=ALU.subtract)
                sp = pr.emit("vector", osplit,
                             wait=[("sc", lx), ("tT", hl_free2)], inc="tV")
                xl_free[sl] = sp
                pr.emit("tensor", None, wait=[("tV", sp), ("sc", cst_ld)])
                for fc in range(DC):
                    pr.emit("tensor", lambda e, fc=fc:
                            e.transpose(pth[:, fc * 128:(fc + 1) * 128],
                                        abs_(ohi_o + fc * 128, 128), ident[:, :]))
                    pr.emit("tensor", lambda e, fc=fc:
                            e.transpose(ptl[:, fc * 128:(fc + 1) * 128],
                                        abs_(olo_o + fc * 128, 128), ident[:, :]))
                t_mm = pr.emit("tensor", None, inc="tT")
                hl_free2 = t_mm

                def orecomb(e):
                    e.tensor_copy(afs(os_o, 2048), pth[:, :])
                    return e.tensor_tensor(afs(os_o, 2048), afs(os_o, 2048),
                                           ptl[:, :], op=ALU.add)
                ev = pr.emit("vector", orecomb,
                             wait=[("tT", t_mm), ("sa", os_free)], inc="tV")
                dma("sync", out[tc * 128:(tc + 1) * 128, :], afs(os_o, 2048),
                    "sa", wait=[("tV", ev)])
                os_free = pr.now("sa")
        AB.reset(mO)
        AFA.reset(mOf)

        pr.emit("sync", None,
                wait=[(q, pr.now(q)) for q in pr.dma_sems])

        block = stack.enter_context(nc.Block())
        for eng in P.ENGINES:
            def body(e, eng=eng):
                for fn in pr.ops[eng]:
                    fn(e)
            getattr(block, eng)(body)
    return nc


def _consts():
    p = np.arange(128, dtype=np.float32)
    c = {}
    c["cident"] = np.eye(128, dtype=BF)
    c["cidentf"] = np.eye(128, dtype=np.float32)
    c["cones_b"] = np.ones((128, 1), dtype=BF)
    c["cones_f"] = np.ones((1, 128), dtype=np.float32)
    c["ciotaf"] = np.broadcast_to(np.arange(256, dtype=np.float32)[None, :],
                                  (128, 256)).copy()
    c["ciotapn"] = np.stack([p, p + 128.0], axis=1).astype(np.float32)
    c["ciotakd"] = np.stack([255.0 - p, 255.0 - (p + 128.0)], axis=1).astype(np.float32)
    c["ciotablk"] = np.broadcast_to(
        (np.arange(8, dtype=np.float32) * 256.0)[None, :], (128, 8)).copy()
    m = np.zeros((128, 512), dtype=np.float32)
    mm_ = np.arange(256)[None, :]
    m[:, 0:256] = (mm_ >= p[:, None]).astype(np.float32)
    m[:, 256:512] = (mm_ >= (p[:, None] + 128)).astype(np.float32)
    c["cmask"] = m.astype(BF)
    return c


def _pack_idx(core, gate_idx, up_idx, down_idx):
    mats = {"gate": gate_idx, "up": up_idx, "down": down_idx}
    blocks = []
    for t in range(core * TPC, (core + 1) * TPC):
        mname, r0, c0 = _tile_src(t)
        blocks.append(mats[mname][r0:r0 + 128, c0:c0 + 128])
    return np.ascontiguousarray(np.concatenate(blocks, axis=1))


_NC_CACHE = {}


def kernel(L0_attn_x, L0_attn_slope_rate, L0_attn_Wqkv, L0_attn_Wg,
           L0_attn_norm_w, L0_attn_Wo, L0_ffn_lut, L0_ffn_gate_idx,
           L0_ffn_up_idx, L0_ffn_down_idx):
    x = np.ascontiguousarray(np.asarray(L0_attn_x, dtype=np.float32)
                             .reshape(B * SEQ, D))
    slope = np.ascontiguousarray(np.asarray(L0_attn_slope_rate,
                                            dtype=np.float32).reshape(16))
    wqkv = np.ascontiguousarray(np.asarray(L0_attn_Wqkv, dtype=np.float32))
    wg = np.ascontiguousarray(np.asarray(L0_attn_Wg, dtype=np.float32))
    wo = np.ascontiguousarray(np.asarray(L0_attn_Wo, dtype=np.float32))
    nw = np.ascontiguousarray(np.asarray(L0_attn_norm_w, dtype=np.float32)
                              .reshape(DC, 128).T)
    lutv = np.ascontiguousarray(np.asarray(L0_ffn_lut, dtype=np.float32))
    gi = np.asarray(L0_ffn_gate_idx, dtype=np.int32)
    ui = np.asarray(L0_ffn_up_idx, dtype=np.int32)
    di = np.asarray(L0_ffn_down_idx, dtype=np.int32)

    consts = _consts()
    key = "full"
    if key not in _NC_CACHE:
        _NC_CACHE[key] = build_program()
    nc = _NC_CACHE[key]

    in_maps = []
    for c in range(NCORES):
        bp = c % 4
        gap = np.zeros((1, 4), dtype=np.float32)
        wm = np.zeros((1, 4), dtype=np.float32)
        for j in range(4):
            if j < bp:
                gap[0, j] = 2048.0 * (bp - 1 - j)
                wm[0, j] = 1.0
        m = dict(consts)
        m["x"] = np.ascontiguousarray(x[c * T:(c + 1) * T])
        m["idxs"] = _pack_idx(c, gi, ui, di)
        m["lut"] = lutv
        m["slope"] = slope
        m["wqkv"] = wqkv
        m["wg"] = wg
        m["wo"] = wo
        m["nw"] = nw
        m["cgap"] = gap
        m["cwm"] = wm
        in_maps.append(m)

    res = run_bass_kernel_spmd(nc, in_maps, core_ids=list(range(NCORES)))
    outs = [np.asarray(r["out"]) for r in res.results]
    return np.concatenate(outs, axis=0).reshape(B, SEQ, D).astype(np.float32)
